# version 01
# baseline (speedup 1.0000x reference)
import math
import numpy as np
SPART = 9
import ml_dtypes
import concourse.bass as bass
import concourse.mybir as mybir
from contextlib import ExitStack

F32 = mybir.dt.float32
BF16 = mybir.dt.bfloat16
AX = mybir.AxisListType
ALU = mybir.AluOpType
ACT = mybir.ActivationFunctionType

B, S, D, HID, NH, DH = 2, 4096, 2048, 8192, 16, 128
TOK = 1024
NT = TOK // 128
G4 = [[0, 1, 2, 3], [4, 5, 6, 7]]
G8 = [list(range(8))]
QC = 0.501953125
EPS_LN = 1e-5
EPS_RMS = float(np.finfo(np.float32).eps)
SCL = float(1.0 / math.sqrt(DH))

ENGS = ("T", "V", "S", "G", "Y")


class MK:
    def __init__(self, nc):
        self.nc = nc
        self.ops = {e: [] for e in ENGS}
        self.sem = {}
        self.cnt = {}

    def mksem(self, ctx, name):
        self.sem[name] = ctx.enter_context(self.nc.semaphore(name))
        self.cnt[name] = 0

    def w(self, eng, sem, val):
        if val <= 0:
            return
        h = self.sem[sem]
        self.ops[eng].append(lambda e, h=h, v=val: e.wait_ge(h, v))

    def wnow(self, eng, sem):
        self.w(eng, sem, self.cnt[sem])

    def d(self, eng, fn):
        self.ops[eng].append(fn)

    def dinc(self, eng, fn, sem, k=1):
        h = self.sem[sem]
        self.cnt[sem] += k
        self.ops[eng].append(lambda e, fn=fn, h=h, k=k: fn(e).then_inc(h, k))
        return self.cnt[sem]

    def dma(self, eng, out, in_, sem=None):
        if sem is None:
            sem = "yd" if eng == "Y" else "gd"
        return self.dinc(eng, lambda e, o=out, i=in_: e.dma_start(out=o, in_=i), sem, 16)

    def drain(self, eng):
        self.ops[eng].append(lambda e: e.drain())


def build(debug=0, stage=9):
    nc = bass.Bass("TRN2", target_bir_lowering=False)

    x_in = nc.dram_tensor("x_in", [TOK, D], F32, kind="ExternalInput").ap()
    wfc_sl = nc.dram_tensor("wfc_sl", [HID, 256], F32, kind="ExternalInput").ap()
    wpr_sl = nc.dram_tensor("wpr_sl", [D, 1024], F32, kind="ExternalInput").ap()
    wqkv_sl = nc.dram_tensor("wqkv_sl", [256, 3 * D], F32, kind="ExternalInput").ap()
    wo_sl = nc.dram_tensor("wo_sl", [256, D], F32, kind="ExternalInput").ap()
    identb_h = nc.dram_tensor("identb", [128, 128], BF16, kind="ExternalInput").ap()
    identf_h = nc.dram_tensor("identf", [128, 128], F32, kind="ExternalInput").ap()
    y_out = nc.dram_tensor("y", [TOK, D], F32, kind="ExternalOutput").ap()

    w1t_in = nc.dram_tensor("w1t_in", [256, HID], BF16)
    w1t_out = nc.dram_tensor("w1t_out", [D, HID], BF16)
    w2t_in = nc.dram_tensor("w2t_in", [16, 1024, 128], BF16)
    w2t_out = nc.dram_tensor("w2t_out", [8, 16, 1024, 128], BF16)
    qkv_in = nc.dram_tensor("qkv_in", [256, 3 * D], BF16)
    qkv_out = nc.dram_tensor("qkv_out", [D, 3 * D], BF16)
    wo_in = nc.dram_tensor("wo_in", [16, 256, 128], BF16)
    wo_out = nc.dram_tensor("wo_out", [8, 16, 256, 128], BF16)
    kt_in = nc.dram_tensor("kt_in", [16, 128, 1024], BF16)
    kt_out = nc.dram_tensor("kt_out", [4, 16, 128, 1024], BF16)
    s_in = nc.dram_tensor("s_in", [4, 16, 1024], F32)
    s_out = nc.dram_tensor("s_out", [16, 1024], F32)

    dbg = {}
    if debug:
        def dbgt(name, shape, dt):
            dbg[name] = nc.dram_tensor("d_" + name, shape, dt, kind="ExternalOutput").ap()
        dbgt("xlnT", [128, 16 * 1024], BF16)
        dbgt("w1t", [D, HID], BF16)
        dbgt("w2t", [8, 16, 1024, 128], BF16)
        dbgt("qkvw", [D, 3 * D], BF16)
        dbgt("wow", [8, 16, 256, 128], BF16)
        if debug >= 2:
            dbgt("x1", [128, NT * D], F32)
            dbgt("ms", [1, 1024], F32)
            dbgt("msraw", [1, 1024], F32)
            dbgt("ones", [128, 1], BF16)
            dbgt("hsq2d", [128, 2 * 512], BF16)
            dbgt("rst", [128, 8], F32)
            dbgt("hsq", [128, 64 * 512], BF16)
        if debug >= 3:
            dbgt("qt", [128, 16 * 1024], BF16)
            dbgt("vvd", [128, 8 * 2048], BF16)
            dbgt("ktf", [4, 16, 128, 1024], BF16)
            dbgt("ssb", [128, 512], F32)
            dbgt("std", [128, 8 * 16], F32)
            dbgt("x2", [128, NT * D], F32)

    mk = MK(nc)
    with ExitStack() as ctx:
        sb = lambda name, shape, dt: ctx.enter_context(nc.sbuf_tensor(name, shape, dt))
        psb = lambda name, shape, dt: ctx.enter_context(nc.psum_tensor(name, shape, dt))

        resid = sb("resid", [128, NT * D], F32)
        xlnT = sb("xlnT", [128, 16 * 1024], BF16)
        work = sb("work", [128, 64 * 512], BF16)
        ebuf = sb("ebuf", [128, 2 * 4096], BF16)
        wbuf = sb("wbuf", [128, 2 * 4096], BF16)
        hsq2 = sb("hsq2", [128, 2 * 512], BF16)
        fstage = sb("fstage", [128, 2 * 512], F32)
        s_sb = sb("s_sb", [128, 512], F32)
        identb = sb("identb_s", [128, 128], BF16)
        identf = sb("identf_s", [128, 128], F32)
        ones_bf = sb("ones_bf", [128, 1], BF16)
        bnst = sb("bnst", [128, 24], F32)
        bnag = sb("bnag", [128, 2], F32)
        m_v = sb("m_v", [128, 1], F32)
        rstd_v = sb("rstd_v", [128, 1], F32)
        bias_v = sb("bias_v", [128, 1], F32)
        zp = sb("zp", [128, 2 * 8], F32)
        zs = sb("zs", [128, 2], F32)
        rbf = sb("rbf", [128, 16], BF16)
        ms_sb = sb("ms_sb", [1, 2 * 512], F32)
        rs_t = sb("rs_t", [128, 8], F32)
        s_t = sb("s_t", [128, 8 * 16], F32)
        qsc_s = sb("qsc_s", [128, 64], F32)
        qsb_s = sb("qsb_s", [128, 64], BF16)
        qth_s = sb("qth_s", [128, 64], F32)
        vstage = sb("vstage", [128, 2 * 128], BF16)

        ps0 = psb("ps0", [128, 512], F32)
        ps1 = psb("ps1", [128, 512], F32)
        ps2 = psb("ps2", [128, 512], F32)
        ps_tpb = psb("ps_tpb", [128, 1024], BF16)
        ps_tp32 = psb("ps_tp32", [128, 512], F32)
        ps_ms = psb("ps_ms", [1, 512], F32)
        ps_s0 = psb("ps_s0", [128, 512], F32)
        ps_s1 = psb("ps_s1", [128, 512], F32)
        PS = [ps0, ps1, ps2]
        PSS = [ps_s0, ps_s1]

        for s in ["yd", "gd", "te", "ve", "se", "w0", "w1",
                  "ccA", "ccB", "ccC", "ccD", "ccE", "ccF"]:
            mk.mksem(ctx, s)
        block = ctx.enter_context(nc.Block())

        residv = resid[:].rearrange("p (t d) -> p t d", d=D)
        xlntv = xlnT[:].rearrange("p (c f) -> p c f", f=1024)
        hsqv = work[:].rearrange("p (c f) -> p c f", f=512)
        qtv = work[:, 0:16 * 1024].rearrange("p (c f) -> p c f", f=1024)
        vv = work[:, 16 * 1024:32 * 1024].rearrange("p (t f) -> p t f", f=D)
        valtv = xlnT[:].rearrange("p (c f) -> p c f", f=1024)
        ebv = ebuf[:].rearrange("p (i f) -> p i f", f=4096)
        wbv = wbuf[:].rearrange("p (i f) -> p i f", f=4096)
        ktsv = fstage[:].bitcast(BF16).rearrange("p (i f) -> p i f", f=1024)
        hsq2v = hsq2[:].rearrange("p (i f) -> p i f", f=512)
        fsv = fstage[:].rearrange("p (i f) -> p i f", f=512)
        zpv = zp[:].rearrange("p (i f) -> p i f", f=8)
        msv = ms_sb[:].rearrange("p (i f) -> p i f", f=512)
        stv = s_t[:].rearrange("p (t h) -> p t h", h=16)
        vstv = vstage[:].rearrange("p (i f) -> p i f", f=128)
        xlnb = ebuf[:, 4096:4096 + 2048]
        stmp = ebuf[:, 0:4096].bitcast(F32)[0:16, 0:1024]
        qstage32 = work[:, 0:32 * 1024].bitcast(F32)

        def wbslot(i, f):
            return wbv[:, i, :].rearrange("p (c f) -> p c f", f=f)

        mk.dma("G", resid[:].rearrange("p (t d) -> p t d", d=D),
               x_in.rearrange("(t p) d -> p t d", p=128))
        x_done = mk.cnt["gd"]
        mk.dma("Y", identb[:], identb_h)
        mk.dma("Y", identf[:], identf_h)
        id_done = mk.cnt["yd"]
        mk.d("V", lambda e: e.memset(ones_bf[:], 1.0))

        def quant_stage(src_ap, fd, inner):
            mk.wnow("Y", "ve")
            ld = mk.dma("Y", qstage32[:, 0:fd].rearrange(
                "p (a b) -> p a b", b=inner), src_ap)
            mk.w("V", "yd", ld)
            mk.wnow("V", "gd")
            mk.wnow("V", "te")
            wb = ebv[:, 0, 0:fd]
            wb3 = wb.rearrange("p (g k) -> p g k", k=64)
            g = fd // 64
            mk.d("V", lambda e, wb=wb, fd=fd: e.tensor_copy(wb, qstage32[:, 0:fd]))
            mk.drain("V")
            mk.d("V", lambda e, wb3=wb3, g=g: e.tensor_reduce(
                qsc_s[:, 0:g], wb3, axis=AX.X, op=ALU.add, apply_absolute_value=True))
            mk.drain("V")
            mk.d("V", lambda e, g=g: e.tensor_scalar(
                qsb_s[:, 0:g], qsc_s[:, 0:g], 1.0 / 64.0, None, ALU.mult))
            mk.drain("V")
            mk.d("V", lambda e, g=g: e.tensor_scalar(
                qsc_s[:, 0:g], qsb_s[:, 0:g], 1e-8, None, ALU.max))
            mk.drain("V")
            mk.d("V", lambda e, g=g: e.tensor_scalar(
                qth_s[:, 0:g], qsc_s[:, 0:g], QC, None, ALU.mult))
            mk.drain("V")
            thb = qth_s[:, 0:g].unsqueeze(2).broadcast_to([128, g, 64])
            scb = qsc_s[:, 0:g].unsqueeze(2).broadcast_to([128, g, 64])
            mp = ebv[:, 1, 0:fd].rearrange("p (g k) -> p g k", k=64)
            mn = work[:, 8192:8192 + fd].rearrange("p (g k) -> p g k", k=64)
            mk.d("V", lambda e, mp=mp, wb3=wb3, thb=thb: e.scalar_tensor_tensor(
                mp, wb3, 1.0, thb, ALU.mult, ALU.is_gt))
            mk.d("V", lambda e, mn=mn, wb3=wb3, thb=thb: e.scalar_tensor_tensor(
                mn, wb3, -1.0, thb, ALU.mult, ALU.is_gt))
            mk.drain("V")
            mk.d("V", lambda e, mp=mp, mn=mn: e.tensor_tensor(mp, mp, mn, ALU.subtract))
            mk.drain("V")
            return mk.dinc("V", lambda e, wb3=wb3, mp=mp, scb=scb: e.tensor_tensor(
                wb3, mp, scb, ALU.mult), "ve")

        first_tb = [True]

        def transpose_blocks(nblk, src_fn, pairs_fn, vready):
            if first_tb[0]:
                mk.w("T", "yd", id_done)
                first_tb[0] = False
            nbat = (nblk + 7) // 8
            for b in range(nbat):
                k0, k1 = b * 8, min(b * 8 + 8, nblk)
                mk.w("T", "ve", vready if b == 0 else mk.cnt["ve"])
                mk.wnow("T", "gd")
                for k in range(k0, k1):
                    fn = lambda e, k=k, j=k - k0: e.transpose(
                        ps_tpb[:, j * 128:(j + 1) * 128], src_fn(k), identb[:])
                    if k == k1 - 1:
                        mk.dinc("T", fn, "te")
                    else:
                        mk.d("T", fn)
                mk.wnow("V", "te")
                stage = ktsv[:, b % 2, :]
                mk.dinc("V", lambda e, st=stage: e.tensor_copy(st, ps_tpb[:]), "ve")
                mk.wnow("G", "ve")
                for dst_ap, src_ap in pairs_fn(b, stage):
                    mk.dma("G", dst_ap, src_ap)

        wfc4 = wfc_sl.rearrange("(st tq p) d -> st p tq d", p=128, tq=16)
        w1ap = w1t_in.ap().rearrange("(dc p) (st tq f) -> p st tq dc f",
                                     p=128, st=4, f=128)
        for st4 in range(4):
            vd = quant_stage(wfc4[st4], 4096, 256)
            srcv = ebv[:, 0, 0:4096].rearrange("p (tq dc f) -> p tq dc f", dc=2, f=128)
            def w1pairs(b, st, st4=st4):
                stv4 = st.rearrange("p (tq dc f) -> p tq dc f", dc=2, f=128)
                return [(w1ap[:, st4, b * 4:b * 4 + 4, dc, :], stv4[:, :, dc, :])
                        for dc in range(2)]
            transpose_blocks(
                32,
                lambda k, srcv=srcv: srcv[:, k // 2, k % 2, :],
                w1pairs,
                vd)
        w1t_ready = mk.cnt["gd"]

        wpr4 = wpr_sl.rearrange("(st tq p) h -> st p tq h", p=128, tq=4)
        w2ap = w2t_in.ap().rearrange("dc (hc p) f -> p dc hc f", p=128)
        for st4 in range(4):
            vd = quant_stage(wpr4[st4], 4096, 1024)
            srcv = ebv[:, 0, 0:4096].rearrange("p (tq hc f) -> p tq hc f", hc=8, f=128)
            transpose_blocks(
                32,
                lambda k, srcv=srcv: srcv[:, k // 8, k % 8, :],
                lambda b, st, st4=st4: [(w2ap[:, st4 * 4 + b, :, :],
                                         st.rearrange("p (hc f) -> p hc f", f=128))],
                vd)
        w2t_ready = mk.cnt["gd"]

        qkv2 = wqkv_sl.rearrange("(r p) f -> r p f", p=128)
        qin2 = qkv_in.ap().rearrange("(r p) f -> r p f", p=128)
        for r in range(2):
            for half in range(2):
                f0, f1 = half * 3072, (half + 1) * 3072
                mk.wnow("Y", "ve")
                ld = mk.dma("Y", qstage32[:, 0:3072], qkv2[r, :, f0:f1])
                mk.w("V", "yd", ld)
                mk.wnow("V", "gd")
                mk.wnow("V", "te")
                vd = mk.dinc("V", lambda e: e.tensor_copy(
                    ebv[:, 0, 0:3072], qstage32[:, 0:3072]), "ve")
                mk.w("G", "ve", vd)
                mk.dma("G", qin2[r, :, f0:f1], ebv[:, 0, 0:3072])
        qkv_ready = mk.cnt["gd"]

        wo2 = wo_sl.rearrange("(r p) f -> r p f", p=128)
        wo_blk = wo_in.ap().rearrange("dc (r p) f -> r p dc f", p=128)
        for r in range(2):
            mk.wnow("Y", "ve")
            ld = mk.dma("Y", qstage32[:, 0:2048], wo2[r])
            mk.w("V", "yd", ld)
            mk.wnow("V", "gd")
            mk.wnow("V", "te")
            vd = mk.dinc("V", lambda e: e.tensor_copy(
                ebv[:, 0, 0:2048], qstage32[:, 0:2048]), "ve")
            mk.w("G", "ve", vd)
            mk.dma("G", wo_blk[r], ebv[:, 0, 0:2048].rearrange("p (dc f) -> p dc f", f=128))
        wo_ready = mk.cnt["gd"]

        def ag(inp, outp, ready, sem, groups=G8):
            mk.w("G", "gd", ready)
            return mk.dinc("G", lambda e, i=inp, o=outp, g=groups: e.collective_compute(
                "AllGather", ALU.bypass, replica_groups=g,
                ins=[i.ap().opt()], outs=[o.ap().opt()]), sem)
        cc_w1t = ag(w1t_in, w1t_out, w1t_ready, "ccA")
        cc_w2t = ag(w2t_in, w2t_out, w2t_ready, "ccB")
        cc_qkv = ag(qkv_in, qkv_out, qkv_ready, "ccC")
        cc_wo = ag(wo_in, wo_out, wo_ready, "ccD")

        def emit_ln(gd_wait=0):
            if gd_wait:
                mk.w("V", "gd", gd_wait)
                mk.w("S", "gd", gd_wait)
            mk.drain("V")
            prev_xln_se = 0
            prev_tr_te = 0
            for t in range(NT):
                xt = residv[:, t, :]
                if prev_xln_se:
                    mk.w("V", "se", prev_xln_se)
                for q4 in range(4):
                    mk.d("V", lambda e, xt=xt, q4=q4: e.bn_stats(
                        bnst[:, q4 * 6:(q4 + 1) * 6], xt[:, q4 * 512:(q4 + 1) * 512]))
                mk.drain("V")
                mk.d("V", lambda e: e.bn_aggr(bnag[:], bnst[:]))
                mk.drain("V")
                mk.d("V", lambda e: e.tensor_copy(m_v[:], bnag[:, 0:1]))
                mk.d("V", lambda e: e.tensor_scalar(
                    rstd_v[:], bnag[:, 1:2], 1.0, EPS_LN, ALU.mult, ALU.add))
                mk.drain("V")
                vd = mk.dinc("V", lambda e: e.reciprocal(rstd_v[:], rstd_v[:]), "ve")
                mk.w("S", "ve", vd)
                sd = mk.dinc("S", lambda e: e.activation(rstd_v[:], rstd_v[:], ACT.Sqrt), "se")
                mk.w("V", "se", sd)
                vd = mk.dinc("V", lambda e: e.scalar_tensor_tensor(
                    bias_v[:], m_v[:], -1.0, rstd_v[:], ALU.mult, ALU.mult), "ve")
                mk.w("S", "ve", vd)
                if prev_tr_te:
                    mk.w("S", "te", prev_tr_te)
                sd = mk.dinc("S", lambda e, xt=xt: e.activation(
                    xlnb[:], xt, ACT.Identity, bias=bias_v[:], scale=rstd_v[:]), "se")
                prev_xln_se = sd
                for bb in range(2):
                    mk.w("T", "se", sd)
                    mk.wnow("T", "ve")
                    for k in range(8):
                        dc = bb * 8 + k
                        fn = lambda e, dc=dc, k=k: e.transpose(
                            ps_tpb[:, k * 128:(k + 1) * 128],
                            xlnb[:, dc * 128:(dc + 1) * 128], identb[:])
                        if k == 7:
                            prev_tr_te = mk.dinc("T", fn, "te")
                        else:
                            mk.d("T", fn)
                    mk.wnow("V", "te")
                    dst = xlntv[:, bb * 8:(bb + 1) * 8, t * 128:(t + 1) * 128]
                    mk.dinc("V", lambda e, dst=dst: e.tensor_copy(
                        dst, ps_tpb[:].rearrange("p (c f) -> p c f", f=128)), "ve")

        emit_ln(gd_wait=wo_ready)

        if debug:
            mk.wnow("G", "ve")
            mk.dma("G", dbg["xlnT"], xlnT[:])
            mk.w("G", "ccD", mk.cnt["ccD"])
            mk.dma("G", dbg["w1t"], w1t_out.ap())
            mk.dma("G", dbg["w2t"], w2t_out.ap())
            mk.dma("G", dbg["qkvw"], qkv_out.ap())
            mk.dma("G", dbg["wow"], wo_out.ap())

        wslot_guard = {0: 0, 1: 0}

        def emit_ffn():
            mk.wnow("T", "ve")
            mk.wnow("T", "se")
            w1v = w1t_out.ap().rearrange("(dc p) h -> p dc h", p=128)
            w2v = w2t_out.ap()
            for hf in range(2):
                psfree = {}
                hsqready = {}
                hsq2ready = {}
                msdone_te = {}
                msevac_ve = {}
                sup_te = {}
                mk.wnow("S", "te")
                for sp in range(32):
                    mk.w("Y", "te", wslot_guard[sp % 2])
                    mk.w("Y", "ccA", cc_w1t)
                    ld = mk.dma("Y", wbslot(sp % 2, 256),
                                w1v[:, :, sp * 256:(sp + 1) * 256],
                                sem="w%d" % (sp % 2))
                    mk.w("T", "w%d" % (sp % 2), ld)
                    for cc2 in range(2):
                        k = sp * 2 + cc2
                        if k >= 3:
                            mk.w("T", "se", psfree[k - 3])
                        te_g = 0
                        for dc in range(16):
                            fn = (lambda e, k=k, cc2=cc2, dc=dc, sp=sp, hf=hf:
                                  e.matmul(PS[k % 3][:],
                                           wbslot(sp % 2, 256)[:, dc, cc2 * 128:(cc2 + 1) * 128],
                                           xlntv[:, dc, hf * 512:(hf + 1) * 512],
                                           start=(dc == 0), stop=(dc == 15)))
                            if dc == 15:
                                te_g = mk.dinc("T", fn, "te")
                            else:
                                mk.d("T", fn)
                        mk.w("S", "te", te_g)
                        if k >= 2:
                            mk.w("S", "ve", hsqready.get(k - 2, 0))
                        sd = mk.dinc("S", lambda e, k=k: e.activation(
                            fsv[:, k % 2, :], PS[k % 3][:], ACT.Relu), "se")
                        psfree[k] = sd
                        mk.w("V", "se", sd)
                        if k >= 2:
                            mk.w("V", "te", msdone_te[(k - 2) // 2])
                        hsqready[k] = mk.dinc("V", lambda e, k=k: e.tensor_tensor(
                            hsqv[:, k, :], fsv[:, k % 2, :], fsv[:, k % 2, :], ALU.mult), "ve")
                        mk.drain("V")
                        hsq2ready[k] = mk.dinc("V", lambda e, k=k: e.tensor_tensor(
                            hsq2v[:, k % 2, :], hsqv[:, k, :], hsqv[:, k, :], ALU.mult), "ve")
                    mk.w("T", "ve", hsq2ready[sp * 2 + 1])
                    if sp >= 1:
                        mk.w("T", "ve", msevac_ve[sp - 1])
                    for cc2 in range(2):
                        k = sp * 2 + cc2
                        fn = lambda e, k=k, cc2=cc2: e.matmul(
                            ps_ms[:], ones_bf[:], hsq2v[:, k % 2, :],
                            start=(cc2 == 0), stop=(cc2 == 1))
                        if cc2 == 1:
                            msdone_te[sp] = mk.dinc("T", fn, "te")
                        else:
                            mk.d("T", fn)
                    sup_te[sp] = msdone_te[sp]
                    wslot_guard[sp % 2] = te_g
                    mk.w("V", "te", msdone_te[sp])
                    if sp == 0:
                        msevac_ve[sp] = mk.dinc("V", lambda e: e.tensor_copy(
                            msv[:, 0, :], ps_ms[:]), "ve")
                    else:
                        mk.d("V", lambda e: e.tensor_copy(s_sb[0:1, :], ps_ms[:]))
                        mk.drain("V")
                        msevac_ve[sp] = mk.dinc("V", lambda e, sp=sp: e.tensor_tensor(
                            msv[:, sp % 2, :], s_sb[0:1, :], msv[:, 1 - sp % 2, :],
                            ALU.add), "ve")
                if debug >= 2:
                    mk.wnow("G", "ve")
                    mk.dma("G", dbg["msraw"].rearrange("o (i f) -> o i f", f=512)[:, hf, :],
                           msv[:, 1, :])
                    if hf == 0:
                        mk.dma("G", dbg["hsq2d"], hsq2[:])
                mrow = msv[:, 1, :]
                mk.drain("V")
                mk.d("V", lambda e, mrow=mrow: e.tensor_scalar(
                    mrow, mrow, 1.0 / HID, EPS_RMS, ALU.mult, ALU.add))
                mk.drain("V")
                vd = mk.dinc("V", lambda e, mrow=mrow: e.reciprocal(mrow, mrow), "ve")
                mk.w("S", "ve", vd)
                sd = mk.dinc("S", lambda e, mrow=mrow: e.activation(mrow, mrow, ACT.Sqrt), "se")
                mk.w("V", "se", sd)
                vd = mk.dinc("V", lambda e, mrow=mrow: e.tensor_scalar(
                    mrow, mrow, 0.5, None, ALU.mult), "ve")
                mk.w("T", "ve", vd)
                for b4 in range(4):
                    fn = lambda e, b4=b4, mrow=mrow: e.transpose(
                        ps_tp32[:, b4 * 128:b4 * 128 + 1],
                        mrow[0:1, b4 * 128:(b4 + 1) * 128], identf[0:1, 0:1])
                    if b4 == 3:
                        td = mk.dinc("T", fn, "te")
                    else:
                        mk.d("T", fn)
                mk.w("V", "te", td)
                mk.dinc("V", lambda e, hf=hf: e.tensor_copy(
                    rs_t[:, hf * 4:(hf + 1) * 4],
                    ps_tp32[:].rearrange("p (b f) -> p b f", f=128)[:, :, 0]), "ve")

                grp_half_te = {}
                stt_ve = {}
                for dcg in range(16):
                    ld_half = {}
                    for half in range(2):
                        hb = dcg * 2 + half
                        mk.w("Y", "te", wslot_guard[hb % 2])
                        mk.w("Y", "ccB", cc_w2t)
                        for r8 in range(4):
                            r = half * 4 + r8
                            mk.dma("Y", wbslot(hb % 2, 128)[:, r8 * 8:(r8 + 1) * 8, :],
                                   w2v[r, dcg].rearrange("(hc p) f -> p hc f", p=128),
                                   sem="w%d" % (hb % 2))
                        ld_half[half] = (hb % 2, mk.cnt["w%d" % (hb % 2)])
                    if dcg >= 3:
                        mk.w("T", "ve", stt_ve[dcg - 3])
                    for c in range(64):
                        hb = dcg * 2 + (c // 32)
                        if c in (0, 32):
                            sl, lv = ld_half[c // 32]
                            mk.w("T", "w%d" % sl, lv)
                        fn = (lambda e, dcg=dcg, c=c, hb=hb:
                              e.matmul(PS[dcg % 3][:],
                                       wbslot(hb % 2, 128)[:, c % 32, :],
                                       hsqv[:, c, :],
                                       start=(c == 0), stop=(c == 63)))
                        if c == 31:
                            grp_half_te[dcg * 2] = mk.dinc("T", fn, "te")
                        elif c == 63:
                            grp_half_te[dcg * 2 + 1] = mk.dinc("T", fn, "te")
                        else:
                            mk.d("T", fn)
                    grp_te = grp_half_te[dcg * 2 + 1]
                    wslot_guard[0] = grp_half_te[dcg * 2]
                    wslot_guard[1] = grp_te
                    mk.w("V", "te", grp_te)
                    ev = mk.dinc("V", lambda e, dcg=dcg: e.tensor_copy(
                        fsv[:, dcg % 2, :], PS[dcg % 3][:]), "ve")
                    val = 0
                    for b4 in range(4):
                        mk.w("T", "ve", ev if b4 == 0 else mk.cnt["ve"])
                        td = mk.dinc("T", lambda e, dcg=dcg, b4=b4: e.transpose(
                            ps_tp32[:, b4 * 128:(b4 + 1) * 128],
                            fsv[:, dcg % 2, b4 * 128:(b4 + 1) * 128], identf[:]), "te")
                        mk.w("V", "te", td)
                        t_ = hf * 4 + b4
                        val = mk.dinc("V", lambda e, t_=t_, dcg=dcg, b4=b4:
                                      e.scalar_tensor_tensor(
                                          residv[:, t_, dcg * 128:(dcg + 1) * 128],
                                          ps_tp32[:, b4 * 128:(b4 + 1) * 128],
                                          rs_t[:, t_:t_ + 1],
                                          residv[:, t_, dcg * 128:(dcg + 1) * 128],
                                          ALU.mult, ALU.add), "ve")
                    stt_ve[dcg] = val

        def emit_attn(upto=99):
            mk.wnow("T", "ve")
            qv = qkv_out.ap().rearrange("(dc p) f -> p dc f", p=128)
            ktov = kt_out.ap()
            wov = wo_out.ap()
            sup_order = list(range(8, 16)) + list(range(0, 8)) + list(range(16, 24))
            sup_te2 = {}
            kt_written = 0
            cc_kt = None
            for si, sp in enumerate(sup_order):
                mk.w("Y", "te", wslot_guard[si % 2])
                mk.w("Y", "ccC", cc_qkv)
                ld = mk.dma("Y", wbslot(si % 2, 256),
                            qv[:, :, sp * 256:(sp + 1) * 256],
                            sem="w%d" % (si % 2))
                mk.w("T", "w%d" % (si % 2), ld)
                base_col = sp * 256
                g_te = 0
                if base_col < 4096:
                    for cc2 in range(2):
                        col = base_col + cc2 * 128
                        ch = col // 128
                        for tb in range(2):
                            gidx = ch * 2 + tb
                            mk.wnow("T", "ve")
                            for dc in range(16):
                                fn = (lambda e, si=si, cc2=cc2, dc=dc, tb=tb, gidx=gidx:
                                      e.matmul(PS[gidx % 3][:],
                                               wbslot(si % 2, 256)[:, dc, cc2 * 128:(cc2 + 1) * 128],
                                               xlntv[:, dc, tb * 512:(tb + 1) * 512],
                                               start=(dc == 0), stop=(dc == 15)))
                                if dc == 15:
                                    g_te = mk.dinc("T", fn, "te")
                                else:
                                    mk.d("T", fn)
                            mk.w("V", "te", g_te)
                            if col < 2048:
                                mk.dinc("V", lambda e, ch=ch, tb=tb, gidx=gidx: e.tensor_copy(
                                    qtv[:, ch, tb * 512:(tb + 1) * 512],
                                    PS[gidx % 3][:]), "ve")
                            else:
                                kc = ch - 16
                                mk.dinc("V", lambda e, kc=kc, tb=tb, gidx=gidx: e.tensor_copy(
                                    ktsv[:, kc % 2, tb * 512:(tb + 1) * 512],
                                    PS[gidx % 3][:]), "ve")
                        if col >= 2048:
                            kc = ch - 16
                            mk.wnow("G", "ve")
                            mk.dma("G", kt_in.ap()[kc], ktsv[:, kc % 2, :])
                            kt_written += 1
                            if kt_written == 16:
                                mk.w("G", "gd", mk.cnt["gd"])
                                cc_kt = mk.dinc("G", lambda e: e.collective_compute(
                                    "AllGather", ALU.bypass, replica_groups=G4,
                                    ins=[kt_in.ap().opt()], outs=[kt_out.ap().opt()]),
                                    "ccE")
                else:
                    vc0 = base_col - 4096
                    for tt in range(NT):
                        mk.wnow("T", "ve")
                        for dc in range(16):
                            fn = (lambda e, si=si, dc=dc, tt=tt:
                                  e.matmul(PS[tt % 3][:, 0:256],
                                           xlntv[:, dc, tt * 128:(tt + 1) * 128],
                                           wbslot(si % 2, 256)[:, dc, :],
                                           start=(dc == 0), stop=(dc == 15)))
                            if dc == 15:
                                g_te = mk.dinc("T", fn, "te")
                            else:
                                mk.d("T", fn)
                        mk.w("V", "te", g_te)
                        mk.dinc("V", lambda e, tt=tt, vc0=vc0: e.tensor_copy(
                            vv[:, tt, vc0:vc0 + 256], PS[tt % 3][:, 0:256]), "ve")
                sup_te2[si] = g_te
                wslot_guard[si % 2] = g_te

            if upto < 1:
                return
            mk.wnow("T", "ve")
            mk.wnow("T", "se")
            se_exp = {}
            te_sdone = {}
            ve_sacc = {}
            ve_r = {}
            head_te = {}
            g = 0
            for h in range(NH):
                mk.w("Y", "te", wslot_guard[h % 2])
                mk.w("Y", "ccE", cc_kt)
                for r4 in range(4):
                    mk.dma("Y", wbv[:, h % 2, r4 * 1024:(r4 + 1) * 1024], ktov[r4, h],
                           sem="w%d" % (h % 2))
                ktl = mk.cnt["w%d" % (h % 2)]
                mk.w("T", "w%d" % (h % 2), ktl)
                for qt in range(8):
                    it = h * 8 + qt
                    for kc in range(8):
                        if g >= 3:
                            mk.w("T", "se", se_exp[g - 3])
                        fn = (lambda e, h=h, qt=qt, kc=kc, g=g: e.matmul(
                            PS[g % 3][:],
                            qtv[:, h, qt * 128:(qt + 1) * 128],
                            wbv[:, h % 2, kc * 512:(kc + 1) * 512],
                            start=True, stop=True))
                        te_mm = mk.dinc("T", fn, "te")
                        head_te[h] = te_mm
                        if SPART < 2:
                            se_exp[g] = mk.dinc("S", lambda e: e.copy(
                                zs[:, 0:1], zs[:, 0:1]), "se")
                            g += 1
                            continue
                        mk.w("S", "te", te_mm)
                        if it >= 2 and kc == 0:
                            mk.w("S", "te", te_sdone[it - 2])
                            mk.w("S", "ve", ve_r[it - 2])
                        se_exp[g] = mk.dinc("S", lambda e, it=it, kc=kc, g=g: e.activation(
                            ebv[:, it % 2, kc * 512:(kc + 1) * 512],
                            PS[g % 3][:], ACT.Exp, scale=SCL,
                            accum_out=zpv[:, it % 2, kc:kc + 1]), "se")
                        g += 1
                    if SPART < 3:
                        ve_r[it] = mk.dinc("V", lambda e: e.memset(
                            rbf[:, (it % 2) * 8:(it % 2) * 8 + 1], 1.0), "ve")
                        te_sdone[it] = head_te[h]
                        ve_sacc[it] = ve_r[it]
                        if qt == 7:
                            wslot_guard[h % 2] = head_te[h]
                        continue
                    mk.w("V", "se", se_exp[g - 1])
                    mk.d("V", lambda e, it=it: e.reduce_sum(
                        zs[:, it % 2:it % 2 + 1], zpv[:, it % 2, :], axis=AX.X))
                    mk.drain("V")
                    mk.d("V", lambda e, it=it: e.reciprocal(
                        zs[:, it % 2:it % 2 + 1], zs[:, it % 2:it % 2 + 1]))
                    mk.drain("V")
                    ve_r[it] = mk.dinc("V", lambda e, it=it: e.tensor_copy(
                        rbf[:, (it % 2) * 8:(it % 2) * 8 + 1], zs[:, it % 2:it % 2 + 1]), "ve")
                    if SPART < 4:
                        te_sdone[it] = head_te[h]
                        ve_sacc[it] = ve_r[it]
                        if qt == 7:
                            wslot_guard[h % 2] = head_te[h]
                        continue
                    mk.w("T", "ve", ve_r[it])
                    if it >= 2:
                        mk.w("T", "ve", ve_sacc[it - 2])
                    for kc8 in range(32):
                        fn = (lambda e, it=it, kc8=kc8: e.matmul(
                            PSS[it % 2][:, kc8:kc8 + 1],
                            ebv[:, it % 2, kc8 * 128:(kc8 + 1) * 128],
                            rbf[:, (it % 2) * 8:(it % 2) * 8 + 1],
                            start=True, stop=True))
                        if kc8 == 31:
                            te_sdone[it] = mk.dinc("T", fn, "te")
                        else:
                            mk.d("T", fn)
                    mk.w("V", "te", te_sdone[it])
                    if qt == 7:
                        wslot_guard[h % 2] = head_te[h]
                    if SPART < 5:
                        ve_sacc[it] = te_sdone[it] * 0 + mk.cnt["ve"]
                        continue
                    if qt == 0:
                        ve_sacc[it] = mk.dinc("V", lambda e, h=h, it=it: e.tensor_copy(
                            s_sb[:, h * 32:(h + 1) * 32], PSS[it % 2][:, 0:32]), "ve")
                    else:
                        ve_sacc[it] = mk.dinc("V", lambda e, h=h, it=it:
                                              e.scalar_tensor_tensor(
                                                  s_sb[:, h * 32:(h + 1) * 32],
                                                  PSS[it % 2][:, 0:32], 1.0,
                                                  s_sb[:, h * 32:(h + 1) * 32],
                                                  ALU.mult, ALU.add), "ve")

            if upto < 2:
                return
            for b4 in range(4):
                mk.w("T", "ve", ve_sacc[NH * 8 - 1] if b4 == 0 else mk.cnt["ve"])
                td = mk.dinc("T", lambda e, b4=b4: e.transpose(
                    ps_tp32[:, b4 * 128:(b4 + 1) * 128],
                    s_sb[:, b4 * 128:(b4 + 1) * 128], identf[:]), "te")
                mk.w("V", "te", td)
                mk.dinc("V", lambda e, b4=b4: e.tensor_copy(
                    fstage[:, b4 * 128:(b4 + 1) * 128],
                    ps_tp32[:, b4 * 128:(b4 + 1) * 128]), "ve")
            mk.wnow("G", "ve")
            dfull = s_in.ap().rearrange("r (blk hlo) (kc8 f) -> hlo blk r kc8 f",
                                        hlo=4, kc8=8)
            for blk in range(4):
                for hlo in range(4):
                    mk.dma("G", dfull[hlo, blk],
                           fstage[hlo * 32:(hlo + 1) * 32,
                                  blk * 128:(blk + 1) * 128])
            mk.w("G", "gd", mk.cnt["gd"])
            cc_s = mk.dinc("G", lambda e: e.collective_compute(
                "ReduceScatter", ALU.add, replica_groups=G4,
                ins=[s_in.ap().opt()], outs=[s_out.ap().opt()]), "ccF")
            mk.w("G", "ccF", cc_s)
            sld = mk.dma("G", stmp, s_out.ap())
            for t in range(NT):
                mk.w("T", "gd", sld)
                mk.wnow("T", "ve")
                td = mk.dinc("T", lambda e, t=t: e.transpose(
                    ps_tp32[:, 0:16], stmp[:, t * 128:(t + 1) * 128],
                    identf[0:16, 0:16]), "te")
                mk.w("V", "te", td)
                mk.dinc("V", lambda e, t=t: e.tensor_copy(
                    stv[:, t, :], ps_tp32[:, 0:16]), "ve")

            if upto < 3:
                return
            for t in range(NT):
                for h in range(NH):
                    i = t * NH + h
                    mk.wnow("V", "te")
                    vd = mk.dinc("V", lambda e, t=t, h=h, i=i: e.tensor_scalar(
                        vstv[:, i % 2, :], vv[:, t, h * 128:(h + 1) * 128],
                        stv[:, t, h:h + 1], None, ALU.mult), "ve")
                    mk.w("T", "ve", vd)
                    k8 = i % 8
                    fn = lambda e, k8=k8, i=i: e.transpose(
                        ps_tpb[:, k8 * 128:(k8 + 1) * 128], vstv[:, i % 2, :], identb[:])
                    mk.dinc("T", fn, "te")
                    if k8 == 7:
                        mk.wnow("V", "te")
                        t0, h0 = (i - 7) // NH, (i - 7) % NH
                        dst2 = valtv[:, h0:h0 + 8, t0 * 128:(t0 + 1) * 128]
                        mk.dinc("V", lambda e, dst2=dst2: e.tensor_copy(
                            dst2, ps_tpb[:].rearrange("p (c f) -> p c f", f=128)), "ve")

            if upto < 4:
                return
            grp_te = {}
            mk.wnow("T", "ve")
            mk.wnow("T", "se")
            for dcg in range(16):
                mk.w("Y", "te", wslot_guard[dcg % 2])
                mk.w("Y", "ccD", cc_wo)
                for r8 in range(8):
                    mk.dma("Y", wbslot(dcg % 2, 128)[:, r8 * 2:(r8 + 1) * 2, :],
                           wov[r8, dcg].rearrange("(vc p) f -> p vc f", p=128),
                           sem="w%d" % (dcg % 2))
                ld = mk.cnt["w%d" % (dcg % 2)]
                for tb in range(2):
                    mk.w("T", "w%d" % (dcg % 2), ld)
                    for vc in range(16):
                        fn = (lambda e, dcg=dcg, vc=vc, tb=tb: e.matmul(
                            PS[(dcg * 2 + tb) % 3][:],
                            wbslot(dcg % 2, 128)[:, vc, :],
                            valtv[:, vc, tb * 512:(tb + 1) * 512],
                            start=(vc == 0), stop=(vc == 15)))
                        if vc == 15:
                            grp_te[(dcg, tb)] = mk.dinc("T", fn, "te")
                        else:
                            mk.d("T", fn)
                    if tb == 1:
                        wslot_guard[dcg % 2] = grp_te[(dcg, tb)]
                    mk.w("V", "te", grp_te[(dcg, tb)])
                    ev = mk.dinc("V", lambda e, dcg=dcg, tb=tb: e.tensor_copy(
                        fsv[:, tb, :], PS[(dcg * 2 + tb) % 3][:]), "ve")
                    for b4 in range(4):
                        mk.w("T", "ve", ev if b4 == 0 else mk.cnt["ve"])
                        td = mk.dinc("T", lambda e, tb=tb, b4=b4: e.transpose(
                            ps_tp32[:, b4 * 128:(b4 + 1) * 128],
                            fsv[:, tb, b4 * 128:(b4 + 1) * 128], identf[:]), "te")
                        mk.w("V", "te", td)
                        t_ = tb * 4 + b4
                        mk.dinc("V", lambda e, t_=t_, dcg=dcg, b4=b4:
                                e.scalar_tensor_tensor(
                                    residv[:, t_, dcg * 128:(dcg + 1) * 128],
                                    ps_tp32[:, b4 * 128:(b4 + 1) * 128], 1.0,
                                    residv[:, t_, dcg * 128:(dcg + 1) * 128],
                                    ALU.mult, ALU.add), "ve")

        if stage >= 2:
            emit_ffn()
            if debug >= 2 and stage < 9:
                mk.wnow("G", "ve")
                mk.dma("G", dbg["x1"], resid[:])
                mk.dma("G", dbg["ms"], ms_sb[:])
                mk.dma("G", dbg["ones"], ones_bf[:])
                mk.dma("G", dbg["rst"], rs_t[:])
                mk.dma("G", dbg["hsq"], work[:])
        if stage >= 3:
            emit_ln()
            emit_attn(upto=int(stage) - 3 if stage < 9 else 99)
            if debug >= 3 and stage >= 7:
                mk.wnow("G", "ve")
                mk.dma("G", dbg["x2"], resid[:])
                mk.dma("G", dbg["qt"], work[:, 0:16 * 1024])
                mk.dma("G", dbg["vvd"], work[:, 16 * 1024:32 * 1024])
                mk.dma("G", dbg["ktf"], kt_out.ap())
                mk.dma("G", dbg["ssb"], s_sb[:])
                mk.dma("G", dbg["std"], s_t[:])
        if stage >= 9:
            emit_ln()
            emit_ffn()

        mk.wnow("G", "ve")
        mk.wnow("G", "se")
        mk.wnow("G", "te")
        mk.dma("G", y_out.rearrange("(t p) d -> p t d", p=128),
               resid[:].rearrange("p (t d) -> p t d", d=D))
        mk.w("G", "gd", mk.cnt["gd"])

        @block.sync
        def _(e):
            for f in mk.ops["Y"]:
                f(e)

        @block.gpsimd
        def _(e):
            for f in mk.ops["G"]:
                f(e)

        @block.vector
        def _(e):
            for f in mk.ops["V"]:
                f(e)

        @block.scalar
        def _(e):
            for f in mk.ops["S"]:
                f(e)

        @block.tensor
        def _(e):
            for f in mk.ops["T"]:
                f(e)

    return nc


def host_inputs(L0_attn_x, L0_attn_w_qkv, L0_attn_w_o, L0_ffn_W_fc, L0_ffn_W_proj):
    x = np.ascontiguousarray(np.asarray(L0_attn_x, dtype=np.float32))
    wqkv = np.ascontiguousarray(np.asarray(L0_attn_w_qkv, dtype=np.float32))
    wo = np.ascontiguousarray(np.asarray(L0_attn_w_o, dtype=np.float32))
    wfc = np.ascontiguousarray(np.asarray(L0_ffn_W_fc, dtype=np.float32))
    wpr = np.ascontiguousarray(np.asarray(L0_ffn_W_proj, dtype=np.float32))
    identb = np.eye(128, dtype=ml_dtypes.bfloat16)
    identf = np.eye(128, dtype=np.float32)
    in_maps = []
    for c in range(8):
        b, r = c // 4, c % 4
        in_maps.append({
            "x_in": np.ascontiguousarray(x[b, r * TOK:(r + 1) * TOK, :]),
            "wfc_sl": np.ascontiguousarray(wfc[:, c * 256:(c + 1) * 256]),
            "wpr_sl": np.ascontiguousarray(wpr[:, c * 1024:(c + 1) * 1024]),
            "wqkv_sl": np.ascontiguousarray(wqkv[c * 256:(c + 1) * 256, :]),
            "wo_sl": np.ascontiguousarray(wo[c * 256:(c + 1) * 256, :]),
            "identb": identb,
            "identf": identf,
        })
    return in_maps


from concourse.bass_utils import run_bass_kernel_spmd

_NC_CACHE = {}


def kernel(L0_attn_x, L0_attn_w_qkv, L0_attn_w_o, L0_ffn_W_fc, L0_ffn_W_proj):
    if "nc" not in _NC_CACHE:
        _NC_CACHE["nc"] = build(debug=0, stage=9)
    nc = _NC_CACHE["nc"]
    in_maps = host_inputs(L0_attn_x, L0_attn_w_qkv, L0_attn_w_o,
                          L0_ffn_W_fc, L0_ffn_W_proj)
    res = run_bass_kernel_spmd(nc, in_maps, core_ids=list(range(8)))
    y = np.zeros((B, S, D), np.float32)
    for c in range(8):
        b, r = c // 4, c % 4
        y[b, r * TOK:(r + 1) * TOK] = np.asarray(res.results[c]["y"], np.float32)
    return y


# version 02
# speedup vs baseline: 1.0681x; 1.0681x over previous
import math
import numpy as np
SPART = 9
import ml_dtypes
import concourse.bass as bass
import concourse.mybir as mybir
from contextlib import ExitStack

F32 = mybir.dt.float32
BF16 = mybir.dt.bfloat16
AX = mybir.AxisListType
ALU = mybir.AluOpType
ACT = mybir.ActivationFunctionType

B, S, D, HID, NH, DH = 2, 4096, 2048, 8192, 16, 128
TOK = 1024
NT = TOK // 128
G4 = [[0, 1, 2, 3], [4, 5, 6, 7]]
G8 = [list(range(8))]
QC = 0.501953125
EPS_LN = 1e-5
EPS_RMS = float(np.finfo(np.float32).eps)
SCL = float(1.0 / math.sqrt(DH))

ENGS = ("T", "V", "S", "G", "Y")


class MK:
    def __init__(self, nc):
        self.nc = nc
        self.ops = {e: [] for e in ENGS}
        self.sem = {}
        self.cnt = {}

    def mksem(self, ctx, name):
        self.sem[name] = ctx.enter_context(self.nc.semaphore(name))
        self.cnt[name] = 0

    def w(self, eng, sem, val):
        if val <= 0:
            return
        h = self.sem[sem]
        self.ops[eng].append(lambda e, h=h, v=val: e.wait_ge(h, v))

    def wnow(self, eng, sem):
        self.w(eng, sem, self.cnt[sem])

    def d(self, eng, fn):
        self.ops[eng].append(fn)

    def dinc(self, eng, fn, sem, k=1):
        h = self.sem[sem]
        self.cnt[sem] += k
        self.ops[eng].append(lambda e, fn=fn, h=h, k=k: fn(e).then_inc(h, k))
        return self.cnt[sem]

    def dma(self, eng, out, in_, sem=None):
        if sem is None:
            sem = "yd" if eng == "Y" else "gd"
        return self.dinc(eng, lambda e, o=out, i=in_: e.dma_start(out=o, in_=i), sem, 16)

    def drain(self, eng):
        self.ops[eng].append(lambda e: e.drain())


def build(debug=0, stage=9):
    nc = bass.Bass("TRN2", target_bir_lowering=False)

    x_in = nc.dram_tensor("x_in", [TOK, D], F32, kind="ExternalInput").ap()
    wfc_sl = nc.dram_tensor("wfc_sl", [HID, 256], F32, kind="ExternalInput").ap()
    wpr_sl = nc.dram_tensor("wpr_sl", [D, 1024], F32, kind="ExternalInput").ap()
    wqkv_sl = nc.dram_tensor("wqkv_sl", [256, 3 * D], F32, kind="ExternalInput").ap()
    wo_sl = nc.dram_tensor("wo_sl", [256, D], F32, kind="ExternalInput").ap()
    identb_h = nc.dram_tensor("identb", [128, 128], BF16, kind="ExternalInput").ap()
    identf_h = nc.dram_tensor("identf", [128, 128], F32, kind="ExternalInput").ap()
    y_out = nc.dram_tensor("y", [TOK, D], F32, kind="ExternalOutput").ap()

    w1t_in = nc.dram_tensor("w1t_in", [256, HID], BF16)
    w1t_out = nc.dram_tensor("w1t_out", [D, HID], BF16)
    w2t_in = nc.dram_tensor("w2t_in", [16, 1024, 128], BF16)
    w2t_out = nc.dram_tensor("w2t_out", [8, 16, 1024, 128], BF16)
    qkv_in = nc.dram_tensor("qkv_in", [256, 3 * D], BF16)
    qkv_out = nc.dram_tensor("qkv_out", [D, 3 * D], BF16)
    wo_in = nc.dram_tensor("wo_in", [16, 256, 128], BF16)
    wo_out = nc.dram_tensor("wo_out", [8, 16, 256, 128], BF16)
    kt_in = nc.dram_tensor("kt_in", [16, 128, 1024], BF16)
    kt_out = nc.dram_tensor("kt_out", [4, 16, 128, 1024], BF16)
    s_in = nc.dram_tensor("s_in", [4, 16, 1024], F32)
    s_out = nc.dram_tensor("s_out", [16, 1024], F32)

    dbg = {}
    if debug:
        def dbgt(name, shape, dt):
            dbg[name] = nc.dram_tensor("d_" + name, shape, dt, kind="ExternalOutput").ap()
        dbgt("xlnT", [128, 16 * 1024], BF16)
        dbgt("w1t", [D, HID], BF16)
        dbgt("w2t", [8, 16, 1024, 128], BF16)
        dbgt("qkvw", [D, 3 * D], BF16)
        dbgt("wow", [8, 16, 256, 128], BF16)
        if debug >= 2:
            dbgt("x1", [128, NT * D], F32)
            dbgt("ms", [1, 1024], F32)
            dbgt("msraw", [1, 1024], F32)
            dbgt("ones", [128, 1], BF16)
            dbgt("hsq2d", [128, 2 * 512], BF16)
            dbgt("rst", [128, 8], F32)
            dbgt("hsq", [128, 64 * 512], BF16)
        if debug >= 3:
            dbgt("qt", [128, 16 * 1024], BF16)
            dbgt("vvd", [128, 8 * 2048], BF16)
            dbgt("ktf", [4, 16, 128, 1024], BF16)
            dbgt("ssb", [128, 512], F32)
            dbgt("std", [128, 8 * 16], F32)
            dbgt("x2", [128, NT * D], F32)

    mk = MK(nc)
    with ExitStack() as ctx:
        sb = lambda name, shape, dt: ctx.enter_context(nc.sbuf_tensor(name, shape, dt))
        psb = lambda name, shape, dt: ctx.enter_context(nc.psum_tensor(name, shape, dt))

        resid = sb("resid", [128, NT * D], F32)
        xlnT = sb("xlnT", [128, 16 * 1024], BF16)
        work = sb("work", [128, 64 * 512], BF16)
        ebuf = sb("ebuf", [128, 2 * 4096], BF16)
        wbuf = sb("wbuf", [128, 2 * 4096], BF16)
        hsq2 = sb("hsq2", [128, 2 * 512], BF16)
        fstage = sb("fstage", [128, 2 * 512], F32)
        s_sb = sb("s_sb", [128, 512], F32)
        identb = sb("identb_s", [128, 128], BF16)
        identf = sb("identf_s", [128, 128], F32)
        ones_bf = sb("ones_bf", [128, 1], BF16)
        bnst = sb("bnst", [128, 24], F32)
        bnag = sb("bnag", [128, 2], F32)
        m_v = sb("m_v", [128, 1], F32)
        rstd_v = sb("rstd_v", [128, 1], F32)
        bias_v = sb("bias_v", [128, 1], F32)
        zp = sb("zp", [128, 2 * 8], F32)
        zs = sb("zs", [128, 2], F32)
        rbf = sb("rbf", [128, 16], BF16)
        ms_sb = sb("ms_sb", [1, 2 * 512], F32)
        rs_t = sb("rs_t", [128, 8], F32)
        s_t = sb("s_t", [128, 8 * 16], F32)
        qsc_s = sb("qsc_s", [128, 64], F32)
        qsb_s = sb("qsb_s", [128, 64], BF16)
        qth_s = sb("qth_s", [128, 64], F32)
        vstage = sb("vstage", [128, 2 * 128], BF16)

        ps0 = psb("ps0", [128, 512], F32)
        ps1 = psb("ps1", [128, 512], F32)
        ps2 = psb("ps2", [128, 512], F32)
        ps_tpb = psb("ps_tpb", [128, 1024], BF16)
        ps_tp32 = psb("ps_tp32", [128, 512], F32)
        ps_ms = psb("ps_ms", [1, 512], F32)
        ps_s0 = psb("ps_s0", [128, 512], F32)
        ps_s1 = psb("ps_s1", [128, 512], F32)
        PS = [ps0, ps1, ps2]
        PSS = [ps_s0, ps_s1]

        for s in ["yd", "gd", "te", "ve", "se", "w0", "w1",
                  "ccA", "ccB", "ccC", "ccD", "ccE", "ccF"]:
            mk.mksem(ctx, s)
        block = ctx.enter_context(nc.Block())

        residv = resid[:].rearrange("p (t d) -> p t d", d=D)
        xlntv = xlnT[:].rearrange("p (c f) -> p c f", f=1024)
        hsqv = work[:].rearrange("p (c f) -> p c f", f=512)
        qtv = work[:, 0:16 * 1024].rearrange("p (c f) -> p c f", f=1024)
        vv = work[:, 16 * 1024:32 * 1024].rearrange("p (t f) -> p t f", f=D)
        valtv = xlnT[:].rearrange("p (c f) -> p c f", f=1024)
        ebv = ebuf[:].rearrange("p (i f) -> p i f", f=4096)
        wbv = wbuf[:].rearrange("p (i f) -> p i f", f=4096)
        ktsv = fstage[:].bitcast(BF16).rearrange("p (i f) -> p i f", f=1024)
        hsq2v = hsq2[:].rearrange("p (i f) -> p i f", f=512)
        fsv = fstage[:].rearrange("p (i f) -> p i f", f=512)
        zpv = zp[:].rearrange("p (i f) -> p i f", f=8)
        msv = ms_sb[:].rearrange("p (i f) -> p i f", f=512)
        stv = s_t[:].rearrange("p (t h) -> p t h", h=16)
        vstv = vstage[:].rearrange("p (i f) -> p i f", f=128)
        xlnb = ebuf[:, 4096:4096 + 2048]
        stmp = ebuf[:, 0:4096].bitcast(F32)[0:16, 0:1024]
        qstage32 = work[:, 0:32 * 1024].bitcast(F32)

        def wbslot(i, f):
            return wbv[:, i, :].rearrange("p (c f) -> p c f", f=f)

        mk.dma("G", resid[:].rearrange("p (t d) -> p t d", d=D),
               x_in.rearrange("(t p) d -> p t d", p=128))
        x_done = mk.cnt["gd"]
        mk.dma("Y", identb[:], identb_h)
        mk.dma("Y", identf[:], identf_h)
        id_done = mk.cnt["yd"]
        mk.d("V", lambda e: e.memset(ones_bf[:], 1.0))

        def quant_stage(src_ap, fd, inner):
            mk.wnow("Y", "ve")
            ld = mk.dma("Y", qstage32[:, 0:fd].rearrange(
                "p (a b) -> p a b", b=inner), src_ap)
            mk.w("V", "yd", ld)
            mk.wnow("V", "gd")
            mk.wnow("V", "te")
            wb = ebv[:, 0, 0:fd]
            wb3 = wb.rearrange("p (g k) -> p g k", k=64)
            g = fd // 64
            mk.d("V", lambda e, wb=wb, fd=fd: e.tensor_copy(wb, qstage32[:, 0:fd]))
            mk.drain("V")
            mk.d("V", lambda e, wb3=wb3, g=g: e.tensor_reduce(
                qsc_s[:, 0:g], wb3, axis=AX.X, op=ALU.add, apply_absolute_value=True))
            mk.drain("V")
            mk.d("V", lambda e, g=g: e.tensor_scalar(
                qsb_s[:, 0:g], qsc_s[:, 0:g], 1.0 / 64.0, None, ALU.mult))
            mk.drain("V")
            mk.d("V", lambda e, g=g: e.tensor_scalar(
                qsc_s[:, 0:g], qsb_s[:, 0:g], 1e-8, None, ALU.max))
            mk.drain("V")
            mk.d("V", lambda e, g=g: e.tensor_scalar(
                qth_s[:, 0:g], qsc_s[:, 0:g], QC, None, ALU.mult))
            mk.drain("V")
            thb = qth_s[:, 0:g].unsqueeze(2).broadcast_to([128, g, 64])
            scb = qsc_s[:, 0:g].unsqueeze(2).broadcast_to([128, g, 64])
            mp = ebv[:, 1, 0:fd].rearrange("p (g k) -> p g k", k=64)
            mn = work[:, 8192:8192 + fd].rearrange("p (g k) -> p g k", k=64)
            mk.d("V", lambda e, mp=mp, wb3=wb3, thb=thb: e.scalar_tensor_tensor(
                mp, wb3, 1.0, thb, ALU.mult, ALU.is_gt))
            mk.d("V", lambda e, mn=mn, wb3=wb3, thb=thb: e.scalar_tensor_tensor(
                mn, wb3, -1.0, thb, ALU.mult, ALU.is_gt))
            mk.drain("V")
            mk.d("V", lambda e, mp=mp, mn=mn: e.tensor_tensor(mp, mp, mn, ALU.subtract))
            mk.drain("V")
            return mk.dinc("V", lambda e, wb3=wb3, mp=mp, scb=scb: e.tensor_tensor(
                wb3, mp, scb, ALU.mult), "ve")

        first_tb = [True]

        def transpose_blocks(nblk, src_fn, pairs_fn, vready):
            if first_tb[0]:
                mk.w("T", "yd", id_done)
                first_tb[0] = False
            nbat = (nblk + 7) // 8
            for b in range(nbat):
                k0, k1 = b * 8, min(b * 8 + 8, nblk)
                mk.w("T", "ve", vready if b == 0 else mk.cnt["ve"])
                mk.wnow("T", "gd")
                for k in range(k0, k1):
                    fn = lambda e, k=k, j=k - k0: e.transpose(
                        ps_tpb[:, j * 128:(j + 1) * 128], src_fn(k), identb[:])
                    if k == k1 - 1:
                        mk.dinc("T", fn, "te")
                    else:
                        mk.d("T", fn)
                mk.wnow("V", "te")
                stage = ktsv[:, b % 2, :]
                mk.dinc("V", lambda e, st=stage: e.tensor_copy(st, ps_tpb[:]), "ve")
                mk.wnow("G", "ve")
                for dst_ap, src_ap in pairs_fn(b, stage):
                    mk.dma("G", dst_ap, src_ap)

        wfc4 = wfc_sl.rearrange("(st tq p) d -> st p tq d", p=128, tq=16)
        w1ap = w1t_in.ap().rearrange("(dc p) (st tq f) -> p st tq dc f",
                                     p=128, st=4, f=128)
        for st4 in range(4):
            vd = quant_stage(wfc4[st4], 4096, 256)
            srcv = ebv[:, 0, 0:4096].rearrange("p (tq dc f) -> p tq dc f", dc=2, f=128)
            def w1pairs(b, st, st4=st4):
                stv4 = st.rearrange("p (tq dc f) -> p tq dc f", dc=2, f=128)
                return [(w1ap[:, st4, b * 4:b * 4 + 4, dc, :], stv4[:, :, dc, :])
                        for dc in range(2)]
            transpose_blocks(
                32,
                lambda k, srcv=srcv: srcv[:, k // 2, k % 2, :],
                w1pairs,
                vd)
        w1t_ready = mk.cnt["gd"]

        wpr4 = wpr_sl.rearrange("(st tq p) h -> st p tq h", p=128, tq=4)
        w2ap = w2t_in.ap().rearrange("dc (hc p) f -> p dc hc f", p=128)
        for st4 in range(4):
            vd = quant_stage(wpr4[st4], 4096, 1024)
            srcv = ebv[:, 0, 0:4096].rearrange("p (tq hc f) -> p tq hc f", hc=8, f=128)
            transpose_blocks(
                32,
                lambda k, srcv=srcv: srcv[:, k // 8, k % 8, :],
                lambda b, st, st4=st4: [(w2ap[:, st4 * 4 + b, :, :],
                                         st.rearrange("p (hc f) -> p hc f", f=128))],
                vd)
        w2t_ready = mk.cnt["gd"]

        qkv2 = wqkv_sl.rearrange("(r p) f -> r p f", p=128)
        qin2 = qkv_in.ap().rearrange("(r p) f -> r p f", p=128)
        for r in range(2):
            for half in range(2):
                f0, f1 = half * 3072, (half + 1) * 3072
                mk.wnow("Y", "ve")
                ld = mk.dma("Y", qstage32[:, 0:3072], qkv2[r, :, f0:f1])
                mk.w("V", "yd", ld)
                mk.wnow("V", "gd")
                mk.wnow("V", "te")
                vd = mk.dinc("V", lambda e: e.tensor_copy(
                    ebv[:, 0, 0:3072], qstage32[:, 0:3072]), "ve")
                mk.w("G", "ve", vd)
                mk.dma("G", qin2[r, :, f0:f1], ebv[:, 0, 0:3072])
        qkv_ready = mk.cnt["gd"]

        wo2 = wo_sl.rearrange("(r p) f -> r p f", p=128)
        wo_blk = wo_in.ap().rearrange("dc (r p) f -> r p dc f", p=128)
        for r in range(2):
            mk.wnow("Y", "ve")
            ld = mk.dma("Y", qstage32[:, 0:2048], wo2[r])
            mk.w("V", "yd", ld)
            mk.wnow("V", "gd")
            mk.wnow("V", "te")
            vd = mk.dinc("V", lambda e: e.tensor_copy(
                ebv[:, 0, 0:2048], qstage32[:, 0:2048]), "ve")
            mk.w("G", "ve", vd)
            mk.dma("G", wo_blk[r], ebv[:, 0, 0:2048].rearrange("p (dc f) -> p dc f", f=128))
        wo_ready = mk.cnt["gd"]

        def ag(inp, outp, ready, sem, groups=G8):
            mk.w("G", "gd", ready)
            return mk.dinc("G", lambda e, i=inp, o=outp, g=groups: e.collective_compute(
                "AllGather", ALU.bypass, replica_groups=g,
                ins=[i.ap().opt()], outs=[o.ap().opt()]), sem)
        cc_w1t = ag(w1t_in, w1t_out, w1t_ready, "ccA")
        cc_w2t = ag(w2t_in, w2t_out, w2t_ready, "ccB")
        cc_qkv = ag(qkv_in, qkv_out, qkv_ready, "ccC")
        cc_wo = ag(wo_in, wo_out, wo_ready, "ccD")

        def emit_ln(gd_wait=0):
            if gd_wait:
                mk.w("V", "gd", gd_wait)
                mk.w("S", "gd", gd_wait)
            mk.drain("V")
            prev_xln_se = 0
            prev_tr_te = 0
            for t in range(NT):
                xt = residv[:, t, :]
                if prev_xln_se:
                    mk.w("V", "se", prev_xln_se)
                for q4 in range(4):
                    mk.d("V", lambda e, xt=xt, q4=q4: e.bn_stats(
                        bnst[:, q4 * 6:(q4 + 1) * 6], xt[:, q4 * 512:(q4 + 1) * 512]))
                mk.drain("V")
                mk.d("V", lambda e: e.bn_aggr(bnag[:], bnst[:]))
                mk.drain("V")
                mk.d("V", lambda e: e.tensor_copy(m_v[:], bnag[:, 0:1]))
                mk.d("V", lambda e: e.tensor_scalar(
                    rstd_v[:], bnag[:, 1:2], 1.0, EPS_LN, ALU.mult, ALU.add))
                mk.drain("V")
                vd = mk.dinc("V", lambda e: e.reciprocal(rstd_v[:], rstd_v[:]), "ve")
                mk.w("S", "ve", vd)
                sd = mk.dinc("S", lambda e: e.activation(rstd_v[:], rstd_v[:], ACT.Sqrt), "se")
                mk.w("V", "se", sd)
                vd = mk.dinc("V", lambda e: e.scalar_tensor_tensor(
                    bias_v[:], m_v[:], -1.0, rstd_v[:], ALU.mult, ALU.mult), "ve")
                mk.w("S", "ve", vd)
                if prev_tr_te:
                    mk.w("S", "te", prev_tr_te)
                sd = mk.dinc("S", lambda e, xt=xt: e.activation(
                    xlnb[:], xt, ACT.Identity, bias=bias_v[:], scale=rstd_v[:]), "se")
                prev_xln_se = sd
                for bb in range(2):
                    mk.w("T", "se", sd)
                    mk.wnow("T", "ve")
                    for k in range(8):
                        dc = bb * 8 + k
                        fn = lambda e, dc=dc, k=k: e.transpose(
                            ps_tpb[:, k * 128:(k + 1) * 128],
                            xlnb[:, dc * 128:(dc + 1) * 128], identb[:])
                        if k == 7:
                            prev_tr_te = mk.dinc("T", fn, "te")
                        else:
                            mk.d("T", fn)
                    mk.wnow("V", "te")
                    dst = xlntv[:, bb * 8:(bb + 1) * 8, t * 128:(t + 1) * 128]
                    mk.dinc("V", lambda e, dst=dst: e.tensor_copy(
                        dst, ps_tpb[:].rearrange("p (c f) -> p c f", f=128)), "ve")

        emit_ln(gd_wait=wo_ready)

        if debug:
            mk.wnow("G", "ve")
            mk.dma("G", dbg["xlnT"], xlnT[:])
            mk.w("G", "ccD", mk.cnt["ccD"])
            mk.dma("G", dbg["w1t"], w1t_out.ap())
            mk.dma("G", dbg["w2t"], w2t_out.ap())
            mk.dma("G", dbg["qkvw"], qkv_out.ap())
            mk.dma("G", dbg["wow"], wo_out.ap())

        wslot_guard = {0: 0, 1: 0}

        def emit_ffn():
            mk.wnow("T", "ve")
            mk.wnow("T", "se")
            w1v = w1t_out.ap().rearrange("(dc p) h -> p dc h", p=128)
            w2v = w2t_out.ap()
            for hf in range(2):
                psfree = {}
                hsqready = {}
                hsq2ready = {}
                msdone_te = {}
                msevac_ve = {}
                sup_te = {}
                mk.wnow("S", "te")
                for sp in range(32):
                    mk.w("Y", "te", wslot_guard[sp % 2])
                    mk.w("Y", "ccA", cc_w1t)
                    ld = mk.dma("Y", wbslot(sp % 2, 256),
                                w1v[:, :, sp * 256:(sp + 1) * 256],
                                sem="w%d" % (sp % 2))
                    mk.w("T", "w%d" % (sp % 2), ld)
                    for cc2 in range(2):
                        k = sp * 2 + cc2
                        if k >= 3:
                            mk.w("T", "se", psfree[k - 3])
                        te_g = 0
                        for dc in range(16):
                            fn = (lambda e, k=k, cc2=cc2, dc=dc, sp=sp, hf=hf:
                                  e.matmul(PS[k % 3][:],
                                           wbslot(sp % 2, 256)[:, dc, cc2 * 128:(cc2 + 1) * 128],
                                           xlntv[:, dc, hf * 512:(hf + 1) * 512],
                                           start=(dc == 0), stop=(dc == 15)))
                            if dc == 15:
                                te_g = mk.dinc("T", fn, "te")
                            else:
                                mk.d("T", fn)
                        mk.w("S", "te", te_g)
                        if k >= 2:
                            mk.w("S", "ve", hsqready.get(k - 2, 0))
                        sd = mk.dinc("S", lambda e, k=k: e.activation(
                            fsv[:, k % 2, :], PS[k % 3][:], ACT.Relu), "se")
                        psfree[k] = sd
                        mk.w("V", "se", sd)
                        if k >= 2:
                            mk.w("V", "te", msdone_te[(k - 2) // 2])
                        hsqready[k] = mk.dinc("V", lambda e, k=k: e.tensor_tensor(
                            hsqv[:, k, :], fsv[:, k % 2, :], fsv[:, k % 2, :], ALU.mult), "ve")
                        mk.drain("V")
                        hsq2ready[k] = mk.dinc("V", lambda e, k=k: e.tensor_tensor(
                            hsq2v[:, k % 2, :], hsqv[:, k, :], hsqv[:, k, :], ALU.mult), "ve")
                    mk.w("T", "ve", hsq2ready[sp * 2 + 1])
                    if sp >= 1:
                        mk.w("T", "ve", msevac_ve[sp - 1])
                    for cc2 in range(2):
                        k = sp * 2 + cc2
                        fn = lambda e, k=k, cc2=cc2: e.matmul(
                            ps_ms[:], ones_bf[:], hsq2v[:, k % 2, :],
                            start=(cc2 == 0), stop=(cc2 == 1))
                        if cc2 == 1:
                            msdone_te[sp] = mk.dinc("T", fn, "te")
                        else:
                            mk.d("T", fn)
                    sup_te[sp] = msdone_te[sp]
                    wslot_guard[sp % 2] = te_g
                    mk.w("V", "te", msdone_te[sp])
                    if sp == 0:
                        msevac_ve[sp] = mk.dinc("V", lambda e: e.tensor_copy(
                            msv[:, 0, :], ps_ms[:]), "ve")
                    else:
                        mk.d("V", lambda e: e.tensor_copy(s_sb[0:1, :], ps_ms[:]))
                        mk.drain("V")
                        msevac_ve[sp] = mk.dinc("V", lambda e, sp=sp: e.tensor_tensor(
                            msv[:, sp % 2, :], s_sb[0:1, :], msv[:, 1 - sp % 2, :],
                            ALU.add), "ve")
                if debug >= 2:
                    mk.wnow("G", "ve")
                    mk.dma("G", dbg["msraw"].rearrange("o (i f) -> o i f", f=512)[:, hf, :],
                           msv[:, 1, :])
                    if hf == 0:
                        mk.dma("G", dbg["hsq2d"], hsq2[:])
                mrow = msv[:, 1, :]
                mk.drain("V")
                mk.d("V", lambda e, mrow=mrow: e.tensor_scalar(
                    mrow, mrow, 1.0 / HID, EPS_RMS, ALU.mult, ALU.add))
                mk.drain("V")
                vd = mk.dinc("V", lambda e, mrow=mrow: e.reciprocal(mrow, mrow), "ve")
                mk.w("S", "ve", vd)
                sd = mk.dinc("S", lambda e, mrow=mrow: e.activation(mrow, mrow, ACT.Sqrt), "se")
                mk.w("V", "se", sd)
                vd = mk.dinc("V", lambda e, mrow=mrow: e.tensor_scalar(
                    mrow, mrow, 0.5, None, ALU.mult), "ve")
                mk.w("T", "ve", vd)
                for b4 in range(4):
                    fn = lambda e, b4=b4, mrow=mrow: e.transpose(
                        ps_tp32[:, b4 * 128:b4 * 128 + 1],
                        mrow[0:1, b4 * 128:(b4 + 1) * 128], identf[0:1, 0:1])
                    if b4 == 3:
                        td = mk.dinc("T", fn, "te")
                    else:
                        mk.d("T", fn)
                mk.w("V", "te", td)
                mk.dinc("V", lambda e, hf=hf: e.tensor_copy(
                    rs_t[:, hf * 4:(hf + 1) * 4],
                    ps_tp32[:].rearrange("p (b f) -> p b f", f=128)[:, :, 0]), "ve")

                grp_half_te = {}
                stt_ve = {}
                for dcg in range(16):
                    ld_half = {}
                    for half in range(2):
                        hb = dcg * 2 + half
                        mk.w("Y", "te", wslot_guard[hb % 2])
                        mk.w("Y", "ccB", cc_w2t)
                        for r8 in range(4):
                            r = half * 4 + r8
                            mk.dma("Y", wbslot(hb % 2, 128)[:, r8 * 8:(r8 + 1) * 8, :],
                                   w2v[r, dcg].rearrange("(hc p) f -> p hc f", p=128),
                                   sem="w%d" % (hb % 2))
                        ld_half[half] = (hb % 2, mk.cnt["w%d" % (hb % 2)])
                    if dcg >= 3:
                        mk.w("T", "ve", stt_ve[dcg - 3])
                    for c in range(64):
                        hb = dcg * 2 + (c // 32)
                        if c in (0, 32):
                            sl, lv = ld_half[c // 32]
                            mk.w("T", "w%d" % sl, lv)
                        fn = (lambda e, dcg=dcg, c=c, hb=hb:
                              e.matmul(PS[dcg % 3][:],
                                       wbslot(hb % 2, 128)[:, c % 32, :],
                                       hsqv[:, c, :],
                                       start=(c == 0), stop=(c == 63)))
                        if c == 31:
                            grp_half_te[dcg * 2] = mk.dinc("T", fn, "te")
                        elif c == 63:
                            grp_half_te[dcg * 2 + 1] = mk.dinc("T", fn, "te")
                        else:
                            mk.d("T", fn)
                    grp_te = grp_half_te[dcg * 2 + 1]
                    wslot_guard[0] = grp_half_te[dcg * 2]
                    wslot_guard[1] = grp_te
                    mk.w("V", "te", grp_te)
                    ev = mk.dinc("V", lambda e, dcg=dcg: e.tensor_copy(
                        fsv[:, dcg % 2, :], PS[dcg % 3][:]), "ve")
                    val = 0
                    for b4 in range(4):
                        mk.w("T", "ve", ev if b4 == 0 else mk.cnt["ve"])
                        td = mk.dinc("T", lambda e, dcg=dcg, b4=b4: e.transpose(
                            ps_tp32[:, b4 * 128:(b4 + 1) * 128],
                            fsv[:, dcg % 2, b4 * 128:(b4 + 1) * 128], identf[:]), "te")
                        mk.w("V", "te", td)
                        t_ = hf * 4 + b4
                        val = mk.dinc("V", lambda e, t_=t_, dcg=dcg, b4=b4:
                                      e.scalar_tensor_tensor(
                                          residv[:, t_, dcg * 128:(dcg + 1) * 128],
                                          ps_tp32[:, b4 * 128:(b4 + 1) * 128],
                                          rs_t[:, t_:t_ + 1],
                                          residv[:, t_, dcg * 128:(dcg + 1) * 128],
                                          ALU.mult, ALU.add), "ve")
                    stt_ve[dcg] = val

        def emit_attn(upto=99):
            mk.wnow("T", "ve")
            qv = qkv_out.ap().rearrange("(dc p) f -> p dc f", p=128)
            ktov = kt_out.ap()
            wov = wo_out.ap()
            sup_order = list(range(8, 16)) + list(range(0, 8)) + list(range(16, 24))
            sup_te2 = {}
            kt_written = 0
            cc_kt = None
            for si, sp in enumerate(sup_order):
                mk.w("Y", "te", wslot_guard[si % 2])
                mk.w("Y", "ccC", cc_qkv)
                ld = mk.dma("Y", wbslot(si % 2, 256),
                            qv[:, :, sp * 256:(sp + 1) * 256],
                            sem="w%d" % (si % 2))
                mk.w("T", "w%d" % (si % 2), ld)
                base_col = sp * 256
                g_te = 0
                if base_col < 4096:
                    for cc2 in range(2):
                        col = base_col + cc2 * 128
                        ch = col // 128
                        for tb in range(2):
                            gidx = ch * 2 + tb
                            mk.wnow("T", "ve")
                            for dc in range(16):
                                fn = (lambda e, si=si, cc2=cc2, dc=dc, tb=tb, gidx=gidx:
                                      e.matmul(PS[gidx % 3][:],
                                               wbslot(si % 2, 256)[:, dc, cc2 * 128:(cc2 + 1) * 128],
                                               xlntv[:, dc, tb * 512:(tb + 1) * 512],
                                               start=(dc == 0), stop=(dc == 15)))
                                if dc == 15:
                                    g_te = mk.dinc("T", fn, "te")
                                else:
                                    mk.d("T", fn)
                            mk.w("V", "te", g_te)
                            if col < 2048:
                                mk.dinc("V", lambda e, ch=ch, tb=tb, gidx=gidx: e.tensor_copy(
                                    qtv[:, ch, tb * 512:(tb + 1) * 512],
                                    PS[gidx % 3][:]), "ve")
                            else:
                                kc = ch - 16
                                mk.dinc("V", lambda e, kc=kc, tb=tb, gidx=gidx: e.tensor_copy(
                                    ktsv[:, kc % 2, tb * 512:(tb + 1) * 512],
                                    PS[gidx % 3][:]), "ve")
                        if col >= 2048:
                            kc = ch - 16
                            mk.wnow("G", "ve")
                            mk.dma("G", kt_in.ap()[kc], ktsv[:, kc % 2, :])
                            kt_written += 1
                            if kt_written == 16:
                                mk.w("G", "gd", mk.cnt["gd"])
                                cc_kt = mk.dinc("G", lambda e: e.collective_compute(
                                    "AllGather", ALU.bypass, replica_groups=G4,
                                    ins=[kt_in.ap().opt()], outs=[kt_out.ap().opt()]),
                                    "ccE")
                else:
                    vc0 = base_col - 4096
                    for tt in range(NT):
                        mk.wnow("T", "ve")
                        for dc in range(16):
                            fn = (lambda e, si=si, dc=dc, tt=tt:
                                  e.matmul(PS[tt % 3][:, 0:256],
                                           xlntv[:, dc, tt * 128:(tt + 1) * 128],
                                           wbslot(si % 2, 256)[:, dc, :],
                                           start=(dc == 0), stop=(dc == 15)))
                            if dc == 15:
                                g_te = mk.dinc("T", fn, "te")
                            else:
                                mk.d("T", fn)
                        mk.w("V", "te", g_te)
                        mk.dinc("V", lambda e, tt=tt, vc0=vc0: e.tensor_copy(
                            vv[:, tt, vc0:vc0 + 256], PS[tt % 3][:, 0:256]), "ve")
                sup_te2[si] = g_te
                wslot_guard[si % 2] = g_te

            if upto < 1:
                return
            mk.wnow("T", "ve")
            mk.wnow("T", "se")
            se_exp = {}
            te_sdone = {}
            ve_sacc = {}
            ve_r = {}
            head_te = {}
            g = 0

            def emit_smm(j):
                hj, qtj = j // 8, j % 8
                mk.w("T", "ve", ve_r[j])
                if j >= 2:
                    mk.w("T", "ve", ve_sacc[j - 2])
                for kc8 in range(32):
                    fn = (lambda e, j=j, kc8=kc8: e.matmul(
                        PSS[j % 2][:, kc8:kc8 + 1],
                        ebv[:, j % 2, kc8 * 128:(kc8 + 1) * 128],
                        rbf[:, (j % 2) * 8:(j % 2) * 8 + 1],
                        start=True, stop=True))
                    if kc8 == 31:
                        te_sdone[j] = mk.dinc("T", fn, "te")
                    else:
                        mk.d("T", fn)
                mk.w("V", "te", te_sdone[j])
                if qtj == 0:
                    ve_sacc[j] = mk.dinc("V", lambda e, hj=hj, j=j: e.tensor_copy(
                        s_sb[:, hj * 32:(hj + 1) * 32], PSS[j % 2][:, 0:32]), "ve")
                else:
                    ve_sacc[j] = mk.dinc("V", lambda e, hj=hj, j=j:
                                         e.scalar_tensor_tensor(
                                             s_sb[:, hj * 32:(hj + 1) * 32],
                                             PSS[j % 2][:, 0:32], 1.0,
                                             s_sb[:, hj * 32:(hj + 1) * 32],
                                             ALU.mult, ALU.add), "ve")

            for h in range(NH):
                mk.w("Y", "te", wslot_guard[h % 2])
                mk.w("Y", "ccE", cc_kt)
                for r4 in range(4):
                    mk.dma("Y", wbv[:, h % 2, r4 * 1024:(r4 + 1) * 1024], ktov[r4, h],
                           sem="w%d" % (h % 2))
                ktl = mk.cnt["w%d" % (h % 2)]
                mk.w("T", "w%d" % (h % 2), ktl)
                for qt in range(8):
                    it = h * 8 + qt
                    for kc in range(8):
                        if g >= 3:
                            mk.w("T", "se", se_exp[g - 3])
                        fn = (lambda e, h=h, qt=qt, kc=kc, g=g: e.matmul(
                            PS[g % 3][:],
                            qtv[:, h, qt * 128:(qt + 1) * 128],
                            wbv[:, h % 2, kc * 512:(kc + 1) * 512],
                            start=True, stop=True))
                        te_mm = mk.dinc("T", fn, "te")
                        head_te[h] = te_mm
                        mk.w("S", "te", te_mm)
                        if it >= 2 and kc == 0:
                            mk.w("S", "te", te_sdone[it - 2])
                            mk.w("S", "ve", ve_r[it - 2])
                        se_exp[g] = mk.dinc("S", lambda e, it=it, kc=kc, g=g: e.activation(
                            ebv[:, it % 2, kc * 512:(kc + 1) * 512],
                            PS[g % 3][:], ACT.Exp, scale=SCL,
                            accum_out=zpv[:, it % 2, kc:kc + 1]), "se")
                        g += 1
                    mk.w("V", "se", se_exp[g - 1])
                    mk.d("V", lambda e, it=it: e.reduce_sum(
                        zs[:, it % 2:it % 2 + 1], zpv[:, it % 2, :], axis=AX.X))
                    mk.drain("V")
                    mk.d("V", lambda e, it=it: e.reciprocal(
                        zs[:, it % 2:it % 2 + 1], zs[:, it % 2:it % 2 + 1]))
                    mk.drain("V")
                    ve_r[it] = mk.dinc("V", lambda e, it=it: e.tensor_copy(
                        rbf[:, (it % 2) * 8:(it % 2) * 8 + 1], zs[:, it % 2:it % 2 + 1]), "ve")
                    if qt == 7:
                        wslot_guard[h % 2] = head_te[h]
                    if it >= 1:
                        emit_smm(it - 1)
            emit_smm(NH * 8 - 1)

            if upto < 2:
                return
            for b4 in range(4):
                mk.w("T", "ve", ve_sacc[NH * 8 - 1] if b4 == 0 else mk.cnt["ve"])
                td = mk.dinc("T", lambda e, b4=b4: e.transpose(
                    ps_tp32[:, b4 * 128:(b4 + 1) * 128],
                    s_sb[:, b4 * 128:(b4 + 1) * 128], identf[:]), "te")
                mk.w("V", "te", td)
                mk.dinc("V", lambda e, b4=b4: e.tensor_copy(
                    fstage[:, b4 * 128:(b4 + 1) * 128],
                    ps_tp32[:, b4 * 128:(b4 + 1) * 128]), "ve")
            mk.wnow("G", "ve")
            dfull = s_in.ap().rearrange("r (blk hlo) (kc8 f) -> hlo blk r kc8 f",
                                        hlo=4, kc8=8)
            for blk in range(4):
                for hlo in range(4):
                    mk.dma("G", dfull[hlo, blk],
                           fstage[hlo * 32:(hlo + 1) * 32,
                                  blk * 128:(blk + 1) * 128])
            mk.w("G", "gd", mk.cnt["gd"])
            cc_s = mk.dinc("G", lambda e: e.collective_compute(
                "ReduceScatter", ALU.add, replica_groups=G4,
                ins=[s_in.ap().opt()], outs=[s_out.ap().opt()]), "ccF")
            mk.w("G", "ccF", cc_s)
            sld = mk.dma("G", stmp, s_out.ap())
            for t in range(NT):
                mk.w("T", "gd", sld)
                mk.wnow("T", "ve")
                td = mk.dinc("T", lambda e, t=t: e.transpose(
                    ps_tp32[:, 0:16], stmp[:, t * 128:(t + 1) * 128],
                    identf[0:16, 0:16]), "te")
                mk.w("V", "te", td)
                mk.dinc("V", lambda e, t=t: e.tensor_copy(
                    stv[:, t, :], ps_tp32[:, 0:16]), "ve")

            if upto < 3:
                return
            for t in range(NT):
                for h in range(NH):
                    i = t * NH + h
                    mk.wnow("V", "te")
                    vd = mk.dinc("V", lambda e, t=t, h=h, i=i: e.tensor_scalar(
                        vstv[:, i % 2, :], vv[:, t, h * 128:(h + 1) * 128],
                        stv[:, t, h:h + 1], None, ALU.mult), "ve")
                    mk.w("T", "ve", vd)
                    k8 = i % 8
                    fn = lambda e, k8=k8, i=i: e.transpose(
                        ps_tpb[:, k8 * 128:(k8 + 1) * 128], vstv[:, i % 2, :], identb[:])
                    mk.dinc("T", fn, "te")
                    if k8 == 7:
                        mk.wnow("V", "te")
                        t0, h0 = (i - 7) // NH, (i - 7) % NH
                        dst2 = valtv[:, h0:h0 + 8, t0 * 128:(t0 + 1) * 128]
                        mk.dinc("V", lambda e, dst2=dst2: e.tensor_copy(
                            dst2, ps_tpb[:].rearrange("p (c f) -> p c f", f=128)), "ve")

            if upto < 4:
                return
            grp_te = {}
            mk.wnow("T", "ve")
            mk.wnow("T", "se")
            for dcg in range(16):
                mk.w("Y", "te", wslot_guard[dcg % 2])
                mk.w("Y", "ccD", cc_wo)
                for r8 in range(8):
                    mk.dma("Y", wbslot(dcg % 2, 128)[:, r8 * 2:(r8 + 1) * 2, :],
                           wov[r8, dcg].rearrange("(vc p) f -> p vc f", p=128),
                           sem="w%d" % (dcg % 2))
                ld = mk.cnt["w%d" % (dcg % 2)]
                for tb in range(2):
                    mk.w("T", "w%d" % (dcg % 2), ld)
                    for vc in range(16):
                        fn = (lambda e, dcg=dcg, vc=vc, tb=tb: e.matmul(
                            PS[(dcg * 2 + tb) % 3][:],
                            wbslot(dcg % 2, 128)[:, vc, :],
                            valtv[:, vc, tb * 512:(tb + 1) * 512],
                            start=(vc == 0), stop=(vc == 15)))
                        if vc == 15:
                            grp_te[(dcg, tb)] = mk.dinc("T", fn, "te")
                        else:
                            mk.d("T", fn)
                    if tb == 1:
                        wslot_guard[dcg % 2] = grp_te[(dcg, tb)]
                    mk.w("V", "te", grp_te[(dcg, tb)])
                    ev = mk.dinc("V", lambda e, dcg=dcg, tb=tb: e.tensor_copy(
                        fsv[:, tb, :], PS[(dcg * 2 + tb) % 3][:]), "ve")
                    for b4 in range(4):
                        mk.w("T", "ve", ev if b4 == 0 else mk.cnt["ve"])
                        td = mk.dinc("T", lambda e, tb=tb, b4=b4: e.transpose(
                            ps_tp32[:, b4 * 128:(b4 + 1) * 128],
                            fsv[:, tb, b4 * 128:(b4 + 1) * 128], identf[:]), "te")
                        mk.w("V", "te", td)
                        t_ = tb * 4 + b4
                        mk.dinc("V", lambda e, t_=t_, dcg=dcg, b4=b4:
                                e.scalar_tensor_tensor(
                                    residv[:, t_, dcg * 128:(dcg + 1) * 128],
                                    ps_tp32[:, b4 * 128:(b4 + 1) * 128], 1.0,
                                    residv[:, t_, dcg * 128:(dcg + 1) * 128],
                                    ALU.mult, ALU.add), "ve")

        if stage >= 2:
            emit_ffn()
            if debug >= 2 and stage < 9:
                mk.wnow("G", "ve")
                mk.dma("G", dbg["x1"], resid[:])
                mk.dma("G", dbg["ms"], ms_sb[:])
                mk.dma("G", dbg["ones"], ones_bf[:])
                mk.dma("G", dbg["rst"], rs_t[:])
                mk.dma("G", dbg["hsq"], work[:])
        if stage >= 3:
            emit_ln()
            emit_attn(upto=int(stage) - 3 if stage < 9 else 99)
            if debug >= 3 and stage >= 7:
                mk.wnow("G", "ve")
                mk.dma("G", dbg["x2"], resid[:])
                mk.dma("G", dbg["qt"], work[:, 0:16 * 1024])
                mk.dma("G", dbg["vvd"], work[:, 16 * 1024:32 * 1024])
                mk.dma("G", dbg["ktf"], kt_out.ap())
                mk.dma("G", dbg["ssb"], s_sb[:])
                mk.dma("G", dbg["std"], s_t[:])
        if stage >= 9:
            emit_ln()
            emit_ffn()

        mk.wnow("G", "ve")
        mk.wnow("G", "se")
        mk.wnow("G", "te")
        mk.dma("G", y_out.rearrange("(t p) d -> p t d", p=128),
               resid[:].rearrange("p (t d) -> p t d", d=D))
        mk.w("G", "gd", mk.cnt["gd"])

        @block.sync
        def _(e):
            for f in mk.ops["Y"]:
                f(e)

        @block.gpsimd
        def _(e):
            for f in mk.ops["G"]:
                f(e)

        @block.vector
        def _(e):
            for f in mk.ops["V"]:
                f(e)

        @block.scalar
        def _(e):
            for f in mk.ops["S"]:
                f(e)

        @block.tensor
        def _(e):
            for f in mk.ops["T"]:
                f(e)

    return nc


def host_inputs(L0_attn_x, L0_attn_w_qkv, L0_attn_w_o, L0_ffn_W_fc, L0_ffn_W_proj):
    x = np.ascontiguousarray(np.asarray(L0_attn_x, dtype=np.float32))
    wqkv = np.ascontiguousarray(np.asarray(L0_attn_w_qkv, dtype=np.float32))
    wo = np.ascontiguousarray(np.asarray(L0_attn_w_o, dtype=np.float32))
    wfc = np.ascontiguousarray(np.asarray(L0_ffn_W_fc, dtype=np.float32))
    wpr = np.ascontiguousarray(np.asarray(L0_ffn_W_proj, dtype=np.float32))
    identb = np.eye(128, dtype=ml_dtypes.bfloat16)
    identf = np.eye(128, dtype=np.float32)
    in_maps = []
    for c in range(8):
        b, r = c // 4, c % 4
        in_maps.append({
            "x_in": np.ascontiguousarray(x[b, r * TOK:(r + 1) * TOK, :]),
            "wfc_sl": np.ascontiguousarray(wfc[:, c * 256:(c + 1) * 256]),
            "wpr_sl": np.ascontiguousarray(wpr[:, c * 1024:(c + 1) * 1024]),
            "wqkv_sl": np.ascontiguousarray(wqkv[c * 256:(c + 1) * 256, :]),
            "wo_sl": np.ascontiguousarray(wo[c * 256:(c + 1) * 256, :]),
            "identb": identb,
            "identf": identf,
        })
    return in_maps


from concourse.bass_utils import run_bass_kernel_spmd

_NC_CACHE = {}


def kernel(L0_attn_x, L0_attn_w_qkv, L0_attn_w_o, L0_ffn_W_fc, L0_ffn_W_proj):
    if "nc" not in _NC_CACHE:
        _NC_CACHE["nc"] = build(debug=0, stage=9)
    nc = _NC_CACHE["nc"]
    in_maps = host_inputs(L0_attn_x, L0_attn_w_qkv, L0_attn_w_o,
                          L0_ffn_W_fc, L0_ffn_W_proj)
    res = run_bass_kernel_spmd(nc, in_maps, core_ids=list(range(8)))
    y = np.zeros((B, S, D), np.float32)
    for c in range(8):
        b, r = c // 4, c % 4
        y[b, r * TOK:(r + 1) * TOK] = np.asarray(res.results[c]["y"], np.float32)
    return y


# version 03
# speedup vs baseline: 1.1084x; 1.0377x over previous
import math
import numpy as np
SPART = 9
import ml_dtypes
import concourse.bass as bass
import concourse.mybir as mybir
from contextlib import ExitStack

F32 = mybir.dt.float32
BF16 = mybir.dt.bfloat16
AX = mybir.AxisListType
ALU = mybir.AluOpType
ACT = mybir.ActivationFunctionType

B, S, D, HID, NH, DH = 2, 4096, 2048, 8192, 16, 128
TOK = 1024
NT = TOK // 128
G4 = [[0, 1, 2, 3], [4, 5, 6, 7]]
G8 = [list(range(8))]
QC = 0.501953125
EPS_LN = 1e-5
EPS_RMS = float(np.finfo(np.float32).eps)
SCL = float(1.0 / math.sqrt(DH))

ENGS = ("T", "V", "S", "G", "Y")


class MK:
    def __init__(self, nc):
        self.nc = nc
        self.ops = {e: [] for e in ENGS}
        self.sem = {}
        self.cnt = {}

    def mksem(self, ctx, name):
        self.sem[name] = ctx.enter_context(self.nc.semaphore(name))
        self.cnt[name] = 0

    def w(self, eng, sem, val):
        if val <= 0:
            return
        h = self.sem[sem]
        self.ops[eng].append(lambda e, h=h, v=val: e.wait_ge(h, v))

    def wnow(self, eng, sem):
        self.w(eng, sem, self.cnt[sem])

    def d(self, eng, fn):
        self.ops[eng].append(fn)

    def dinc(self, eng, fn, sem, k=1):
        h = self.sem[sem]
        self.cnt[sem] += k
        self.ops[eng].append(lambda e, fn=fn, h=h, k=k: fn(e).then_inc(h, k))
        return self.cnt[sem]

    def dma(self, eng, out, in_, sem=None):
        if sem is None:
            sem = "yd" if eng == "Y" else "gd"
        return self.dinc(eng, lambda e, o=out, i=in_: e.dma_start(out=o, in_=i), sem, 16)

    def drain(self, eng):
        self.ops[eng].append(lambda e: e.drain())


def build(debug=0, stage=9):
    nc = bass.Bass("TRN2", target_bir_lowering=False)

    x_in = nc.dram_tensor("x_in", [TOK, D], F32, kind="ExternalInput").ap()
    wfc_sl = nc.dram_tensor("wfc_sl", [HID, 256], F32, kind="ExternalInput").ap()
    wpr_sl = nc.dram_tensor("wpr_sl", [D, 1024], F32, kind="ExternalInput").ap()
    wqkv_sl = nc.dram_tensor("wqkv_sl", [256, 3 * D], F32, kind="ExternalInput").ap()
    wo_sl = nc.dram_tensor("wo_sl", [256, D], F32, kind="ExternalInput").ap()
    identb_h = nc.dram_tensor("identb", [128, 128], BF16, kind="ExternalInput").ap()
    identf_h = nc.dram_tensor("identf", [128, 128], F32, kind="ExternalInput").ap()
    y_out = nc.dram_tensor("y", [TOK, D], F32, kind="ExternalOutput").ap()

    w1t_in = nc.dram_tensor("w1t_in", [256, HID], BF16)
    w1t_out = nc.dram_tensor("w1t_out", [D, HID], BF16)
    w2t_in = nc.dram_tensor("w2t_in", [16, 1024, 128], BF16)
    w2t_out = nc.dram_tensor("w2t_out", [8, 16, 1024, 128], BF16)
    qkv_in = nc.dram_tensor("qkv_in", [256, 3 * D], BF16)
    qkv_out = nc.dram_tensor("qkv_out", [D, 3 * D], BF16)
    wo_in = nc.dram_tensor("wo_in", [16, 256, 128], BF16)
    wo_out = nc.dram_tensor("wo_out", [8, 16, 256, 128], BF16)
    kt_in = nc.dram_tensor("kt_in", [16, 128, 1024], BF16)
    kt_out = nc.dram_tensor("kt_out", [4, 16, 128, 1024], BF16)
    s_in = nc.dram_tensor("s_in", [4, 16, 1024], F32)
    s_out = nc.dram_tensor("s_out", [16, 1024], F32)

    dbg = {}
    if debug:
        def dbgt(name, shape, dt):
            dbg[name] = nc.dram_tensor("d_" + name, shape, dt, kind="ExternalOutput").ap()
        dbgt("xlnT", [128, 16 * 1024], BF16)
        dbgt("w1t", [D, HID], BF16)
        dbgt("w2t", [8, 16, 1024, 128], BF16)
        dbgt("qkvw", [D, 3 * D], BF16)
        dbgt("wow", [8, 16, 256, 128], BF16)
        if debug >= 2:
            dbgt("x1", [128, NT * D], F32)
            dbgt("ms", [1, 1024], F32)
            dbgt("msraw", [1, 1024], F32)
            dbgt("ones", [128, 1], BF16)
            dbgt("hsq2d", [128, 2 * 512], BF16)
            dbgt("rst", [128, 8], F32)
            dbgt("hsq", [128, 64 * 512], BF16)
        if debug >= 3:
            dbgt("qt", [128, 16 * 1024], BF16)
            dbgt("vvd", [128, 8 * 2048], BF16)
            dbgt("ktf", [4, 16, 128, 1024], BF16)
            dbgt("ssb", [128, 512], F32)
            dbgt("std", [128, 8 * 16], F32)
            dbgt("x2", [128, NT * D], F32)

    mk = MK(nc)
    with ExitStack() as ctx:
        sb = lambda name, shape, dt: ctx.enter_context(nc.sbuf_tensor(name, shape, dt))
        psb = lambda name, shape, dt: ctx.enter_context(nc.psum_tensor(name, shape, dt))

        resid = sb("resid", [128, NT * D], F32)
        xlnT = sb("xlnT", [128, 16 * 1024], BF16)
        work = sb("work", [128, 64 * 512], BF16)
        ebuf = sb("ebuf", [128, 2 * 4096], BF16)
        wbuf = sb("wbuf", [128, 2 * 4096], BF16)
        hsq2 = sb("hsq2", [128, 2 * 512], BF16)
        fstage = sb("fstage", [128, 2 * 512], F32)
        s_sb = sb("s_sb", [128, 512], F32)
        identb = sb("identb_s", [128, 128], BF16)
        identf = sb("identf_s", [128, 128], F32)
        ones_bf = sb("ones_bf", [128, 1], BF16)
        bnst = sb("bnst", [128, 24], F32)
        bnag = sb("bnag", [128, 2], F32)
        m_v = sb("m_v", [128, 1], F32)
        rstd_v = sb("rstd_v", [128, 1], F32)
        bias_v = sb("bias_v", [128, 1], F32)
        zp = sb("zp", [128, 2 * 8], F32)
        zs = sb("zs", [128, 2], F32)
        rbf = sb("rbf", [128, 16], BF16)
        ms_sb = sb("ms_sb", [1, 2 * 512], F32)
        rs_t = sb("rs_t", [128, 8], F32)
        s_t = sb("s_t", [128, 8 * 16], F32)
        qsc_s = sb("qsc_s", [128, 64], F32)
        qsb_s = sb("qsb_s", [128, 64], BF16)
        qth_s = sb("qth_s", [128, 64], F32)
        vstage = sb("vstage", [128, 2 * 128], BF16)

        ps0 = psb("ps0", [128, 512], F32)
        ps1 = psb("ps1", [128, 512], F32)
        ps2 = psb("ps2", [128, 512], F32)
        ps_tpb = psb("ps_tpb", [128, 1024], BF16)
        ps_tp32 = psb("ps_tp32", [128, 512], F32)
        ps_ms = psb("ps_ms", [1, 512], F32)
        ps_s0 = psb("ps_s0", [128, 512], F32)
        ps_s1 = psb("ps_s1", [128, 512], F32)
        PS = [ps0, ps1, ps2]
        PSS = [ps_s0, ps_s1]

        for s in ["yd", "gd", "te", "ve", "se", "w0", "w1",
                  "ccA", "ccB", "ccC", "ccD", "ccE", "ccF"]:
            mk.mksem(ctx, s)
        block = ctx.enter_context(nc.Block())

        residv = resid[:].rearrange("p (t d) -> p t d", d=D)
        xlntv = xlnT[:].rearrange("p (c f) -> p c f", f=1024)
        hsqv = work[:].rearrange("p (c f) -> p c f", f=512)
        qtv = work[:, 0:16 * 1024].rearrange("p (c f) -> p c f", f=1024)
        vv = work[:, 16 * 1024:32 * 1024].rearrange("p (t f) -> p t f", f=D)
        valtv = xlnT[:].rearrange("p (c f) -> p c f", f=1024)
        ebv = ebuf[:].rearrange("p (i f) -> p i f", f=4096)
        wbv = wbuf[:].rearrange("p (i f) -> p i f", f=4096)
        ktsv = fstage[:].bitcast(BF16).rearrange("p (i f) -> p i f", f=1024)
        hsq2v = hsq2[:].rearrange("p (i f) -> p i f", f=512)
        fsv = fstage[:].rearrange("p (i f) -> p i f", f=512)
        zpv = zp[:].rearrange("p (i f) -> p i f", f=8)
        msv = ms_sb[:].rearrange("p (i f) -> p i f", f=512)
        stv = s_t[:].rearrange("p (t h) -> p t h", h=16)
        vstv = vstage[:].rearrange("p (i f) -> p i f", f=128)
        xlnb = ebuf[:, 4096:4096 + 2048]
        stmp = ebuf[:, 0:4096].bitcast(F32)[0:16, 0:1024]
        qstage32 = work[:, 0:32 * 1024].bitcast(F32)

        def wbslot(i, f):
            return wbv[:, i, :].rearrange("p (c f) -> p c f", f=f)

        mk.dma("G", resid[:].rearrange("p (t d) -> p t d", d=D),
               x_in.rearrange("(t p) d -> p t d", p=128))
        x_done = mk.cnt["gd"]
        mk.dma("Y", identb[:], identb_h)
        mk.dma("Y", identf[:], identf_h)
        id_done = mk.cnt["yd"]
        mk.d("V", lambda e: e.memset(ones_bf[:], 1.0))

        def quant_stage(src_ap, fd, inner):
            mk.wnow("Y", "ve")
            ld = mk.dma("Y", qstage32[:, 0:fd].rearrange(
                "p (a b) -> p a b", b=inner), src_ap)
            mk.w("V", "yd", ld)
            mk.wnow("V", "gd")
            mk.wnow("V", "te")
            wb = ebv[:, 0, 0:fd]
            wb3 = wb.rearrange("p (g k) -> p g k", k=64)
            g = fd // 64
            mk.d("V", lambda e, wb=wb, fd=fd: e.tensor_copy(wb, qstage32[:, 0:fd]))
            mk.drain("V")
            mk.d("V", lambda e, wb3=wb3, g=g: e.tensor_reduce(
                qsc_s[:, 0:g], wb3, axis=AX.X, op=ALU.add, apply_absolute_value=True))
            mk.drain("V")
            mk.d("V", lambda e, g=g: e.tensor_scalar(
                qsb_s[:, 0:g], qsc_s[:, 0:g], 1.0 / 64.0, None, ALU.mult))
            mk.drain("V")
            mk.d("V", lambda e, g=g: e.tensor_scalar(
                qsc_s[:, 0:g], qsb_s[:, 0:g], 1e-8, None, ALU.max))
            mk.drain("V")
            mk.d("V", lambda e, g=g: e.tensor_scalar(
                qth_s[:, 0:g], qsc_s[:, 0:g], QC, None, ALU.mult))
            mk.drain("V")
            thb = qth_s[:, 0:g].unsqueeze(2).broadcast_to([128, g, 64])
            scb = qsc_s[:, 0:g].unsqueeze(2).broadcast_to([128, g, 64])
            mp = ebv[:, 1, 0:fd].rearrange("p (g k) -> p g k", k=64)
            mn = work[:, 8192:8192 + fd].rearrange("p (g k) -> p g k", k=64)
            mk.d("V", lambda e, mp=mp, wb3=wb3, thb=thb: e.scalar_tensor_tensor(
                mp, wb3, 1.0, thb, ALU.mult, ALU.is_gt))
            mk.d("V", lambda e, mn=mn, wb3=wb3, thb=thb: e.scalar_tensor_tensor(
                mn, wb3, -1.0, thb, ALU.mult, ALU.is_gt))
            mk.drain("V")
            mk.d("V", lambda e, mp=mp, mn=mn: e.tensor_tensor(mp, mp, mn, ALU.subtract))
            mk.drain("V")
            return mk.dinc("V", lambda e, wb3=wb3, mp=mp, scb=scb: e.tensor_tensor(
                wb3, mp, scb, ALU.mult), "ve")

        first_tb = [True]

        def transpose_blocks(nblk, src_fn, pairs_fn, vready):
            if first_tb[0]:
                mk.w("T", "yd", id_done)
                first_tb[0] = False
            nbat = (nblk + 7) // 8
            for b in range(nbat):
                k0, k1 = b * 8, min(b * 8 + 8, nblk)
                mk.w("T", "ve", vready if b == 0 else mk.cnt["ve"])
                mk.wnow("T", "gd")
                for k in range(k0, k1):
                    fn = lambda e, k=k, j=k - k0: e.transpose(
                        ps_tpb[:, j * 128:(j + 1) * 128], src_fn(k), identb[:])
                    if k == k1 - 1:
                        mk.dinc("T", fn, "te")
                    else:
                        mk.d("T", fn)
                mk.wnow("V", "te")
                stage = ktsv[:, b % 2, :]
                mk.dinc("V", lambda e, st=stage: e.tensor_copy(st, ps_tpb[:]), "ve")
                mk.wnow("G", "ve")
                for dst_ap, src_ap in pairs_fn(b, stage):
                    mk.dma("G", dst_ap, src_ap)

        wfc4 = wfc_sl.rearrange("(st tq p) d -> st p tq d", p=128, tq=16)
        w1ap = w1t_in.ap().rearrange("(dc p) (st tq f) -> p st tq dc f",
                                     p=128, st=4, f=128)
        for st4 in range(4):
            vd = quant_stage(wfc4[st4], 4096, 256)
            srcv = ebv[:, 0, 0:4096].rearrange("p (tq dc f) -> p tq dc f", dc=2, f=128)
            def w1pairs(b, st, st4=st4):
                stv4 = st.rearrange("p (tq dc f) -> p tq dc f", dc=2, f=128)
                return [(w1ap[:, st4, b * 4:b * 4 + 4, dc, :], stv4[:, :, dc, :])
                        for dc in range(2)]
            transpose_blocks(
                32,
                lambda k, srcv=srcv: srcv[:, k // 2, k % 2, :],
                w1pairs,
                vd)
        w1t_ready = mk.cnt["gd"]
        mk.w("G", "gd", w1t_ready)
        cc_w1t = mk.dinc("G", lambda e: e.collective_compute(
            "AllGather", ALU.bypass, replica_groups=G8,
            ins=[w1t_in.ap().opt()], outs=[w1t_out.ap().opt()]), "ccA")

        wpr4 = wpr_sl.rearrange("(st tq p) h -> st p tq h", p=128, tq=4)
        w2ap = w2t_in.ap().rearrange("dc (hc p) f -> p dc hc f", p=128)
        for st4 in range(4):
            vd = quant_stage(wpr4[st4], 4096, 1024)
            srcv = ebv[:, 0, 0:4096].rearrange("p (tq hc f) -> p tq hc f", hc=8, f=128)
            transpose_blocks(
                32,
                lambda k, srcv=srcv: srcv[:, k // 8, k % 8, :],
                lambda b, st, st4=st4: [(w2ap[:, st4 * 4 + b, :, :],
                                         st.rearrange("p (hc f) -> p hc f", f=128))],
                vd)
        w2t_ready = mk.cnt["gd"]
        mk.w("G", "gd", w2t_ready)
        cc_w2t = mk.dinc("G", lambda e: e.collective_compute(
            "AllGather", ALU.bypass, replica_groups=G8,
            ins=[w2t_in.ap().opt()], outs=[w2t_out.ap().opt()]), "ccB")

        qkv2 = wqkv_sl.rearrange("(r p) f -> r p f", p=128)
        qin2 = qkv_in.ap().rearrange("(r p) f -> r p f", p=128)
        for r in range(2):
            for half in range(2):
                f0, f1 = half * 3072, (half + 1) * 3072
                mk.wnow("Y", "ve")
                ld = mk.dma("Y", qstage32[:, 0:3072], qkv2[r, :, f0:f1])
                mk.w("V", "yd", ld)
                mk.wnow("V", "gd")
                mk.wnow("V", "te")
                vd = mk.dinc("V", lambda e: e.tensor_copy(
                    ebv[:, 0, 0:3072], qstage32[:, 0:3072]), "ve")
                mk.w("G", "ve", vd)
                mk.dma("G", qin2[r, :, f0:f1], ebv[:, 0, 0:3072])
        qkv_ready = mk.cnt["gd"]
        mk.w("G", "gd", qkv_ready)
        cc_qkv = mk.dinc("G", lambda e: e.collective_compute(
            "AllGather", ALU.bypass, replica_groups=G8,
            ins=[qkv_in.ap().opt()], outs=[qkv_out.ap().opt()]), "ccC")

        wo2 = wo_sl.rearrange("(r p) f -> r p f", p=128)
        wo_blk = wo_in.ap().rearrange("dc (r p) f -> r p dc f", p=128)
        for r in range(2):
            mk.wnow("Y", "ve")
            ld = mk.dma("Y", qstage32[:, 0:2048], wo2[r])
            mk.w("V", "yd", ld)
            mk.wnow("V", "gd")
            mk.wnow("V", "te")
            vd = mk.dinc("V", lambda e: e.tensor_copy(
                ebv[:, 0, 0:2048], qstage32[:, 0:2048]), "ve")
            mk.w("G", "ve", vd)
            mk.dma("G", wo_blk[r], ebv[:, 0, 0:2048].rearrange("p (dc f) -> p dc f", f=128))
        wo_ready = mk.cnt["gd"]
        mk.w("G", "gd", wo_ready)
        cc_wo = mk.dinc("G", lambda e: e.collective_compute(
            "AllGather", ALU.bypass, replica_groups=G8,
            ins=[wo_in.ap().opt()], outs=[wo_out.ap().opt()]), "ccD")


        def emit_ln(gd_wait=0):
            if gd_wait:
                mk.w("V", "gd", gd_wait)
                mk.w("S", "gd", gd_wait)
            mk.drain("V")
            prev_xln_se = 0
            prev_tr_te = 0
            for t in range(NT):
                xt = residv[:, t, :]
                if prev_xln_se:
                    mk.w("V", "se", prev_xln_se)
                for q4 in range(4):
                    mk.d("V", lambda e, xt=xt, q4=q4: e.bn_stats(
                        bnst[:, q4 * 6:(q4 + 1) * 6], xt[:, q4 * 512:(q4 + 1) * 512]))
                mk.drain("V")
                mk.d("V", lambda e: e.bn_aggr(bnag[:], bnst[:]))
                mk.drain("V")
                mk.d("V", lambda e: e.tensor_copy(m_v[:], bnag[:, 0:1]))
                mk.d("V", lambda e: e.tensor_scalar(
                    rstd_v[:], bnag[:, 1:2], 1.0, EPS_LN, ALU.mult, ALU.add))
                mk.drain("V")
                vd = mk.dinc("V", lambda e: e.reciprocal(rstd_v[:], rstd_v[:]), "ve")
                mk.w("S", "ve", vd)
                sd = mk.dinc("S", lambda e: e.activation(rstd_v[:], rstd_v[:], ACT.Sqrt), "se")
                mk.w("V", "se", sd)
                vd = mk.dinc("V", lambda e: e.scalar_tensor_tensor(
                    bias_v[:], m_v[:], -1.0, rstd_v[:], ALU.mult, ALU.mult), "ve")
                mk.w("S", "ve", vd)
                if prev_tr_te:
                    mk.w("S", "te", prev_tr_te)
                sd = mk.dinc("S", lambda e, xt=xt: e.activation(
                    xlnb[:], xt, ACT.Identity, bias=bias_v[:], scale=rstd_v[:]), "se")
                prev_xln_se = sd
                for bb in range(2):
                    mk.w("T", "se", sd)
                    mk.wnow("T", "ve")
                    for k in range(8):
                        dc = bb * 8 + k
                        fn = lambda e, dc=dc, k=k: e.transpose(
                            ps_tpb[:, k * 128:(k + 1) * 128],
                            xlnb[:, dc * 128:(dc + 1) * 128], identb[:])
                        if k == 7:
                            prev_tr_te = mk.dinc("T", fn, "te")
                        else:
                            mk.d("T", fn)
                    mk.wnow("V", "te")
                    dst = xlntv[:, bb * 8:(bb + 1) * 8, t * 128:(t + 1) * 128]
                    mk.dinc("V", lambda e, dst=dst: e.tensor_copy(
                        dst, ps_tpb[:].rearrange("p (c f) -> p c f", f=128)), "ve")

        emit_ln(gd_wait=wo_ready)

        if debug:
            mk.wnow("G", "ve")
            mk.dma("G", dbg["xlnT"], xlnT[:])
            mk.w("G", "ccD", mk.cnt["ccD"])
            mk.dma("G", dbg["w1t"], w1t_out.ap())
            mk.dma("G", dbg["w2t"], w2t_out.ap())
            mk.dma("G", dbg["qkvw"], qkv_out.ap())
            mk.dma("G", dbg["wow"], wo_out.ap())

        wslot_guard = {0: 0, 1: 0}

        def emit_ffn():
            mk.wnow("T", "ve")
            mk.wnow("T", "se")
            w1v = w1t_out.ap().rearrange("(dc p) h -> p dc h", p=128)
            w2v = w2t_out.ap()
            for hf in range(2):
                psfree = {}
                hsqready = {}
                hsq2ready = {}
                msdone_te = {}
                msevac_ve = {}
                sup_te = {}
                mk.wnow("S", "te")
                for sp in range(32):
                    mk.w("Y", "te", wslot_guard[sp % 2])
                    mk.w("Y", "ccA", cc_w1t)
                    ld = mk.dma("Y", wbslot(sp % 2, 256),
                                w1v[:, :, sp * 256:(sp + 1) * 256],
                                sem="w%d" % (sp % 2))
                    mk.w("T", "w%d" % (sp % 2), ld)
                    for cc2 in range(2):
                        k = sp * 2 + cc2
                        if k >= 3:
                            mk.w("T", "se", psfree[k - 3])
                        te_g = 0
                        for dc in range(16):
                            fn = (lambda e, k=k, cc2=cc2, dc=dc, sp=sp, hf=hf:
                                  e.matmul(PS[k % 3][:],
                                           wbslot(sp % 2, 256)[:, dc, cc2 * 128:(cc2 + 1) * 128],
                                           xlntv[:, dc, hf * 512:(hf + 1) * 512],
                                           start=(dc == 0), stop=(dc == 15)))
                            if dc == 15:
                                te_g = mk.dinc("T", fn, "te")
                            else:
                                mk.d("T", fn)
                        mk.w("S", "te", te_g)
                        if k < 2:
                            mk.w("S", "gd", wo_ready)
                        if k >= 2:
                            mk.w("S", "ve", hsqready.get(k - 2, 0))
                        sd = mk.dinc("S", lambda e, k=k: e.activation(
                            fsv[:, k % 2, :], PS[k % 3][:], ACT.Relu), "se")
                        psfree[k] = sd
                        mk.w("V", "se", sd)
                        if k >= 2:
                            mk.w("V", "te", msdone_te[(k - 2) // 2])
                        hsqready[k] = mk.dinc("V", lambda e, k=k: e.tensor_tensor(
                            hsqv[:, k, :], fsv[:, k % 2, :], fsv[:, k % 2, :], ALU.mult), "ve")
                        mk.drain("V")
                        hsq2ready[k] = mk.dinc("V", lambda e, k=k: e.tensor_tensor(
                            hsq2v[:, k % 2, :], hsqv[:, k, :], hsqv[:, k, :], ALU.mult), "ve")
                    mk.w("T", "ve", hsq2ready[sp * 2 + 1])
                    if sp >= 1:
                        mk.w("T", "ve", msevac_ve[sp - 1])
                    for cc2 in range(2):
                        k = sp * 2 + cc2
                        fn = lambda e, k=k, cc2=cc2: e.matmul(
                            ps_ms[:], ones_bf[:], hsq2v[:, k % 2, :],
                            start=(cc2 == 0), stop=(cc2 == 1))
                        if cc2 == 1:
                            msdone_te[sp] = mk.dinc("T", fn, "te")
                        else:
                            mk.d("T", fn)
                    sup_te[sp] = msdone_te[sp]
                    wslot_guard[sp % 2] = te_g
                    mk.w("V", "te", msdone_te[sp])
                    if sp == 0:
                        msevac_ve[sp] = mk.dinc("V", lambda e: e.tensor_copy(
                            msv[:, 0, :], ps_ms[:]), "ve")
                    else:
                        mk.d("V", lambda e: e.tensor_copy(s_sb[0:1, :], ps_ms[:]))
                        mk.drain("V")
                        msevac_ve[sp] = mk.dinc("V", lambda e, sp=sp: e.tensor_tensor(
                            msv[:, sp % 2, :], s_sb[0:1, :], msv[:, 1 - sp % 2, :],
                            ALU.add), "ve")
                if debug >= 2:
                    mk.wnow("G", "ve")
                    mk.dma("G", dbg["msraw"].rearrange("o (i f) -> o i f", f=512)[:, hf, :],
                           msv[:, 1, :])
                    if hf == 0:
                        mk.dma("G", dbg["hsq2d"], hsq2[:])
                mrow = msv[:, 1, :]
                mk.drain("V")
                mk.d("V", lambda e, mrow=mrow: e.tensor_scalar(
                    mrow, mrow, 1.0 / HID, EPS_RMS, ALU.mult, ALU.add))
                mk.drain("V")
                vd = mk.dinc("V", lambda e, mrow=mrow: e.reciprocal(mrow, mrow), "ve")
                mk.w("S", "ve", vd)
                sd = mk.dinc("S", lambda e, mrow=mrow: e.activation(mrow, mrow, ACT.Sqrt), "se")
                mk.w("V", "se", sd)
                vd = mk.dinc("V", lambda e, mrow=mrow: e.tensor_scalar(
                    mrow, mrow, 0.5, None, ALU.mult), "ve")
                mk.w("T", "ve", vd)
                for b4 in range(4):
                    fn = lambda e, b4=b4, mrow=mrow: e.transpose(
                        ps_tp32[:, b4 * 128:b4 * 128 + 1],
                        mrow[0:1, b4 * 128:(b4 + 1) * 128], identf[0:1, 0:1])
                    if b4 == 3:
                        td = mk.dinc("T", fn, "te")
                    else:
                        mk.d("T", fn)
                mk.w("V", "te", td)
                mk.dinc("V", lambda e, hf=hf: e.tensor_copy(
                    rs_t[:, hf * 4:(hf + 1) * 4],
                    ps_tp32[:].rearrange("p (b f) -> p b f", f=128)[:, :, 0]), "ve")

                grp_half_te = {}
                stt_ve = {}
                for dcg in range(16):
                    ld_half = {}
                    for half in range(2):
                        hb = dcg * 2 + half
                        mk.w("Y", "te", wslot_guard[hb % 2])
                        mk.w("Y", "ccB", cc_w2t)
                        for r8 in range(4):
                            r = half * 4 + r8
                            mk.dma("Y", wbslot(hb % 2, 128)[:, r8 * 8:(r8 + 1) * 8, :],
                                   w2v[r, dcg].rearrange("(hc p) f -> p hc f", p=128),
                                   sem="w%d" % (hb % 2))
                        ld_half[half] = (hb % 2, mk.cnt["w%d" % (hb % 2)])
                    if dcg >= 3:
                        mk.w("T", "ve", stt_ve[dcg - 3])
                    for c in range(64):
                        hb = dcg * 2 + (c // 32)
                        if c in (0, 32):
                            sl, lv = ld_half[c // 32]
                            mk.w("T", "w%d" % sl, lv)
                        fn = (lambda e, dcg=dcg, c=c, hb=hb:
                              e.matmul(PS[dcg % 3][:],
                                       wbslot(hb % 2, 128)[:, c % 32, :],
                                       hsqv[:, c, :],
                                       start=(c == 0), stop=(c == 63)))
                        if c == 31:
                            grp_half_te[dcg * 2] = mk.dinc("T", fn, "te")
                        elif c == 63:
                            grp_half_te[dcg * 2 + 1] = mk.dinc("T", fn, "te")
                        else:
                            mk.d("T", fn)
                    grp_te = grp_half_te[dcg * 2 + 1]
                    wslot_guard[0] = grp_half_te[dcg * 2]
                    wslot_guard[1] = grp_te
                    mk.w("V", "te", grp_te)
                    ev = mk.dinc("V", lambda e, dcg=dcg: e.tensor_copy(
                        fsv[:, dcg % 2, :], PS[dcg % 3][:]), "ve")
                    val = 0
                    for b4 in range(4):
                        mk.w("T", "ve", ev if b4 == 0 else mk.cnt["ve"])
                        td = mk.dinc("T", lambda e, dcg=dcg, b4=b4: e.transpose(
                            ps_tp32[:, b4 * 128:(b4 + 1) * 128],
                            fsv[:, dcg % 2, b4 * 128:(b4 + 1) * 128], identf[:]), "te")
                        mk.w("V", "te", td)
                        t_ = hf * 4 + b4
                        val = mk.dinc("V", lambda e, t_=t_, dcg=dcg, b4=b4:
                                      e.scalar_tensor_tensor(
                                          residv[:, t_, dcg * 128:(dcg + 1) * 128],
                                          ps_tp32[:, b4 * 128:(b4 + 1) * 128],
                                          rs_t[:, t_:t_ + 1],
                                          residv[:, t_, dcg * 128:(dcg + 1) * 128],
                                          ALU.mult, ALU.add), "ve")
                    stt_ve[dcg] = val

        def emit_attn(upto=99):
            mk.wnow("T", "ve")
            qv = qkv_out.ap().rearrange("(dc p) f -> p dc f", p=128)
            ktov = kt_out.ap()
            wov = wo_out.ap()
            sup_order = list(range(8, 16)) + list(range(0, 8)) + list(range(16, 24))
            sup_te2 = {}
            kt_written = 0
            cc_kt = None
            for si, sp in enumerate(sup_order):
                mk.w("Y", "te", wslot_guard[si % 2])
                mk.w("Y", "ccC", cc_qkv)
                ld = mk.dma("Y", wbslot(si % 2, 256),
                            qv[:, :, sp * 256:(sp + 1) * 256],
                            sem="w%d" % (si % 2))
                mk.w("T", "w%d" % (si % 2), ld)
                base_col = sp * 256
                g_te = 0
                if base_col < 4096:
                    for cc2 in range(2):
                        col = base_col + cc2 * 128
                        ch = col // 128
                        for tb in range(2):
                            gidx = ch * 2 + tb
                            mk.wnow("T", "ve")
                            for dc in range(16):
                                fn = (lambda e, si=si, cc2=cc2, dc=dc, tb=tb, gidx=gidx:
                                      e.matmul(PS[gidx % 3][:],
                                               wbslot(si % 2, 256)[:, dc, cc2 * 128:(cc2 + 1) * 128],
                                               xlntv[:, dc, tb * 512:(tb + 1) * 512],
                                               start=(dc == 0), stop=(dc == 15)))
                                if dc == 15:
                                    g_te = mk.dinc("T", fn, "te")
                                else:
                                    mk.d("T", fn)
                            mk.w("V", "te", g_te)
                            if col < 2048:
                                mk.dinc("V", lambda e, ch=ch, tb=tb, gidx=gidx: e.tensor_copy(
                                    qtv[:, ch, tb * 512:(tb + 1) * 512],
                                    PS[gidx % 3][:]), "ve")
                            else:
                                kc = ch - 16
                                mk.dinc("V", lambda e, kc=kc, tb=tb, gidx=gidx: e.tensor_copy(
                                    ktsv[:, kc % 2, tb * 512:(tb + 1) * 512],
                                    PS[gidx % 3][:]), "ve")
                        if col >= 2048:
                            kc = ch - 16
                            mk.wnow("G", "ve")
                            mk.dma("G", kt_in.ap()[kc], ktsv[:, kc % 2, :])
                            kt_written += 1
                            if kt_written == 16:
                                mk.w("G", "gd", mk.cnt["gd"])
                                cc_kt = mk.dinc("G", lambda e: e.collective_compute(
                                    "AllGather", ALU.bypass, replica_groups=G4,
                                    ins=[kt_in.ap().opt()], outs=[kt_out.ap().opt()]),
                                    "ccE")
                else:
                    vc0 = base_col - 4096
                    for tt in range(NT):
                        mk.wnow("T", "ve")
                        for dc in range(16):
                            fn = (lambda e, si=si, dc=dc, tt=tt:
                                  e.matmul(PS[tt % 3][:, 0:256],
                                           xlntv[:, dc, tt * 128:(tt + 1) * 128],
                                           wbslot(si % 2, 256)[:, dc, :],
                                           start=(dc == 0), stop=(dc == 15)))
                            if dc == 15:
                                g_te = mk.dinc("T", fn, "te")
                            else:
                                mk.d("T", fn)
                        mk.w("V", "te", g_te)
                        mk.dinc("V", lambda e, tt=tt, vc0=vc0: e.tensor_copy(
                            vv[:, tt, vc0:vc0 + 256], PS[tt % 3][:, 0:256]), "ve")
                sup_te2[si] = g_te
                wslot_guard[si % 2] = g_te

            if upto < 1:
                return
            mk.wnow("T", "ve")
            mk.wnow("T", "se")
            se_exp = {}
            te_sdone = {}
            ve_sacc = {}
            ve_r = {}
            head_te = {}
            g = 0

            def emit_smm(j):
                hj, qtj = j // 8, j % 8
                mk.w("T", "ve", ve_r[j])
                if j >= 2:
                    mk.w("T", "ve", ve_sacc[j - 2])
                for kc8 in range(32):
                    fn = (lambda e, j=j, kc8=kc8: e.matmul(
                        PSS[j % 2][:, kc8:kc8 + 1],
                        ebv[:, j % 2, kc8 * 128:(kc8 + 1) * 128],
                        rbf[:, (j % 2) * 8:(j % 2) * 8 + 1],
                        start=True, stop=True))
                    if kc8 == 31:
                        te_sdone[j] = mk.dinc("T", fn, "te")
                    else:
                        mk.d("T", fn)
                mk.w("V", "te", te_sdone[j])
                if qtj == 0:
                    ve_sacc[j] = mk.dinc("V", lambda e, hj=hj, j=j: e.tensor_copy(
                        s_sb[:, hj * 32:(hj + 1) * 32], PSS[j % 2][:, 0:32]), "ve")
                else:
                    ve_sacc[j] = mk.dinc("V", lambda e, hj=hj, j=j:
                                         e.scalar_tensor_tensor(
                                             s_sb[:, hj * 32:(hj + 1) * 32],
                                             PSS[j % 2][:, 0:32], 1.0,
                                             s_sb[:, hj * 32:(hj + 1) * 32],
                                             ALU.mult, ALU.add), "ve")

            for h in range(NH):
                mk.w("Y", "te", wslot_guard[h % 2])
                mk.w("Y", "ccE", cc_kt)
                for r4 in range(4):
                    mk.dma("Y", wbv[:, h % 2, r4 * 1024:(r4 + 1) * 1024], ktov[r4, h],
                           sem="w%d" % (h % 2))
                ktl = mk.cnt["w%d" % (h % 2)]
                mk.w("T", "w%d" % (h % 2), ktl)
                for qt in range(8):
                    it = h * 8 + qt
                    for kc in range(8):
                        if g >= 3:
                            mk.w("T", "se", se_exp[g - 3])
                        fn = (lambda e, h=h, qt=qt, kc=kc, g=g: e.matmul(
                            PS[g % 3][:],
                            qtv[:, h, qt * 128:(qt + 1) * 128],
                            wbv[:, h % 2, kc * 512:(kc + 1) * 512],
                            start=True, stop=True))
                        te_mm = mk.dinc("T", fn, "te")
                        head_te[h] = te_mm
                        mk.w("S", "te", te_mm)
                        if it >= 2 and kc == 0:
                            mk.w("S", "te", te_sdone[it - 2])
                            mk.w("S", "ve", ve_r[it - 2])
                        se_exp[g] = mk.dinc("S", lambda e, it=it, kc=kc, g=g: e.activation(
                            ebv[:, it % 2, kc * 512:(kc + 1) * 512],
                            PS[g % 3][:], ACT.Exp, scale=SCL,
                            accum_out=zpv[:, it % 2, kc:kc + 1]), "se")
                        g += 1
                    mk.w("V", "se", se_exp[g - 1])
                    mk.d("V", lambda e, it=it: e.reduce_sum(
                        zs[:, it % 2:it % 2 + 1], zpv[:, it % 2, :], axis=AX.X))
                    mk.drain("V")
                    mk.d("V", lambda e, it=it: e.reciprocal(
                        zs[:, it % 2:it % 2 + 1], zs[:, it % 2:it % 2 + 1]))
                    mk.drain("V")
                    ve_r[it] = mk.dinc("V", lambda e, it=it: e.tensor_copy(
                        rbf[:, (it % 2) * 8:(it % 2) * 8 + 1], zs[:, it % 2:it % 2 + 1]), "ve")
                    if qt == 7:
                        wslot_guard[h % 2] = head_te[h]
                    if it >= 1:
                        emit_smm(it - 1)
            emit_smm(NH * 8 - 1)

            if upto < 2:
                return
            for b4 in range(4):
                mk.w("T", "ve", ve_sacc[NH * 8 - 1] if b4 == 0 else mk.cnt["ve"])
                td = mk.dinc("T", lambda e, b4=b4: e.transpose(
                    ps_tp32[:, b4 * 128:(b4 + 1) * 128],
                    s_sb[:, b4 * 128:(b4 + 1) * 128], identf[:]), "te")
                mk.w("V", "te", td)
                mk.dinc("V", lambda e, b4=b4: e.tensor_copy(
                    fstage[:, b4 * 128:(b4 + 1) * 128],
                    ps_tp32[:, b4 * 128:(b4 + 1) * 128]), "ve")
            mk.wnow("G", "ve")
            dfull = s_in.ap().rearrange("r (blk hlo) (kc8 f) -> hlo blk r kc8 f",
                                        hlo=4, kc8=8)
            for blk in range(4):
                for hlo in range(4):
                    mk.dma("G", dfull[hlo, blk],
                           fstage[hlo * 32:(hlo + 1) * 32,
                                  blk * 128:(blk + 1) * 128])
            mk.w("G", "gd", mk.cnt["gd"])
            cc_s = mk.dinc("G", lambda e: e.collective_compute(
                "ReduceScatter", ALU.add, replica_groups=G4,
                ins=[s_in.ap().opt()], outs=[s_out.ap().opt()]), "ccF")
            mk.w("G", "ccF", cc_s)
            sld = mk.dma("G", stmp, s_out.ap())
            for t in range(NT):
                mk.w("T", "gd", sld)
                mk.wnow("T", "ve")
                td = mk.dinc("T", lambda e, t=t: e.transpose(
                    ps_tp32[:, 0:16], stmp[:, t * 128:(t + 1) * 128],
                    identf[0:16, 0:16]), "te")
                mk.w("V", "te", td)
                mk.dinc("V", lambda e, t=t: e.tensor_copy(
                    stv[:, t, :], ps_tp32[:, 0:16]), "ve")

            if upto < 3:
                return
            for t in range(NT):
                for h in range(NH):
                    i = t * NH + h
                    mk.wnow("V", "te")
                    vd = mk.dinc("V", lambda e, t=t, h=h, i=i: e.tensor_scalar(
                        vstv[:, i % 2, :], vv[:, t, h * 128:(h + 1) * 128],
                        stv[:, t, h:h + 1], None, ALU.mult), "ve")
                    mk.w("T", "ve", vd)
                    k8 = i % 8
                    fn = lambda e, k8=k8, i=i: e.transpose(
                        ps_tpb[:, k8 * 128:(k8 + 1) * 128], vstv[:, i % 2, :], identb[:])
                    mk.dinc("T", fn, "te")
                    if k8 == 7:
                        mk.wnow("V", "te")
                        t0, h0 = (i - 7) // NH, (i - 7) % NH
                        dst2 = valtv[:, h0:h0 + 8, t0 * 128:(t0 + 1) * 128]
                        mk.dinc("V", lambda e, dst2=dst2: e.tensor_copy(
                            dst2, ps_tpb[:].rearrange("p (c f) -> p c f", f=128)), "ve")

            if upto < 4:
                return
            grp_te = {}
            mk.wnow("T", "ve")
            mk.wnow("T", "se")
            for dcg in range(16):
                mk.w("Y", "te", wslot_guard[dcg % 2])
                mk.w("Y", "ccD", cc_wo)
                for r8 in range(8):
                    mk.dma("Y", wbslot(dcg % 2, 128)[:, r8 * 2:(r8 + 1) * 2, :],
                           wov[r8, dcg].rearrange("(vc p) f -> p vc f", p=128),
                           sem="w%d" % (dcg % 2))
                ld = mk.cnt["w%d" % (dcg % 2)]
                for tb in range(2):
                    mk.w("T", "w%d" % (dcg % 2), ld)
                    for vc in range(16):
                        fn = (lambda e, dcg=dcg, vc=vc, tb=tb: e.matmul(
                            PS[(dcg * 2 + tb) % 3][:],
                            wbslot(dcg % 2, 128)[:, vc, :],
                            valtv[:, vc, tb * 512:(tb + 1) * 512],
                            start=(vc == 0), stop=(vc == 15)))
                        if vc == 15:
                            grp_te[(dcg, tb)] = mk.dinc("T", fn, "te")
                        else:
                            mk.d("T", fn)
                    if tb == 1:
                        wslot_guard[dcg % 2] = grp_te[(dcg, tb)]
                    mk.w("V", "te", grp_te[(dcg, tb)])
                    ev = mk.dinc("V", lambda e, dcg=dcg, tb=tb: e.tensor_copy(
                        fsv[:, tb, :], PS[(dcg * 2 + tb) % 3][:]), "ve")
                    for b4 in range(4):
                        mk.w("T", "ve", ev if b4 == 0 else mk.cnt["ve"])
                        td = mk.dinc("T", lambda e, tb=tb, b4=b4: e.transpose(
                            ps_tp32[:, b4 * 128:(b4 + 1) * 128],
                            fsv[:, tb, b4 * 128:(b4 + 1) * 128], identf[:]), "te")
                        mk.w("V", "te", td)
                        t_ = tb * 4 + b4
                        mk.dinc("V", lambda e, t_=t_, dcg=dcg, b4=b4:
                                e.scalar_tensor_tensor(
                                    residv[:, t_, dcg * 128:(dcg + 1) * 128],
                                    ps_tp32[:, b4 * 128:(b4 + 1) * 128], 1.0,
                                    residv[:, t_, dcg * 128:(dcg + 1) * 128],
                                    ALU.mult, ALU.add), "ve")

        if stage >= 2:
            emit_ffn()
            if debug >= 2 and stage < 9:
                mk.wnow("G", "ve")
                mk.dma("G", dbg["x1"], resid[:])
                mk.dma("G", dbg["ms"], ms_sb[:])
                mk.dma("G", dbg["ones"], ones_bf[:])
                mk.dma("G", dbg["rst"], rs_t[:])
                mk.dma("G", dbg["hsq"], work[:])
        if stage >= 3:
            emit_ln()
            emit_attn(upto=int(stage) - 3 if stage < 9 else 99)
            if debug >= 3 and stage >= 7:
                mk.wnow("G", "ve")
                mk.dma("G", dbg["x2"], resid[:])
                mk.dma("G", dbg["qt"], work[:, 0:16 * 1024])
                mk.dma("G", dbg["vvd"], work[:, 16 * 1024:32 * 1024])
                mk.dma("G", dbg["ktf"], kt_out.ap())
                mk.dma("G", dbg["ssb"], s_sb[:])
                mk.dma("G", dbg["std"], s_t[:])
        if stage >= 9:
            emit_ln()
            emit_ffn()

        mk.wnow("G", "ve")
        mk.wnow("G", "se")
        mk.wnow("G", "te")
        mk.dma("G", y_out.rearrange("(t p) d -> p t d", p=128),
               resid[:].rearrange("p (t d) -> p t d", d=D))
        mk.w("G", "gd", mk.cnt["gd"])

        @block.sync
        def _(e):
            for f in mk.ops["Y"]:
                f(e)

        @block.gpsimd
        def _(e):
            for f in mk.ops["G"]:
                f(e)

        @block.vector
        def _(e):
            for f in mk.ops["V"]:
                f(e)

        @block.scalar
        def _(e):
            for f in mk.ops["S"]:
                f(e)

        @block.tensor
        def _(e):
            for f in mk.ops["T"]:
                f(e)

    return nc


def host_inputs(L0_attn_x, L0_attn_w_qkv, L0_attn_w_o, L0_ffn_W_fc, L0_ffn_W_proj):
    x = np.ascontiguousarray(np.asarray(L0_attn_x, dtype=np.float32))
    wqkv = np.ascontiguousarray(np.asarray(L0_attn_w_qkv, dtype=np.float32))
    wo = np.ascontiguousarray(np.asarray(L0_attn_w_o, dtype=np.float32))
    wfc = np.ascontiguousarray(np.asarray(L0_ffn_W_fc, dtype=np.float32))
    wpr = np.ascontiguousarray(np.asarray(L0_ffn_W_proj, dtype=np.float32))
    identb = np.eye(128, dtype=ml_dtypes.bfloat16)
    identf = np.eye(128, dtype=np.float32)
    in_maps = []
    for c in range(8):
        b, r = c // 4, c % 4
        in_maps.append({
            "x_in": np.ascontiguousarray(x[b, r * TOK:(r + 1) * TOK, :]),
            "wfc_sl": np.ascontiguousarray(wfc[:, c * 256:(c + 1) * 256]),
            "wpr_sl": np.ascontiguousarray(wpr[:, c * 1024:(c + 1) * 1024]),
            "wqkv_sl": np.ascontiguousarray(wqkv[c * 256:(c + 1) * 256, :]),
            "wo_sl": np.ascontiguousarray(wo[c * 256:(c + 1) * 256, :]),
            "identb": identb,
            "identf": identf,
        })
    return in_maps


from concourse.bass_utils import run_bass_kernel_spmd

_NC_CACHE = {}


def kernel(L0_attn_x, L0_attn_w_qkv, L0_attn_w_o, L0_ffn_W_fc, L0_ffn_W_proj):
    if "nc" not in _NC_CACHE:
        _NC_CACHE["nc"] = build(debug=0, stage=9)
    nc = _NC_CACHE["nc"]
    in_maps = host_inputs(L0_attn_x, L0_attn_w_qkv, L0_attn_w_o,
                          L0_ffn_W_fc, L0_ffn_W_proj)
    res = run_bass_kernel_spmd(nc, in_maps, core_ids=list(range(8)))
    y = np.zeros((B, S, D), np.float32)
    for c in range(8):
        b, r = c // 4, c % 4
        y[b, r * TOK:(r + 1) * TOK] = np.asarray(res.results[c]["y"], np.float32)
    return y


# version 04 (final)
# speedup vs baseline: 1.1693x; 1.0549x over previous
import math
import numpy as np
SPART = 9
import ml_dtypes
import concourse.bass as bass
import concourse.mybir as mybir
from contextlib import ExitStack

F32 = mybir.dt.float32
BF16 = mybir.dt.bfloat16
AX = mybir.AxisListType
ALU = mybir.AluOpType
ACT = mybir.ActivationFunctionType

B, S, D, HID, NH, DH = 2, 4096, 2048, 8192, 16, 128
TOK = 1024
NT = TOK // 128
G4 = [[0, 1, 2, 3], [4, 5, 6, 7]]
G8 = [list(range(8))]
QC = 0.501953125
EPS_LN = 1e-5
EPS_RMS = float(np.finfo(np.float32).eps)
SCL = float(1.0 / math.sqrt(DH))

ENGS = ("T", "V", "S", "G", "Y")


class MK:
    def __init__(self, nc):
        self.nc = nc
        self.ops = {e: [] for e in ENGS}
        self.sem = {}
        self.cnt = {}

    def mksem(self, ctx, name):
        self.sem[name] = ctx.enter_context(self.nc.semaphore(name))
        self.cnt[name] = 0

    def w(self, eng, sem, val):
        if val <= 0:
            return
        h = self.sem[sem]
        self.ops[eng].append(lambda e, h=h, v=val: e.wait_ge(h, v))

    def wnow(self, eng, sem):
        self.w(eng, sem, self.cnt[sem])

    def d(self, eng, fn):
        self.ops[eng].append(fn)

    def dinc(self, eng, fn, sem, k=1):
        h = self.sem[sem]
        self.cnt[sem] += k
        self.ops[eng].append(lambda e, fn=fn, h=h, k=k: fn(e).then_inc(h, k))
        return self.cnt[sem]

    def dma(self, eng, out, in_, sem=None):
        if sem is None:
            sem = "yd" if eng == "Y" else "gd"
        return self.dinc(eng, lambda e, o=out, i=in_: e.dma_start(out=o, in_=i), sem, 16)

    def drain(self, eng):
        self.ops[eng].append(lambda e: e.drain())


def build(debug=0, stage=9):
    nc = bass.Bass("TRN2", target_bir_lowering=False)

    x_in = nc.dram_tensor("x_in", [TOK, D], F32, kind="ExternalInput").ap()
    wfc_sl = nc.dram_tensor("wfc_sl", [HID, 256], F32, kind="ExternalInput").ap()
    wpr_sl = nc.dram_tensor("wpr_sl", [D, 1024], F32, kind="ExternalInput").ap()
    wqkv_sl = nc.dram_tensor("wqkv_sl", [256, 3 * D], F32, kind="ExternalInput").ap()
    wo_sl = nc.dram_tensor("wo_sl", [256, D], F32, kind="ExternalInput").ap()
    identb_h = nc.dram_tensor("identb", [128, 128], BF16, kind="ExternalInput").ap()
    identf_h = nc.dram_tensor("identf", [128, 128], F32, kind="ExternalInput").ap()
    y_out = nc.dram_tensor("y", [TOK, D], F32, kind="ExternalOutput").ap()

    w1t_in = nc.dram_tensor("w1t_in", [256, HID], BF16)
    w1t_out = nc.dram_tensor("w1t_out", [D, HID], BF16)
    w2t_in = nc.dram_tensor("w2t_in", [16, 1024, 128], BF16)
    w2t_out = nc.dram_tensor("w2t_out", [8, 16, 1024, 128], BF16)
    qkv_in = nc.dram_tensor("qkv_in", [256, 3 * D], BF16)
    qkv_out = nc.dram_tensor("qkv_out", [D, 3 * D], BF16)
    wo_in = nc.dram_tensor("wo_in", [16, 256, 128], BF16)
    wo_out = nc.dram_tensor("wo_out", [8, 16, 256, 128], BF16)
    kt_in = nc.dram_tensor("kt_in", [16, 128, 1024], BF16)
    kt_out = nc.dram_tensor("kt_out", [4, 16, 128, 1024], BF16)
    s_in = nc.dram_tensor("s_in", [4, 16, 1024], F32)
    s_out = nc.dram_tensor("s_out", [16, 1024], F32)

    dbg = {}
    if debug:
        def dbgt(name, shape, dt):
            dbg[name] = nc.dram_tensor("d_" + name, shape, dt, kind="ExternalOutput").ap()
        dbgt("xlnT", [128, 16 * 1024], BF16)
        dbgt("w1t", [D, HID], BF16)
        dbgt("w2t", [8, 16, 1024, 128], BF16)
        dbgt("qkvw", [D, 3 * D], BF16)
        dbgt("wow", [8, 16, 256, 128], BF16)
        if debug >= 2:
            dbgt("x1", [128, NT * D], F32)
            dbgt("ms", [1, 1024], F32)
            dbgt("msraw", [1, 1024], F32)
            dbgt("ones", [128, 1], BF16)
            dbgt("hsq2d", [128, 2 * 512], BF16)
            dbgt("rst", [128, 8], F32)
            dbgt("hsq", [128, 64 * 512], BF16)
        if debug >= 3:
            dbgt("qt", [128, 16 * 1024], BF16)
            dbgt("vvd", [128, 8 * 2048], BF16)
            dbgt("ktf", [4, 16, 128, 1024], BF16)
            dbgt("ssb", [128, 512], F32)
            dbgt("std", [128, 8 * 16], F32)
            dbgt("x2", [128, NT * D], F32)

    mk = MK(nc)
    with ExitStack() as ctx:
        sb = lambda name, shape, dt: ctx.enter_context(nc.sbuf_tensor(name, shape, dt))
        psb = lambda name, shape, dt: ctx.enter_context(nc.psum_tensor(name, shape, dt))

        resid = sb("resid", [128, NT * D], F32)
        xlnT = sb("xlnT", [128, 16 * 1024], BF16)
        work = sb("work", [128, 64 * 512], BF16)
        ebuf = sb("ebuf", [128, 2 * 4096], BF16)
        wbuf = sb("wbuf", [128, 2 * 4096], BF16)
        hsq2 = sb("hsq2", [128, 2 * 512], BF16)
        fstage = sb("fstage", [128, 2 * 512], F32)
        s_sb = sb("s_sb", [128, 512], F32)
        identb = sb("identb_s", [128, 128], BF16)
        identf = sb("identf_s", [128, 128], F32)
        ones_bf = sb("ones_bf", [128, 1], BF16)
        bnst = sb("bnst", [128, 24], F32)
        bnag = sb("bnag", [128, 2], F32)
        m_v = sb("m_v", [128, 1], F32)
        rstd_v = sb("rstd_v", [128, 1], F32)
        bias_v = sb("bias_v", [128, 1], F32)
        zp = sb("zp", [128, 2 * 8], F32)
        zs = sb("zs", [128, 2], F32)
        rbf = sb("rbf", [128, 16], BF16)
        ms_sb = sb("ms_sb", [1, 2 * 512], F32)
        rs_t = sb("rs_t", [128, 8], F32)
        s_t = sb("s_t", [128, 8 * 16], F32)
        qsc_s = sb("qsc_s", [128, 64], F32)
        qsb_s = sb("qsb_s", [128, 64], BF16)
        qth_s = sb("qth_s", [128, 64], F32)
        vstage = sb("vstage", [128, 2 * 128], BF16)

        ps0 = psb("ps0", [128, 512], F32)
        ps1 = psb("ps1", [128, 512], F32)
        ps2 = psb("ps2", [128, 512], F32)
        ps_tpb = psb("ps_tpb", [128, 1024], BF16)
        ps_tp32 = psb("ps_tp32", [128, 512], F32)
        ps_ms = psb("ps_ms", [1, 512], F32)
        ps_s0 = psb("ps_s0", [128, 512], F32)
        ps_s1 = psb("ps_s1", [128, 512], F32)
        PS = [ps0, ps1, ps2]
        PSS = [ps_s0, ps_s1]

        for s in ["yd", "gd", "te", "ve", "se", "w0", "w1",
                  "ccA", "ccB", "ccC", "ccD", "ccE", "ccF"]:
            mk.mksem(ctx, s)
        block = ctx.enter_context(nc.Block())

        residv = resid[:].rearrange("p (t d) -> p t d", d=D)
        xlntv = xlnT[:].rearrange("p (c f) -> p c f", f=1024)
        hsqv = work[:].rearrange("p (c f) -> p c f", f=512)
        qtv = work[:, 0:16 * 1024].rearrange("p (c f) -> p c f", f=1024)
        vv = work[:, 16 * 1024:32 * 1024].rearrange("p (t f) -> p t f", f=D)
        valtv = xlnT[:].rearrange("p (c f) -> p c f", f=1024)
        ebv = ebuf[:].rearrange("p (i f) -> p i f", f=4096)
        wbv = wbuf[:].rearrange("p (i f) -> p i f", f=4096)
        ktsv = fstage[:].bitcast(BF16).rearrange("p (i f) -> p i f", f=1024)
        _hsq2a = hsq2[:].rearrange("p (i f) -> p i f", f=512)
        _hsq2b = s_sb[:].bitcast(BF16).rearrange("p (i f) -> p i f", f=512)

        def hsq2v_slot(k):
            return _hsq2a[:, k % 2, :] if (k % 4) < 2 else _hsq2b[:, k % 2, :]
        fsv = fstage[:].rearrange("p (i f) -> p i f", f=512)
        zpv = zp[:].rearrange("p (i f) -> p i f", f=8)
        msv = ms_sb[:].rearrange("p (i f) -> p i f", f=512)
        stv = s_t[:].rearrange("p (t h) -> p t h", h=16)
        vstv = vstage[:].rearrange("p (i f) -> p i f", f=128)
        xlnb = ebuf[:, 4096:4096 + 2048]
        stmp = ebuf[:, 0:4096].bitcast(F32)[0:16, 0:1024]
        qstage32 = work[:, 0:32 * 1024].bitcast(F32)

        def wbslot(i, f):
            return wbv[:, i, :].rearrange("p (c f) -> p c f", f=f)

        mk.dma("G", resid[:].rearrange("p (t d) -> p t d", d=D),
               x_in.rearrange("(t p) d -> p t d", p=128))
        x_done = mk.cnt["gd"]
        mk.dma("Y", identb[:], identb_h)
        mk.dma("Y", identf[:], identf_h)
        id_done = mk.cnt["yd"]
        mk.d("V", lambda e: e.memset(ones_bf[:], 1.0))

        def quant_stage(src_ap, fd, inner):
            mk.wnow("Y", "ve")
            ld = mk.dma("Y", qstage32[:, 0:fd].rearrange(
                "p (a b) -> p a b", b=inner), src_ap)
            mk.w("V", "yd", ld)
            mk.wnow("V", "gd")
            mk.wnow("V", "te")
            wb = ebv[:, 0, 0:fd]
            wb3 = wb.rearrange("p (g k) -> p g k", k=64)
            g = fd // 64
            mk.d("V", lambda e, wb=wb, fd=fd: e.tensor_copy(wb, qstage32[:, 0:fd]))
            mk.drain("V")
            mk.d("V", lambda e, wb3=wb3, g=g: e.tensor_reduce(
                qsc_s[:, 0:g], wb3, axis=AX.X, op=ALU.add, apply_absolute_value=True))
            mk.drain("V")
            mk.d("V", lambda e, g=g: e.tensor_scalar(
                qsb_s[:, 0:g], qsc_s[:, 0:g], 1.0 / 64.0, None, ALU.mult))
            mk.drain("V")
            mk.d("V", lambda e, g=g: e.tensor_scalar(
                qsc_s[:, 0:g], qsb_s[:, 0:g], 1e-8, None, ALU.max))
            mk.drain("V")
            mk.d("V", lambda e, g=g: e.tensor_scalar(
                qth_s[:, 0:g], qsc_s[:, 0:g], QC, None, ALU.mult))
            mk.drain("V")
            thb = qth_s[:, 0:g].unsqueeze(2).broadcast_to([128, g, 64])
            scb = qsc_s[:, 0:g].unsqueeze(2).broadcast_to([128, g, 64])
            mp = ebv[:, 1, 0:fd].rearrange("p (g k) -> p g k", k=64)
            mn = work[:, 8192:8192 + fd].rearrange("p (g k) -> p g k", k=64)
            mk.d("V", lambda e, mp=mp, wb3=wb3, thb=thb: e.scalar_tensor_tensor(
                mp, wb3, 1.0, thb, ALU.mult, ALU.is_gt))
            mk.d("V", lambda e, mn=mn, wb3=wb3, thb=thb: e.scalar_tensor_tensor(
                mn, wb3, -1.0, thb, ALU.mult, ALU.is_gt))
            mk.drain("V")
            mk.d("V", lambda e, mp=mp, mn=mn: e.tensor_tensor(mp, mp, mn, ALU.subtract))
            mk.drain("V")
            return mk.dinc("V", lambda e, wb3=wb3, mp=mp, scb=scb: e.tensor_tensor(
                wb3, mp, scb, ALU.mult), "ve")

        first_tb = [True]

        def transpose_blocks(nblk, src_fn, pairs_fn, vready):
            if first_tb[0]:
                mk.w("T", "yd", id_done)
                first_tb[0] = False
            nbat = (nblk + 7) // 8
            for b in range(nbat):
                k0, k1 = b * 8, min(b * 8 + 8, nblk)
                mk.w("T", "ve", vready if b == 0 else mk.cnt["ve"])
                mk.wnow("T", "gd")
                for k in range(k0, k1):
                    fn = lambda e, k=k, j=k - k0: e.transpose(
                        ps_tpb[:, j * 128:(j + 1) * 128], src_fn(k), identb[:])
                    if k == k1 - 1:
                        mk.dinc("T", fn, "te")
                    else:
                        mk.d("T", fn)
                mk.wnow("V", "te")
                stage = ktsv[:, b % 2, :]
                mk.dinc("V", lambda e, st=stage: e.tensor_copy(st, ps_tpb[:]), "ve")
                mk.wnow("G", "ve")
                for dst_ap, src_ap in pairs_fn(b, stage):
                    mk.dma("G", dst_ap, src_ap)

        wfc4 = wfc_sl.rearrange("(st tq p) d -> st p tq d", p=128, tq=16)
        w1ap = w1t_in.ap().rearrange("(dc p) (st tq f) -> p st tq dc f",
                                     p=128, st=4, f=128)
        for st4 in range(4):
            vd = quant_stage(wfc4[st4], 4096, 256)
            srcv = ebv[:, 0, 0:4096].rearrange("p (tq dc f) -> p tq dc f", dc=2, f=128)
            def w1pairs(b, st, st4=st4):
                stv4 = st.rearrange("p (tq dc f) -> p tq dc f", dc=2, f=128)
                return [(w1ap[:, st4, b * 4:b * 4 + 4, dc, :], stv4[:, :, dc, :])
                        for dc in range(2)]
            transpose_blocks(
                32,
                lambda k, srcv=srcv: srcv[:, k // 2, k % 2, :],
                w1pairs,
                vd)
        w1t_ready = mk.cnt["gd"]
        mk.w("G", "gd", w1t_ready)
        cc_w1t = mk.dinc("G", lambda e: e.collective_compute(
            "AllGather", ALU.bypass, replica_groups=G8,
            ins=[w1t_in.ap().opt()], outs=[w1t_out.ap().opt()]), "ccA")

        wpr4 = wpr_sl.rearrange("(st tq p) h -> st p tq h", p=128, tq=4)
        w2ap = w2t_in.ap().rearrange("dc (hc p) f -> p dc hc f", p=128)
        for st4 in range(4):
            vd = quant_stage(wpr4[st4], 4096, 1024)
            srcv = ebv[:, 0, 0:4096].rearrange("p (tq hc f) -> p tq hc f", hc=8, f=128)
            transpose_blocks(
                32,
                lambda k, srcv=srcv: srcv[:, k // 8, k % 8, :],
                lambda b, st, st4=st4: [(w2ap[:, st4 * 4 + b, :, :],
                                         st.rearrange("p (hc f) -> p hc f", f=128))],
                vd)
        w2t_ready = mk.cnt["gd"]
        mk.w("G", "gd", w2t_ready)
        cc_w2t = mk.dinc("G", lambda e: e.collective_compute(
            "AllGather", ALU.bypass, replica_groups=G8,
            ins=[w2t_in.ap().opt()], outs=[w2t_out.ap().opt()]), "ccB")

        qkv2 = wqkv_sl.rearrange("(r p) f -> r p f", p=128)
        qin2 = qkv_in.ap().rearrange("(r p) f -> r p f", p=128)
        for r in range(2):
            for half in range(2):
                f0, f1 = half * 3072, (half + 1) * 3072
                mk.wnow("Y", "ve")
                ld = mk.dma("Y", qstage32[:, 0:3072], qkv2[r, :, f0:f1])
                mk.w("V", "yd", ld)
                mk.wnow("V", "gd")
                mk.wnow("V", "te")
                vd = mk.dinc("V", lambda e: e.tensor_copy(
                    ebv[:, 0, 0:3072], qstage32[:, 0:3072]), "ve")
                mk.w("G", "ve", vd)
                mk.dma("G", qin2[r, :, f0:f1], ebv[:, 0, 0:3072])
        qkv_ready = mk.cnt["gd"]
        mk.w("G", "gd", qkv_ready)
        cc_qkv = mk.dinc("G", lambda e: e.collective_compute(
            "AllGather", ALU.bypass, replica_groups=G8,
            ins=[qkv_in.ap().opt()], outs=[qkv_out.ap().opt()]), "ccC")

        wo2 = wo_sl.rearrange("(r p) f -> r p f", p=128)
        wo_blk = wo_in.ap().rearrange("dc (r p) f -> r p dc f", p=128)
        for r in range(2):
            mk.wnow("Y", "ve")
            ld = mk.dma("Y", qstage32[:, 0:2048], wo2[r])
            mk.w("V", "yd", ld)
            mk.wnow("V", "gd")
            mk.wnow("V", "te")
            vd = mk.dinc("V", lambda e: e.tensor_copy(
                ebv[:, 0, 0:2048], qstage32[:, 0:2048]), "ve")
            mk.w("G", "ve", vd)
            mk.dma("G", wo_blk[r], ebv[:, 0, 0:2048].rearrange("p (dc f) -> p dc f", f=128))
        wo_ready = mk.cnt["gd"]
        mk.w("G", "gd", wo_ready)
        cc_wo = mk.dinc("G", lambda e: e.collective_compute(
            "AllGather", ALU.bypass, replica_groups=G8,
            ins=[wo_in.ap().opt()], outs=[wo_out.ap().opt()]), "ccD")


        def emit_ln(gd_wait=0):
            if gd_wait:
                mk.w("V", "gd", gd_wait)
                mk.w("S", "gd", gd_wait)
            mk.drain("V")
            prev_xln_se = 0
            prev_tr_te = 0
            for t in range(NT):
                xt = residv[:, t, :]
                if prev_xln_se:
                    mk.w("V", "se", prev_xln_se)
                for q4 in range(4):
                    mk.d("V", lambda e, xt=xt, q4=q4: e.bn_stats(
                        bnst[:, q4 * 6:(q4 + 1) * 6], xt[:, q4 * 512:(q4 + 1) * 512]))
                mk.drain("V")
                mk.d("V", lambda e: e.bn_aggr(bnag[:], bnst[:]))
                mk.drain("V")
                mk.d("V", lambda e: e.tensor_copy(m_v[:], bnag[:, 0:1]))
                mk.d("V", lambda e: e.tensor_scalar(
                    rstd_v[:], bnag[:, 1:2], 1.0, EPS_LN, ALU.mult, ALU.add))
                mk.drain("V")
                vd = mk.dinc("V", lambda e: e.reciprocal(rstd_v[:], rstd_v[:]), "ve")
                mk.w("S", "ve", vd)
                sd = mk.dinc("S", lambda e: e.activation(rstd_v[:], rstd_v[:], ACT.Sqrt), "se")
                mk.w("V", "se", sd)
                vd = mk.dinc("V", lambda e: e.scalar_tensor_tensor(
                    bias_v[:], m_v[:], -1.0, rstd_v[:], ALU.mult, ALU.mult), "ve")
                mk.w("S", "ve", vd)
                if prev_tr_te:
                    mk.w("S", "te", prev_tr_te)
                sd = mk.dinc("S", lambda e, xt=xt: e.activation(
                    xlnb[:], xt, ACT.Identity, bias=bias_v[:], scale=rstd_v[:]), "se")
                prev_xln_se = sd
                for bb in range(2):
                    mk.w("T", "se", sd)
                    mk.wnow("T", "ve")
                    for k in range(8):
                        dc = bb * 8 + k
                        fn = lambda e, dc=dc, k=k: e.transpose(
                            ps_tpb[:, k * 128:(k + 1) * 128],
                            xlnb[:, dc * 128:(dc + 1) * 128], identb[:])
                        if k == 7:
                            prev_tr_te = mk.dinc("T", fn, "te")
                        else:
                            mk.d("T", fn)
                    mk.wnow("V", "te")
                    dst = xlntv[:, bb * 8:(bb + 1) * 8, t * 128:(t + 1) * 128]
                    mk.dinc("V", lambda e, dst=dst: e.tensor_copy(
                        dst, ps_tpb[:].rearrange("p (c f) -> p c f", f=128)), "ve")

        emit_ln(gd_wait=wo_ready)

        if debug:
            mk.wnow("G", "ve")
            mk.dma("G", dbg["xlnT"], xlnT[:])
            mk.w("G", "ccD", mk.cnt["ccD"])
            mk.dma("G", dbg["w1t"], w1t_out.ap())
            mk.dma("G", dbg["w2t"], w2t_out.ap())
            mk.dma("G", dbg["qkvw"], qkv_out.ap())
            mk.dma("G", dbg["wow"], wo_out.ap())

        wslot_guard = {0: 0, 1: 0}

        def emit_ffn():
            mk.wnow("T", "ve")
            mk.wnow("T", "se")
            w1v = w1t_out.ap().rearrange("(dc p) h -> p dc h", p=128)
            w2v = w2t_out.ap()
            for hf in range(2):
                psfree = {}
                hsqready = {}
                hsq2ready = {}
                msdone_te = {}
                msevac_ve = {}
                sup_te = {}
                mstmp2 = ebuf[0:1, 0:4096].bitcast(F32)[0:1, 0:512]

                def emit_ms_group(sp2):
                    mk.w("T", "ve", hsq2ready[sp2 * 2 + 1])
                    if sp2 >= 2:
                        mk.w("T", "ve", msevac_ve[sp2 - 2])
                    for cc2 in range(2):
                        k2 = sp2 * 2 + cc2
                        fn = lambda e, k2=k2, cc2=cc2, sp2=sp2: e.matmul(
                            PSS[sp2 % 2][0:1, :], ones_bf[:], hsq2v_slot(k2),
                            start=(cc2 == 0), stop=(cc2 == 1))
                        if cc2 == 1:
                            msdone_te[sp2] = mk.dinc("T", fn, "te")
                        else:
                            mk.d("T", fn)
                    mk.w("V", "te", msdone_te[sp2])
                    if sp2 == 0:
                        msevac_ve[sp2] = mk.dinc("V", lambda e: e.tensor_copy(
                            msv[:, 0, :], PSS[0][0:1, :]), "ve")
                    else:
                        mk.d("V", lambda e, sp2=sp2: e.tensor_copy(
                            mstmp2, PSS[sp2 % 2][0:1, :]))
                        mk.drain("V")
                        msevac_ve[sp2] = mk.dinc("V", lambda e, sp2=sp2: e.tensor_tensor(
                            msv[:, sp2 % 2, :], mstmp2, msv[:, 1 - sp2 % 2, :],
                            ALU.add), "ve")
                mk.wnow("S", "te")
                for sp in range(32):
                    mk.w("Y", "te", wslot_guard[sp % 2])
                    mk.w("Y", "ccA", cc_w1t)
                    ld = mk.dma("Y", wbslot(sp % 2, 256),
                                w1v[:, :, sp * 256:(sp + 1) * 256],
                                sem="w%d" % (sp % 2))
                    if sp >= 2:
                        emit_ms_group(sp - 2)
                    mk.w("T", "w%d" % (sp % 2), ld)
                    for cc2 in range(2):
                        k = sp * 2 + cc2
                        if k >= 3:
                            mk.w("T", "se", psfree[k - 3])
                        te_g = 0
                        for dc in range(16):
                            fn = (lambda e, k=k, cc2=cc2, dc=dc, sp=sp, hf=hf:
                                  e.matmul(PS[k % 3][:],
                                           wbslot(sp % 2, 256)[:, dc, cc2 * 128:(cc2 + 1) * 128],
                                           xlntv[:, dc, hf * 512:(hf + 1) * 512],
                                           start=(dc == 0), stop=(dc == 15)))
                            if dc == 15:
                                te_g = mk.dinc("T", fn, "te")
                            else:
                                mk.d("T", fn)
                        mk.w("S", "te", te_g)
                        if k < 2:
                            mk.w("S", "gd", wo_ready)
                        if k >= 2:
                            mk.w("S", "ve", hsqready.get(k - 2, 0))
                        sd = mk.dinc("S", lambda e, k=k: e.activation(
                            fsv[:, k % 2, :], PS[k % 3][:], ACT.Relu), "se")
                        psfree[k] = sd
                        mk.w("V", "se", sd)
                        if k >= 4:
                            mk.w("V", "te", msdone_te.get((k - 4) // 2, 0))
                        hsqready[k] = mk.dinc("V", lambda e, k=k: e.tensor_tensor(
                            hsqv[:, k, :], fsv[:, k % 2, :], fsv[:, k % 2, :], ALU.mult), "ve")
                        mk.drain("V")
                        hsq2ready[k] = mk.dinc("V", lambda e, k=k: e.tensor_tensor(
                            hsq2v_slot(k), hsqv[:, k, :], hsqv[:, k, :], ALU.mult), "ve")
                    sup_te[sp] = te_g
                    wslot_guard[sp % 2] = te_g
                    if sp == 31:
                        emit_ms_group(30)
                        emit_ms_group(31)
                if debug >= 2:
                    mk.wnow("G", "ve")
                    mk.dma("G", dbg["msraw"].rearrange("o (i f) -> o i f", f=512)[:, hf, :],
                           msv[:, 1, :])
                    if hf == 0:
                        mk.dma("G", dbg["hsq2d"], hsq2[:])
                mrow = msv[:, 1, :]
                mk.drain("V")
                mk.d("V", lambda e, mrow=mrow: e.tensor_scalar(
                    mrow, mrow, 1.0 / HID, EPS_RMS, ALU.mult, ALU.add))
                mk.drain("V")
                vd = mk.dinc("V", lambda e, mrow=mrow: e.reciprocal(mrow, mrow), "ve")
                mk.w("S", "ve", vd)
                sd = mk.dinc("S", lambda e, mrow=mrow: e.activation(mrow, mrow, ACT.Sqrt), "se")
                mk.w("V", "se", sd)
                vd = mk.dinc("V", lambda e, mrow=mrow: e.tensor_scalar(
                    mrow, mrow, 0.5, None, ALU.mult), "ve")
                mk.w("T", "ve", vd)
                for b4 in range(4):
                    fn = lambda e, b4=b4, mrow=mrow: e.transpose(
                        ps_tp32[:, b4 * 128:b4 * 128 + 1],
                        mrow[0:1, b4 * 128:(b4 + 1) * 128], identf[0:1, 0:1])
                    if b4 == 3:
                        td = mk.dinc("T", fn, "te")
                    else:
                        mk.d("T", fn)
                mk.w("V", "te", td)
                mk.dinc("V", lambda e, hf=hf: e.tensor_copy(
                    rs_t[:, hf * 4:(hf + 1) * 4],
                    ps_tp32[:].rearrange("p (b f) -> p b f", f=128)[:, :, 0]), "ve")

                grp_half_te = {}
                stt_ve = {}
                for dcg in range(16):
                    ld_half = {}
                    for half in range(2):
                        hb = dcg * 2 + half
                        mk.w("Y", "te", wslot_guard[hb % 2])
                        mk.w("Y", "ccB", cc_w2t)
                        for r8 in range(4):
                            r = half * 4 + r8
                            mk.dma("Y", wbslot(hb % 2, 128)[:, r8 * 8:(r8 + 1) * 8, :],
                                   w2v[r, dcg].rearrange("(hc p) f -> p hc f", p=128),
                                   sem="w%d" % (hb % 2))
                        ld_half[half] = (hb % 2, mk.cnt["w%d" % (hb % 2)])
                    if dcg >= 3:
                        mk.w("T", "ve", stt_ve[dcg - 3])
                    for c in range(64):
                        hb = dcg * 2 + (c // 32)
                        if c in (0, 32):
                            sl, lv = ld_half[c // 32]
                            mk.w("T", "w%d" % sl, lv)
                        fn = (lambda e, dcg=dcg, c=c, hb=hb:
                              e.matmul(PS[dcg % 3][:],
                                       wbslot(hb % 2, 128)[:, c % 32, :],
                                       hsqv[:, c, :],
                                       start=(c == 0), stop=(c == 63)))
                        if c == 31:
                            grp_half_te[dcg * 2] = mk.dinc("T", fn, "te")
                        elif c == 63:
                            grp_half_te[dcg * 2 + 1] = mk.dinc("T", fn, "te")
                        else:
                            mk.d("T", fn)
                    grp_te = grp_half_te[dcg * 2 + 1]
                    wslot_guard[0] = grp_half_te[dcg * 2]
                    wslot_guard[1] = grp_te
                    mk.w("V", "te", grp_te)
                    ev = mk.dinc("V", lambda e, dcg=dcg: e.tensor_copy(
                        fsv[:, dcg % 2, :], PS[dcg % 3][:]), "ve")
                    val = 0
                    for b4 in range(4):
                        mk.w("T", "ve", ev if b4 == 0 else mk.cnt["ve"])
                        td = mk.dinc("T", lambda e, dcg=dcg, b4=b4: e.transpose(
                            ps_tp32[:, b4 * 128:(b4 + 1) * 128],
                            fsv[:, dcg % 2, b4 * 128:(b4 + 1) * 128], identf[:]), "te")
                        mk.w("V", "te", td)
                        t_ = hf * 4 + b4
                        val = mk.dinc("V", lambda e, t_=t_, dcg=dcg, b4=b4:
                                      e.scalar_tensor_tensor(
                                          residv[:, t_, dcg * 128:(dcg + 1) * 128],
                                          ps_tp32[:, b4 * 128:(b4 + 1) * 128],
                                          rs_t[:, t_:t_ + 1],
                                          residv[:, t_, dcg * 128:(dcg + 1) * 128],
                                          ALU.mult, ALU.add), "ve")
                    stt_ve[dcg] = val

        def emit_attn(upto=99):
            bank_ve = {0: 0, 1: 0, 2: 0}
            mk.wnow("T", "ve")
            qv = qkv_out.ap().rearrange("(dc p) f -> p dc f", p=128)
            ktov = kt_out.ap()
            wov = wo_out.ap()
            sup_order = list(range(8, 16)) + list(range(0, 8)) + list(range(16, 24))
            sup_te2 = {}
            kt_written = 0
            cc_kt = None
            for si, sp in enumerate(sup_order):
                mk.w("Y", "te", wslot_guard[si % 2])
                mk.w("Y", "ccC", cc_qkv)
                ld = mk.dma("Y", wbslot(si % 2, 256),
                            qv[:, :, sp * 256:(sp + 1) * 256],
                            sem="w%d" % (si % 2))
                mk.w("T", "w%d" % (si % 2), ld)
                base_col = sp * 256
                g_te = 0
                if base_col < 4096:
                    for cc2 in range(2):
                        col = base_col + cc2 * 128
                        ch = col // 128
                        for tb in range(2):
                            gidx = ch * 2 + tb
                            mk.w("T", "ve", bank_ve[gidx % 3])
                            for dc in range(16):
                                fn = (lambda e, si=si, cc2=cc2, dc=dc, tb=tb, gidx=gidx:
                                      e.matmul(PS[gidx % 3][:],
                                               wbslot(si % 2, 256)[:, dc, cc2 * 128:(cc2 + 1) * 128],
                                               xlntv[:, dc, tb * 512:(tb + 1) * 512],
                                               start=(dc == 0), stop=(dc == 15)))
                                if dc == 15:
                                    g_te = mk.dinc("T", fn, "te")
                                else:
                                    mk.d("T", fn)
                            mk.w("V", "te", g_te)
                            if col < 2048:
                                bank_ve[gidx % 3] = mk.dinc(
                                    "V", lambda e, ch=ch, tb=tb, gidx=gidx: e.tensor_copy(
                                        qtv[:, ch, tb * 512:(tb + 1) * 512],
                                        PS[gidx % 3][:]), "ve")
                            else:
                                kc = ch - 16
                                bank_ve[gidx % 3] = mk.dinc(
                                    "V", lambda e, kc=kc, tb=tb, gidx=gidx: e.tensor_copy(
                                        ktsv[:, kc % 2, tb * 512:(tb + 1) * 512],
                                        PS[gidx % 3][:]), "ve")
                        if col >= 2048:
                            kc = ch - 16
                            mk.wnow("G", "ve")
                            mk.dma("G", kt_in.ap()[kc], ktsv[:, kc % 2, :])
                            kt_written += 1
                            if kt_written == 16:
                                mk.w("G", "gd", mk.cnt["gd"])
                                cc_kt = mk.dinc("G", lambda e: e.collective_compute(
                                    "AllGather", ALU.bypass, replica_groups=G4,
                                    ins=[kt_in.ap().opt()], outs=[kt_out.ap().opt()]),
                                    "ccE")
                else:
                    vc0 = base_col - 4096
                    for tt in range(NT):
                        mk.w("T", "ve", bank_ve[tt % 3])
                        for dc in range(16):
                            fn = (lambda e, si=si, dc=dc, tt=tt:
                                  e.matmul(PS[tt % 3][:, 0:256],
                                           xlntv[:, dc, tt * 128:(tt + 1) * 128],
                                           wbslot(si % 2, 256)[:, dc, :],
                                           start=(dc == 0), stop=(dc == 15)))
                            if dc == 15:
                                g_te = mk.dinc("T", fn, "te")
                            else:
                                mk.d("T", fn)
                        mk.w("V", "te", g_te)
                        bank_ve[tt % 3] = mk.dinc(
                            "V", lambda e, tt=tt, vc0=vc0: e.tensor_copy(
                                vv[:, tt, vc0:vc0 + 256], PS[tt % 3][:, 0:256]), "ve")
                sup_te2[si] = g_te
                wslot_guard[si % 2] = g_te

            if upto < 1:
                return
            mk.wnow("T", "ve")
            mk.wnow("T", "se")
            se_exp = {}
            te_sdone = {}
            ve_sacc = {}
            ve_r = {}
            head_te = {}
            g = 0

            def emit_smm(j):
                hj, qtj = j // 8, j % 8
                mk.w("T", "ve", ve_r[j])
                if j >= 2:
                    mk.w("T", "ve", ve_sacc[j - 2])
                for kc8 in range(32):
                    fn = (lambda e, j=j, kc8=kc8: e.matmul(
                        PSS[j % 2][:, kc8:kc8 + 1],
                        ebv[:, j % 2, kc8 * 128:(kc8 + 1) * 128],
                        rbf[:, (j % 2) * 8:(j % 2) * 8 + 1],
                        start=True, stop=True))
                    if kc8 == 31:
                        te_sdone[j] = mk.dinc("T", fn, "te")
                    else:
                        mk.d("T", fn)
                mk.w("V", "te", te_sdone[j])
                if qtj == 0:
                    ve_sacc[j] = mk.dinc("V", lambda e, hj=hj, j=j: e.tensor_copy(
                        s_sb[:, hj * 32:(hj + 1) * 32], PSS[j % 2][:, 0:32]), "ve")
                else:
                    ve_sacc[j] = mk.dinc("V", lambda e, hj=hj, j=j:
                                         e.scalar_tensor_tensor(
                                             s_sb[:, hj * 32:(hj + 1) * 32],
                                             PSS[j % 2][:, 0:32], 1.0,
                                             s_sb[:, hj * 32:(hj + 1) * 32],
                                             ALU.mult, ALU.add), "ve")

            for h in range(NH):
                mk.w("Y", "te", wslot_guard[h % 2])
                mk.w("Y", "ccE", cc_kt)
                for r4 in range(4):
                    mk.dma("Y", wbv[:, h % 2, r4 * 1024:(r4 + 1) * 1024], ktov[r4, h],
                           sem="w%d" % (h % 2))
                ktl = mk.cnt["w%d" % (h % 2)]
                mk.w("T", "w%d" % (h % 2), ktl)
                for qt in range(8):
                    it = h * 8 + qt
                    for kc in range(8):
                        if g >= 3:
                            mk.w("T", "se", se_exp[g - 3])
                        fn = (lambda e, h=h, qt=qt, kc=kc, g=g: e.matmul(
                            PS[g % 3][:],
                            qtv[:, h, qt * 128:(qt + 1) * 128],
                            wbv[:, h % 2, kc * 512:(kc + 1) * 512],
                            start=True, stop=True))
                        te_mm = mk.dinc("T", fn, "te")
                        head_te[h] = te_mm
                        mk.w("S", "te", te_mm)
                        if it >= 2 and kc == 0:
                            mk.w("S", "te", te_sdone[it - 2])
                            mk.w("S", "ve", ve_r[it - 2])
                        se_exp[g] = mk.dinc("S", lambda e, it=it, kc=kc, g=g: e.activation(
                            ebv[:, it % 2, kc * 512:(kc + 1) * 512],
                            PS[g % 3][:], ACT.Exp, scale=SCL,
                            accum_out=zpv[:, it % 2, kc:kc + 1]), "se")
                        g += 1
                    mk.w("V", "se", se_exp[g - 1])
                    mk.d("V", lambda e, it=it: e.reduce_sum(
                        zs[:, it % 2:it % 2 + 1], zpv[:, it % 2, :], axis=AX.X))
                    mk.drain("V")
                    mk.d("V", lambda e, it=it: e.reciprocal(
                        zs[:, it % 2:it % 2 + 1], zs[:, it % 2:it % 2 + 1]))
                    mk.drain("V")
                    ve_r[it] = mk.dinc("V", lambda e, it=it: e.tensor_copy(
                        rbf[:, (it % 2) * 8:(it % 2) * 8 + 1], zs[:, it % 2:it % 2 + 1]), "ve")
                    if qt == 7:
                        wslot_guard[h % 2] = head_te[h]
                    if it >= 1:
                        emit_smm(it - 1)
            emit_smm(NH * 8 - 1)

            if upto < 2:
                return
            for b4 in range(4):
                mk.w("T", "ve", ve_sacc[NH * 8 - 1] if b4 == 0 else mk.cnt["ve"])
                td = mk.dinc("T", lambda e, b4=b4: e.transpose(
                    ps_tp32[:, b4 * 128:(b4 + 1) * 128],
                    s_sb[:, b4 * 128:(b4 + 1) * 128], identf[:]), "te")
                mk.w("V", "te", td)
                mk.dinc("V", lambda e, b4=b4: e.tensor_copy(
                    fstage[:, b4 * 128:(b4 + 1) * 128],
                    ps_tp32[:, b4 * 128:(b4 + 1) * 128]), "ve")
            mk.wnow("G", "ve")
            dfull = s_in.ap().rearrange("r (blk hlo) (kc8 f) -> hlo blk r kc8 f",
                                        hlo=4, kc8=8)
            for blk in range(4):
                for hlo in range(4):
                    mk.dma("G", dfull[hlo, blk],
                           fstage[hlo * 32:(hlo + 1) * 32,
                                  blk * 128:(blk + 1) * 128])
            mk.w("G", "gd", mk.cnt["gd"])
            cc_s = mk.dinc("G", lambda e: e.collective_compute(
                "ReduceScatter", ALU.add, replica_groups=G4,
                ins=[s_in.ap().opt()], outs=[s_out.ap().opt()]), "ccF")
            mk.w("G", "ccF", cc_s)
            sld = mk.dma("G", stmp, s_out.ap())
            for t in range(NT):
                mk.w("T", "gd", sld)
                mk.wnow("T", "ve")
                td = mk.dinc("T", lambda e, t=t: e.transpose(
                    ps_tp32[:, 0:16], stmp[:, t * 128:(t + 1) * 128],
                    identf[0:16, 0:16]), "te")
                mk.w("V", "te", td)
                mk.dinc("V", lambda e, t=t: e.tensor_copy(
                    stv[:, t, :], ps_tp32[:, 0:16]), "ve")

            if upto < 3:
                return
            for t in range(NT):
                for h in range(NH):
                    i = t * NH + h
                    mk.wnow("V", "te")
                    vd = mk.dinc("V", lambda e, t=t, h=h, i=i: e.tensor_scalar(
                        vstv[:, i % 2, :], vv[:, t, h * 128:(h + 1) * 128],
                        stv[:, t, h:h + 1], None, ALU.mult), "ve")
                    mk.w("T", "ve", vd)
                    k8 = i % 8
                    fn = lambda e, k8=k8, i=i: e.transpose(
                        ps_tpb[:, k8 * 128:(k8 + 1) * 128], vstv[:, i % 2, :], identb[:])
                    mk.dinc("T", fn, "te")
                    if k8 == 7:
                        mk.wnow("V", "te")
                        t0, h0 = (i - 7) // NH, (i - 7) % NH
                        dst2 = valtv[:, h0:h0 + 8, t0 * 128:(t0 + 1) * 128]
                        mk.dinc("V", lambda e, dst2=dst2: e.tensor_copy(
                            dst2, ps_tpb[:].rearrange("p (c f) -> p c f", f=128)), "ve")

            if upto < 4:
                return
            grp_te = {}
            mk.wnow("T", "ve")
            mk.wnow("T", "se")
            for dcg in range(16):
                mk.w("Y", "te", wslot_guard[dcg % 2])
                mk.w("Y", "ccD", cc_wo)
                for r8 in range(8):
                    mk.dma("Y", wbslot(dcg % 2, 128)[:, r8 * 2:(r8 + 1) * 2, :],
                           wov[r8, dcg].rearrange("(vc p) f -> p vc f", p=128),
                           sem="w%d" % (dcg % 2))
                ld = mk.cnt["w%d" % (dcg % 2)]
                for tb in range(2):
                    mk.w("T", "w%d" % (dcg % 2), ld)
                    for vc in range(16):
                        fn = (lambda e, dcg=dcg, vc=vc, tb=tb: e.matmul(
                            PS[(dcg * 2 + tb) % 3][:],
                            wbslot(dcg % 2, 128)[:, vc, :],
                            valtv[:, vc, tb * 512:(tb + 1) * 512],
                            start=(vc == 0), stop=(vc == 15)))
                        if vc == 15:
                            grp_te[(dcg, tb)] = mk.dinc("T", fn, "te")
                        else:
                            mk.d("T", fn)
                    if tb == 1:
                        wslot_guard[dcg % 2] = grp_te[(dcg, tb)]
                    mk.w("V", "te", grp_te[(dcg, tb)])
                    ev = mk.dinc("V", lambda e, dcg=dcg, tb=tb: e.tensor_copy(
                        fsv[:, tb, :], PS[(dcg * 2 + tb) % 3][:]), "ve")
                    for b4 in range(4):
                        mk.w("T", "ve", ev if b4 == 0 else mk.cnt["ve"])
                        td = mk.dinc("T", lambda e, tb=tb, b4=b4: e.transpose(
                            ps_tp32[:, b4 * 128:(b4 + 1) * 128],
                            fsv[:, tb, b4 * 128:(b4 + 1) * 128], identf[:]), "te")
                        mk.w("V", "te", td)
                        t_ = tb * 4 + b4
                        mk.dinc("V", lambda e, t_=t_, dcg=dcg, b4=b4:
                                e.scalar_tensor_tensor(
                                    residv[:, t_, dcg * 128:(dcg + 1) * 128],
                                    ps_tp32[:, b4 * 128:(b4 + 1) * 128], 1.0,
                                    residv[:, t_, dcg * 128:(dcg + 1) * 128],
                                    ALU.mult, ALU.add), "ve")

        if stage >= 2:
            emit_ffn()
            if debug >= 2 and stage < 9:
                mk.wnow("G", "ve")
                mk.dma("G", dbg["x1"], resid[:])
                mk.dma("G", dbg["ms"], ms_sb[:])
                mk.dma("G", dbg["ones"], ones_bf[:])
                mk.dma("G", dbg["rst"], rs_t[:])
                mk.dma("G", dbg["hsq"], work[:])
        if stage >= 3:
            emit_ln()
            emit_attn(upto=int(stage) - 3 if stage < 9 else 99)
            if debug >= 3 and stage >= 7:
                mk.wnow("G", "ve")
                mk.dma("G", dbg["x2"], resid[:])
                mk.dma("G", dbg["qt"], work[:, 0:16 * 1024])
                mk.dma("G", dbg["vvd"], work[:, 16 * 1024:32 * 1024])
                mk.dma("G", dbg["ktf"], kt_out.ap())
                mk.dma("G", dbg["ssb"], s_sb[:])
                mk.dma("G", dbg["std"], s_t[:])
        if stage >= 9:
            emit_ln()
            emit_ffn()

        mk.wnow("G", "ve")
        mk.wnow("G", "se")
        mk.wnow("G", "te")
        mk.dma("G", y_out.rearrange("(t p) d -> p t d", p=128),
               resid[:].rearrange("p (t d) -> p t d", d=D))
        mk.w("G", "gd", mk.cnt["gd"])

        @block.sync
        def _(e):
            for f in mk.ops["Y"]:
                f(e)

        @block.gpsimd
        def _(e):
            for f in mk.ops["G"]:
                f(e)

        @block.vector
        def _(e):
            for f in mk.ops["V"]:
                f(e)

        @block.scalar
        def _(e):
            for f in mk.ops["S"]:
                f(e)

        @block.tensor
        def _(e):
            for f in mk.ops["T"]:
                f(e)

    return nc


def host_inputs(L0_attn_x, L0_attn_w_qkv, L0_attn_w_o, L0_ffn_W_fc, L0_ffn_W_proj):
    x = np.ascontiguousarray(np.asarray(L0_attn_x, dtype=np.float32))
    wqkv = np.ascontiguousarray(np.asarray(L0_attn_w_qkv, dtype=np.float32))
    wo = np.ascontiguousarray(np.asarray(L0_attn_w_o, dtype=np.float32))
    wfc = np.ascontiguousarray(np.asarray(L0_ffn_W_fc, dtype=np.float32))
    wpr = np.ascontiguousarray(np.asarray(L0_ffn_W_proj, dtype=np.float32))
    identb = np.eye(128, dtype=ml_dtypes.bfloat16)
    identf = np.eye(128, dtype=np.float32)
    in_maps = []
    for c in range(8):
        b, r = c // 4, c % 4
        in_maps.append({
            "x_in": np.ascontiguousarray(x[b, r * TOK:(r + 1) * TOK, :]),
            "wfc_sl": np.ascontiguousarray(wfc[:, c * 256:(c + 1) * 256]),
            "wpr_sl": np.ascontiguousarray(wpr[:, c * 1024:(c + 1) * 1024]),
            "wqkv_sl": np.ascontiguousarray(wqkv[c * 256:(c + 1) * 256, :]),
            "wo_sl": np.ascontiguousarray(wo[c * 256:(c + 1) * 256, :]),
            "identb": identb,
            "identf": identf,
        })
    return in_maps


from concourse.bass_utils import run_bass_kernel_spmd

_NC_CACHE = {}


def kernel(L0_attn_x, L0_attn_w_qkv, L0_attn_w_o, L0_ffn_W_fc, L0_ffn_W_proj):
    if "nc" not in _NC_CACHE:
        _NC_CACHE["nc"] = build(debug=0, stage=9)
    nc = _NC_CACHE["nc"]
    in_maps = host_inputs(L0_attn_x, L0_attn_w_qkv, L0_attn_w_o,
                          L0_ffn_W_fc, L0_ffn_W_proj)
    res = run_bass_kernel_spmd(nc, in_maps, core_ids=list(range(8)))
    y = np.zeros((B, S, D), np.float32)
    for c in range(8):
        b, r = c // 4, c % 4
        y[b, r * TOK:(r + 1) * TOK] = np.asarray(res.results[c]["y"], np.float32)
    return y


# version 05
# speedup vs baseline: 1.1890x; 1.0168x over previous
import math
import numpy as np
SPART = 9
import ml_dtypes
import concourse.bass as bass
import concourse.mybir as mybir
from contextlib import ExitStack

F32 = mybir.dt.float32
BF16 = mybir.dt.bfloat16
AX = mybir.AxisListType
ALU = mybir.AluOpType
ACT = mybir.ActivationFunctionType

B, S, D, HID, NH, DH = 2, 4096, 2048, 8192, 16, 128
TOK = 1024
NT = TOK // 128
G4 = [[0, 1, 2, 3], [4, 5, 6, 7]]
G8 = [list(range(8))]
QC = 0.501953125
EPS_LN = 1e-5
EPS_RMS = float(np.finfo(np.float32).eps)
SCL = float(1.0 / math.sqrt(DH))

ENGS = ("T", "V", "S", "G", "Y")


class MK:
    def __init__(self, nc):
        self.nc = nc
        self.ops = {e: [] for e in ENGS}
        self.sem = {}
        self.cnt = {}

    def mksem(self, ctx, name):
        self.sem[name] = ctx.enter_context(self.nc.semaphore(name))
        self.cnt[name] = 0

    def w(self, eng, sem, val):
        if val <= 0:
            return
        h = self.sem[sem]
        self.ops[eng].append(lambda e, h=h, v=val: e.wait_ge(h, v))

    def wnow(self, eng, sem):
        self.w(eng, sem, self.cnt[sem])

    def d(self, eng, fn):
        self.ops[eng].append(fn)

    def dinc(self, eng, fn, sem, k=1):
        h = self.sem[sem]
        self.cnt[sem] += k
        self.ops[eng].append(lambda e, fn=fn, h=h, k=k: fn(e).then_inc(h, k))
        return self.cnt[sem]

    def dma(self, eng, out, in_, sem=None):
        if sem is None:
            sem = "yd" if eng == "Y" else "gd"
        return self.dinc(eng, lambda e, o=out, i=in_: e.dma_start(out=o, in_=i), sem, 16)

    def drain(self, eng):
        self.ops[eng].append(lambda e: e.drain())


def build(debug=0, stage=9):
    nc = bass.Bass("TRN2", target_bir_lowering=False)

    x_in = nc.dram_tensor("x_in", [TOK, D], F32, kind="ExternalInput").ap()
    wfc_sl = nc.dram_tensor("wfc_sl", [HID, 256], F32, kind="ExternalInput").ap()
    wpr_sl = nc.dram_tensor("wpr_sl", [D, 1024], F32, kind="ExternalInput").ap()
    wqkv_sl = nc.dram_tensor("wqkv_sl", [256, 3 * D], F32, kind="ExternalInput").ap()
    wo_sl = nc.dram_tensor("wo_sl", [256, D], F32, kind="ExternalInput").ap()
    identb_h = nc.dram_tensor("identb", [128, 128], BF16, kind="ExternalInput").ap()
    identf_h = nc.dram_tensor("identf", [128, 128], F32, kind="ExternalInput").ap()
    y_out = nc.dram_tensor("y", [TOK, D], F32, kind="ExternalOutput").ap()

    w1t_in = nc.dram_tensor("w1t_in", [256, HID], BF16)
    w1t_out = nc.dram_tensor("w1t_out", [D, HID], BF16)
    w2t_in = nc.dram_tensor("w2t_in", [16, 1024, 128], BF16)
    w2t_out = nc.dram_tensor("w2t_out", [8, 16, 1024, 128], BF16)
    qkv_in = nc.dram_tensor("qkv_in", [256, 3 * D], BF16)
    qkv_out = nc.dram_tensor("qkv_out", [D, 3 * D], BF16)
    wo_in = nc.dram_tensor("wo_in", [16, 256, 128], BF16)
    wo_out = nc.dram_tensor("wo_out", [8, 16, 256, 128], BF16)
    kt_in = nc.dram_tensor("kt_in", [16, 128, 1024], BF16)
    kt_out = nc.dram_tensor("kt_out", [4, 16, 128, 1024], BF16)
    s_in = nc.dram_tensor("s_in", [4, 16, 1024], F32)
    s_out = nc.dram_tensor("s_out", [16, 1024], F32)

    dbg = {}
    if debug:
        def dbgt(name, shape, dt):
            dbg[name] = nc.dram_tensor("d_" + name, shape, dt, kind="ExternalOutput").ap()
        dbgt("xlnT", [128, 16 * 1024], BF16)
        dbgt("w1t", [D, HID], BF16)
        dbgt("w2t", [8, 16, 1024, 128], BF16)
        dbgt("qkvw", [D, 3 * D], BF16)
        dbgt("wow", [8, 16, 256, 128], BF16)
        if debug >= 2:
            dbgt("x1", [128, NT * D], F32)
            dbgt("ms", [1, 1024], F32)
            dbgt("msraw", [1, 1024], F32)
            dbgt("ones", [128, 1], BF16)
            dbgt("hsq2d", [128, 2 * 512], BF16)
            dbgt("rst", [128, 8], F32)
            dbgt("hsq", [128, 64 * 512], BF16)
        if debug >= 3:
            dbgt("qt", [128, 16 * 1024], BF16)
            dbgt("vvd", [128, 8 * 2048], BF16)
            dbgt("ktf", [4, 16, 128, 1024], BF16)
            dbgt("ssb", [128, 512], F32)
            dbgt("std", [128, 8 * 16], F32)
            dbgt("x2", [128, NT * D], F32)

    mk = MK(nc)
    with ExitStack() as ctx:
        sb = lambda name, shape, dt: ctx.enter_context(nc.sbuf_tensor(name, shape, dt))
        psb = lambda name, shape, dt: ctx.enter_context(nc.psum_tensor(name, shape, dt))

        resid = sb("resid", [128, NT * D], F32)
        xlnT = sb("xlnT", [128, 16 * 1024], BF16)
        work = sb("work", [128, 64 * 512], BF16)
        ebuf = sb("ebuf", [128, 2 * 4096], BF16)
        wbuf = sb("wbuf", [128, 2 * 4096], BF16)
        hsq2 = sb("hsq2", [128, 2 * 512], BF16)
        fstage = sb("fstage", [128, 2 * 512], F32)
        s_sb = sb("s_sb", [128, 512], F32)
        identb = sb("identb_s", [128, 128], BF16)
        identf = sb("identf_s", [128, 128], F32)
        ones_bf = sb("ones_bf", [128, 1], BF16)
        bnst = sb("bnst", [128, 24], F32)
        bnag = sb("bnag", [128, 2], F32)
        m_v = sb("m_v", [128, 1], F32)
        rstd_v = sb("rstd_v", [128, 1], F32)
        bias_v = sb("bias_v", [128, 1], F32)
        zp = sb("zp", [128, 2 * 8], F32)
        zs = sb("zs", [128, 2], F32)
        rbf = sb("rbf", [128, 16], BF16)
        ms_sb = sb("ms_sb", [1, 2 * 512], F32)
        rs_t = sb("rs_t", [128, 8], F32)
        s_t = sb("s_t", [128, 8 * 16], F32)
        qsc_s = sb("qsc_s", [128, 64], F32)
        qsb_s = sb("qsb_s", [128, 64], BF16)
        qth_s = sb("qth_s", [128, 64], F32)
        vstage = sb("vstage", [128, 2 * 128], BF16)

        ps0 = psb("ps0", [128, 512], F32)
        ps1 = psb("ps1", [128, 512], F32)
        ps2 = psb("ps2", [128, 512], F32)
        ps_tpb = psb("ps_tpb", [128, 1024], BF16)
        ps_tp32 = psb("ps_tp32", [128, 512], F32)
        ps_ms = psb("ps_ms", [1, 512], F32)
        ps_s0 = psb("ps_s0", [128, 512], F32)
        ps_s1 = psb("ps_s1", [128, 512], F32)
        PS = [ps0, ps1, ps2]
        PSS = [ps_s0, ps_s1]

        for s in ["yd", "gd", "te", "ve", "se", "w0", "w1",
                  "ccA", "ccB", "ccC", "ccD", "ccE", "ccF"]:
            mk.mksem(ctx, s)
        block = ctx.enter_context(nc.Block())

        residv = resid[:].rearrange("p (t d) -> p t d", d=D)
        xlntv = xlnT[:].rearrange("p (c f) -> p c f", f=1024)
        hsqv = work[:].rearrange("p (c f) -> p c f", f=512)
        qtv = work[:, 0:16 * 1024].rearrange("p (c f) -> p c f", f=1024)
        vv = work[:, 16 * 1024:32 * 1024].rearrange("p (t f) -> p t f", f=D)
        valtv = xlnT[:].rearrange("p (c f) -> p c f", f=1024)
        ebv = ebuf[:].rearrange("p (i f) -> p i f", f=4096)
        wbv = wbuf[:].rearrange("p (i f) -> p i f", f=4096)
        ktsv = fstage[:].bitcast(BF16).rearrange("p (i f) -> p i f", f=1024)
        _hsq2a = hsq2[:].rearrange("p (i f) -> p i f", f=512)
        _hsq2b = s_sb[:].bitcast(BF16).rearrange("p (i f) -> p i f", f=512)

        def hsq2v_slot(k):
            return _hsq2a[:, k % 2, :] if (k % 4) < 2 else _hsq2b[:, k % 2, :]
        fsv = fstage[:].rearrange("p (i f) -> p i f", f=512)
        zpv = zp[:].rearrange("p (i f) -> p i f", f=8)
        msv = ms_sb[:].rearrange("p (i f) -> p i f", f=512)
        stv = s_t[:].rearrange("p (t h) -> p t h", h=16)
        vstv = vstage[:].rearrange("p (i f) -> p i f", f=128)
        xlnb = ebuf[:, 4096:4096 + 2048]
        stmp = ebuf[:, 0:4096].bitcast(F32)[0:16, 0:1024]
        qstage32 = work[:, 0:32 * 1024].bitcast(F32)

        def wbslot(i, f):
            return wbv[:, i, :].rearrange("p (c f) -> p c f", f=f)

        mk.dma("G", resid[:].rearrange("p (t d) -> p t d", d=D),
               x_in.rearrange("(t p) d -> p t d", p=128))
        x_done = mk.cnt["gd"]
        mk.dma("Y", identb[:], identb_h)
        mk.dma("Y", identf[:], identf_h)
        id_done = mk.cnt["yd"]
        mk.d("V", lambda e: e.memset(ones_bf[:], 1.0))

        def quant_stage(src_ap, fd, inner):
            mk.wnow("Y", "ve")
            ld = mk.dma("Y", qstage32[:, 0:fd].rearrange(
                "p (a b) -> p a b", b=inner), src_ap)
            mk.w("V", "yd", ld)
            mk.wnow("V", "gd")
            mk.wnow("V", "te")
            wb = ebv[:, 0, 0:fd]
            wb3 = wb.rearrange("p (g k) -> p g k", k=64)
            g = fd // 64
            mk.d("V", lambda e, wb=wb, fd=fd: e.tensor_copy(wb, qstage32[:, 0:fd]))
            mk.drain("V")
            mk.d("V", lambda e, wb3=wb3, g=g: e.tensor_reduce(
                qsc_s[:, 0:g], wb3, axis=AX.X, op=ALU.add, apply_absolute_value=True))
            mk.drain("V")
            mk.d("V", lambda e, g=g: e.tensor_scalar(
                qsb_s[:, 0:g], qsc_s[:, 0:g], 1.0 / 64.0, None, ALU.mult))
            mk.drain("V")
            mk.d("V", lambda e, g=g: e.tensor_scalar(
                qsc_s[:, 0:g], qsb_s[:, 0:g], 1e-8, None, ALU.max))
            mk.drain("V")
            mk.d("V", lambda e, g=g: e.tensor_scalar(
                qth_s[:, 0:g], qsc_s[:, 0:g], QC, None, ALU.mult))
            mk.drain("V")
            thb = qth_s[:, 0:g].unsqueeze(2).broadcast_to([128, g, 64])
            scb = qsc_s[:, 0:g].unsqueeze(2).broadcast_to([128, g, 64])
            mp = ebv[:, 1, 0:fd].rearrange("p (g k) -> p g k", k=64)
            mn = work[:, 8192:8192 + fd].rearrange("p (g k) -> p g k", k=64)
            mk.d("V", lambda e, mp=mp, wb3=wb3, thb=thb: e.scalar_tensor_tensor(
                mp, wb3, 1.0, thb, ALU.mult, ALU.is_gt))
            mk.d("V", lambda e, mn=mn, wb3=wb3, thb=thb: e.scalar_tensor_tensor(
                mn, wb3, -1.0, thb, ALU.mult, ALU.is_gt))
            mk.drain("V")
            mk.d("V", lambda e, mp=mp, mn=mn: e.tensor_tensor(mp, mp, mn, ALU.subtract))
            mk.drain("V")
            return mk.dinc("V", lambda e, wb3=wb3, mp=mp, scb=scb: e.tensor_tensor(
                wb3, mp, scb, ALU.mult), "ve")

        first_tb = [True]

        def transpose_blocks(nblk, src_fn, pairs_fn, vready):
            if first_tb[0]:
                mk.w("T", "yd", id_done)
                first_tb[0] = False
            nbat = (nblk + 7) // 8
            for b in range(nbat):
                k0, k1 = b * 8, min(b * 8 + 8, nblk)
                mk.w("T", "ve", vready if b == 0 else mk.cnt["ve"])
                mk.wnow("T", "gd")
                for k in range(k0, k1):
                    fn = lambda e, k=k, j=k - k0: e.transpose(
                        ps_tpb[:, j * 128:(j + 1) * 128], src_fn(k), identb[:])
                    if k == k1 - 1:
                        mk.dinc("T", fn, "te")
                    else:
                        mk.d("T", fn)
                mk.wnow("V", "te")
                stage = ktsv[:, b % 2, :]
                mk.dinc("V", lambda e, st=stage: e.tensor_copy(st, ps_tpb[:]), "ve")
                mk.wnow("G", "ve")
                for dst_ap, src_ap in pairs_fn(b, stage):
                    mk.dma("G", dst_ap, src_ap)

        wfc4 = wfc_sl.rearrange("(st tq p) d -> st p tq d", p=128, tq=16)
        w1ap = w1t_in.ap().rearrange("(dc p) (st tq f) -> p st tq dc f",
                                     p=128, st=4, f=128)
        for st4 in range(4):
            vd = quant_stage(wfc4[st4], 4096, 256)
            srcv = ebv[:, 0, 0:4096].rearrange("p (tq dc f) -> p tq dc f", dc=2, f=128)
            def w1pairs(b, st, st4=st4):
                stv4 = st.rearrange("p (tq dc f) -> p tq dc f", dc=2, f=128)
                return [(w1ap[:, st4, b * 4:b * 4 + 4, dc, :], stv4[:, :, dc, :])
                        for dc in range(2)]
            transpose_blocks(
                32,
                lambda k, srcv=srcv: srcv[:, k // 2, k % 2, :],
                w1pairs,
                vd)
        w1t_ready = mk.cnt["gd"]
        mk.w("G", "gd", w1t_ready)
        cc_w1t = mk.dinc("G", lambda e: e.collective_compute(
            "AllGather", ALU.bypass, replica_groups=G8,
            ins=[w1t_in.ap().opt()], outs=[w1t_out.ap().opt()]), "ccA")

        wpr4 = wpr_sl.rearrange("(st tq p) h -> st p tq h", p=128, tq=4)
        w2ap = w2t_in.ap().rearrange("dc (hc p) f -> p dc hc f", p=128)
        for st4 in range(4):
            vd = quant_stage(wpr4[st4], 4096, 1024)
            srcv = ebv[:, 0, 0:4096].rearrange("p (tq hc f) -> p tq hc f", hc=8, f=128)
            transpose_blocks(
                32,
                lambda k, srcv=srcv: srcv[:, k // 8, k % 8, :],
                lambda b, st, st4=st4: [(w2ap[:, st4 * 4 + b, :, :],
                                         st.rearrange("p (hc f) -> p hc f", f=128))],
                vd)
        w2t_ready = mk.cnt["gd"]
        mk.w("G", "gd", w2t_ready)
        cc_w2t = mk.dinc("G", lambda e: e.collective_compute(
            "AllGather", ALU.bypass, replica_groups=G8,
            ins=[w2t_in.ap().opt()], outs=[w2t_out.ap().opt()]), "ccB")

        qkv2 = wqkv_sl.rearrange("(r p) f -> r p f", p=128)
        qin2 = qkv_in.ap().rearrange("(r p) f -> r p f", p=128)
        for r in range(2):
            for half in range(2):
                f0, f1 = half * 3072, (half + 1) * 3072
                mk.wnow("Y", "ve")
                ld = mk.dma("Y", qstage32[:, 0:3072], qkv2[r, :, f0:f1])
                mk.w("V", "yd", ld)
                mk.wnow("V", "gd")
                mk.wnow("V", "te")
                vd = mk.dinc("V", lambda e: e.tensor_copy(
                    ebv[:, 0, 0:3072], qstage32[:, 0:3072]), "ve")
                mk.w("G", "ve", vd)
                mk.dma("G", qin2[r, :, f0:f1], ebv[:, 0, 0:3072])
        qkv_ready = mk.cnt["gd"]
        mk.w("G", "gd", qkv_ready)
        cc_qkv = mk.dinc("G", lambda e: e.collective_compute(
            "AllGather", ALU.bypass, replica_groups=G8,
            ins=[qkv_in.ap().opt()], outs=[qkv_out.ap().opt()]), "ccC")

        wo2 = wo_sl.rearrange("(r p) f -> r p f", p=128)
        wo_blk = wo_in.ap().rearrange("dc (r p) f -> r p dc f", p=128)
        for r in range(2):
            mk.wnow("Y", "ve")
            ld = mk.dma("Y", qstage32[:, 0:2048], wo2[r])
            mk.w("V", "yd", ld)
            mk.wnow("V", "gd")
            mk.wnow("V", "te")
            vd = mk.dinc("V", lambda e: e.tensor_copy(
                ebv[:, 0, 0:2048], qstage32[:, 0:2048]), "ve")
            mk.w("G", "ve", vd)
            mk.dma("G", wo_blk[r], ebv[:, 0, 0:2048].rearrange("p (dc f) -> p dc f", f=128))
        wo_ready = mk.cnt["gd"]
        mk.w("G", "gd", wo_ready)
        cc_wo = mk.dinc("G", lambda e: e.collective_compute(
            "AllGather", ALU.bypass, replica_groups=G8,
            ins=[wo_in.ap().opt()], outs=[wo_out.ap().opt()]), "ccD")


        def emit_ln(gd_wait=0):
            if gd_wait:
                mk.w("V", "gd", gd_wait)
                mk.w("S", "gd", gd_wait)
            mk.drain("V")
            prev_xln_se = 0
            prev_tr_te = 0
            for t in range(NT):
                xt = residv[:, t, :]
                if prev_xln_se:
                    mk.w("V", "se", prev_xln_se)
                for q4 in range(4):
                    mk.d("V", lambda e, xt=xt, q4=q4: e.bn_stats(
                        bnst[:, q4 * 6:(q4 + 1) * 6], xt[:, q4 * 512:(q4 + 1) * 512]))
                mk.drain("V")
                mk.d("V", lambda e: e.bn_aggr(bnag[:], bnst[:]))
                mk.drain("V")
                mk.d("V", lambda e: e.tensor_copy(m_v[:], bnag[:, 0:1]))
                mk.d("V", lambda e: e.tensor_scalar(
                    rstd_v[:], bnag[:, 1:2], 1.0, EPS_LN, ALU.mult, ALU.add))
                mk.drain("V")
                vd = mk.dinc("V", lambda e: e.reciprocal(rstd_v[:], rstd_v[:]), "ve")
                mk.w("S", "ve", vd)
                sd = mk.dinc("S", lambda e: e.activation(rstd_v[:], rstd_v[:], ACT.Sqrt), "se")
                mk.w("V", "se", sd)
                vd = mk.dinc("V", lambda e: e.scalar_tensor_tensor(
                    bias_v[:], m_v[:], -1.0, rstd_v[:], ALU.mult, ALU.mult), "ve")
                mk.w("S", "ve", vd)
                if prev_tr_te:
                    mk.w("S", "te", prev_tr_te)
                sd = mk.dinc("S", lambda e, xt=xt: e.activation(
                    xlnb[:], xt, ACT.Identity, bias=bias_v[:], scale=rstd_v[:]), "se")
                prev_xln_se = sd
                for bb in range(2):
                    mk.w("T", "se", sd)
                    mk.wnow("T", "ve")
                    for k in range(8):
                        dc = bb * 8 + k
                        fn = lambda e, dc=dc, k=k: e.transpose(
                            ps_tpb[:, k * 128:(k + 1) * 128],
                            xlnb[:, dc * 128:(dc + 1) * 128], identb[:])
                        if k == 7:
                            prev_tr_te = mk.dinc("T", fn, "te")
                        else:
                            mk.d("T", fn)
                    mk.wnow("V", "te")
                    dst = xlntv[:, bb * 8:(bb + 1) * 8, t * 128:(t + 1) * 128]
                    mk.dinc("V", lambda e, dst=dst: e.tensor_copy(
                        dst, ps_tpb[:].rearrange("p (c f) -> p c f", f=128)), "ve")

        emit_ln(gd_wait=wo_ready)

        if debug:
            mk.wnow("G", "ve")
            mk.dma("G", dbg["xlnT"], xlnT[:])
            mk.w("G", "ccD", mk.cnt["ccD"])
            mk.dma("G", dbg["w1t"], w1t_out.ap())
            mk.dma("G", dbg["w2t"], w2t_out.ap())
            mk.dma("G", dbg["qkvw"], qkv_out.ap())
            mk.dma("G", dbg["wow"], wo_out.ap())

        wslot_guard = {0: 0, 1: 0}

        def emit_ffn():
            mk.wnow("T", "ve")
            mk.wnow("T", "se")
            w1v = w1t_out.ap().rearrange("(dc p) h -> p dc h", p=128)
            w2v = w2t_out.ap()
            for hf in range(2):
                psfree = {}
                hsqready = {}
                hsq2ready = {}
                msdone_te = {}
                msevac_ve = {}
                sup_te = {}
                mstmp2 = ebuf[0:1, 0:4096].bitcast(F32)[0:1, 0:512]

                def emit_ms_group(sp2):
                    mk.w("T", "ve", hsq2ready[sp2 * 2 + 1])
                    if sp2 >= 2:
                        mk.w("T", "ve", msevac_ve[sp2 - 2])
                    for cc2 in range(2):
                        k2 = sp2 * 2 + cc2
                        fn = lambda e, k2=k2, cc2=cc2, sp2=sp2: e.matmul(
                            PSS[sp2 % 2][0:1, :], ones_bf[:], hsq2v_slot(k2),
                            start=(cc2 == 0), stop=(cc2 == 1))
                        if cc2 == 1:
                            msdone_te[sp2] = mk.dinc("T", fn, "te")
                        else:
                            mk.d("T", fn)
                    mk.w("V", "te", msdone_te[sp2])
                    if sp2 == 0:
                        msevac_ve[sp2] = mk.dinc("V", lambda e: e.tensor_copy(
                            msv[:, 0, :], PSS[0][0:1, :]), "ve")
                    else:
                        mk.d("V", lambda e, sp2=sp2: e.tensor_copy(
                            mstmp2, PSS[sp2 % 2][0:1, :]))
                        mk.drain("V")
                        msevac_ve[sp2] = mk.dinc("V", lambda e, sp2=sp2: e.tensor_tensor(
                            msv[:, sp2 % 2, :], mstmp2, msv[:, 1 - sp2 % 2, :],
                            ALU.add), "ve")
                mk.wnow("S", "te")
                for sp in range(32):
                    mk.w("Y", "te", wslot_guard[sp % 2])
                    mk.w("Y", "ccA", cc_w1t)
                    ld = mk.dma("Y", wbslot(sp % 2, 256),
                                w1v[:, :, sp * 256:(sp + 1) * 256],
                                sem="w%d" % (sp % 2))
                    if sp >= 2:
                        emit_ms_group(sp - 2)
                    mk.w("T", "w%d" % (sp % 2), ld)
                    for cc2 in range(2):
                        k = sp * 2 + cc2
                        if k >= 3:
                            mk.w("T", "se", psfree[k - 3])
                        te_g = 0
                        for dc in range(16):
                            fn = (lambda e, k=k, cc2=cc2, dc=dc, sp=sp, hf=hf:
                                  e.matmul(PS[k % 3][:],
                                           wbslot(sp % 2, 256)[:, dc, cc2 * 128:(cc2 + 1) * 128],
                                           xlntv[:, dc, hf * 512:(hf + 1) * 512],
                                           start=(dc == 0), stop=(dc == 15)))
                            if dc == 15:
                                te_g = mk.dinc("T", fn, "te")
                            else:
                                mk.d("T", fn)
                        mk.w("S", "te", te_g)
                        if k < 2:
                            mk.w("S", "gd", wo_ready)
                        if k >= 2:
                            mk.w("S", "ve", hsqready.get(k - 2, 0))
                        sd = mk.dinc("S", lambda e, k=k: e.activation(
                            fsv[:, k % 2, :], PS[k % 3][:], ACT.Relu), "se")
                        psfree[k] = sd
                        mk.w("V", "se", sd)
                        if k >= 4:
                            mk.w("V", "te", msdone_te.get((k - 4) // 2, 0))
                        hsqready[k] = mk.dinc("V", lambda e, k=k: e.tensor_tensor(
                            hsqv[:, k, :], fsv[:, k % 2, :], fsv[:, k % 2, :], ALU.mult), "ve")
                        mk.drain("V")
                        hsq2ready[k] = mk.dinc("V", lambda e, k=k: e.tensor_tensor(
                            hsq2v_slot(k), hsqv[:, k, :], hsqv[:, k, :], ALU.mult), "ve")
                    sup_te[sp] = te_g
                    wslot_guard[sp % 2] = te_g
                    if sp == 31:
                        emit_ms_group(30)
                        emit_ms_group(31)
                if debug >= 2:
                    mk.wnow("G", "ve")
                    mk.dma("G", dbg["msraw"].rearrange("o (i f) -> o i f", f=512)[:, hf, :],
                           msv[:, 1, :])
                    if hf == 0:
                        mk.dma("G", dbg["hsq2d"], hsq2[:])
                mrow = msv[:, 1, :]
                mk.drain("V")
                mk.d("V", lambda e, mrow=mrow: e.tensor_scalar(
                    mrow, mrow, 1.0 / HID, EPS_RMS, ALU.mult, ALU.add))
                mk.drain("V")
                vd = mk.dinc("V", lambda e, mrow=mrow: e.reciprocal(mrow, mrow), "ve")
                mk.w("S", "ve", vd)
                sd = mk.dinc("S", lambda e, mrow=mrow: e.activation(mrow, mrow, ACT.Sqrt), "se")
                mk.w("V", "se", sd)
                vd = mk.dinc("V", lambda e, mrow=mrow: e.tensor_scalar(
                    mrow, mrow, 0.5, None, ALU.mult), "ve")
                mk.w("T", "ve", vd)
                for b4 in range(4):
                    fn = lambda e, b4=b4, mrow=mrow: e.transpose(
                        ps_tp32[:, b4 * 128:b4 * 128 + 1],
                        mrow[0:1, b4 * 128:(b4 + 1) * 128], identf[0:1, 0:1])
                    if b4 == 3:
                        td = mk.dinc("T", fn, "te")
                    else:
                        mk.d("T", fn)
                mk.w("V", "te", td)
                mk.dinc("V", lambda e, hf=hf: e.tensor_copy(
                    rs_t[:, hf * 4:(hf + 1) * 4],
                    ps_tp32[:].rearrange("p (b f) -> p b f", f=128)[:, :, 0]), "ve")

                grp_half_te = {}
                stt_ve = {}
                ev_of = {}

                def proj_epilogue(j, hf=hf):
                    val = 0
                    for b4 in range(4):
                        if b4 == 0:
                            mk.w("T", "ve", ev_of[j])
                            mk.w("T", "ve", stt_ve.get(j - 1, 0))
                        else:
                            mk.wnow("T", "ve")
                        td = mk.dinc("T", lambda e, j=j, b4=b4: e.transpose(
                            ps_tp32[:, b4 * 128:(b4 + 1) * 128],
                            fsv[:, j % 2, b4 * 128:(b4 + 1) * 128], identf[:]), "te")
                        mk.w("V", "te", td)
                        t_ = hf * 4 + b4
                        val = mk.dinc("V", lambda e, t_=t_, j=j, b4=b4:
                                      e.scalar_tensor_tensor(
                                          residv[:, t_, j * 128:(j + 1) * 128],
                                          ps_tp32[:, b4 * 128:(b4 + 1) * 128],
                                          rs_t[:, t_:t_ + 1],
                                          residv[:, t_, j * 128:(j + 1) * 128],
                                          ALU.mult, ALU.add), "ve")
                    stt_ve[j] = val

                for dcg in range(16):
                    ld_half = {}
                    for half in range(2):
                        hb = dcg * 2 + half
                        mk.w("Y", "te", wslot_guard[hb % 2])
                        mk.w("Y", "ccB", cc_w2t)
                        for r8 in range(4):
                            r = half * 4 + r8
                            mk.dma("Y", wbslot(hb % 2, 128)[:, r8 * 8:(r8 + 1) * 8, :],
                                   w2v[r, dcg].rearrange("(hc p) f -> p hc f", p=128),
                                   sem="w%d" % (hb % 2))
                        ld_half[half] = (hb % 2, mk.cnt["w%d" % (hb % 2)])
                    if dcg >= 3:
                        mk.w("T", "ve", ev_of[dcg - 3])
                    for c in range(64):
                        hb = dcg * 2 + (c // 32)
                        if c in (0, 32):
                            sl, lv = ld_half[c // 32]
                            mk.w("T", "w%d" % sl, lv)
                        fn = (lambda e, dcg=dcg, c=c, hb=hb:
                              e.matmul(PS[dcg % 3][:],
                                       wbslot(hb % 2, 128)[:, c % 32, :],
                                       hsqv[:, c, :],
                                       start=(c == 0), stop=(c == 63)))
                        if c == 31:
                            grp_half_te[dcg * 2] = mk.dinc("T", fn, "te")
                        elif c == 63:
                            grp_half_te[dcg * 2 + 1] = mk.dinc("T", fn, "te")
                        else:
                            mk.d("T", fn)
                    grp_te = grp_half_te[dcg * 2 + 1]
                    wslot_guard[0] = grp_half_te[dcg * 2]
                    wslot_guard[1] = grp_te
                    mk.w("V", "te", grp_te)
                    ev = mk.dinc("V", lambda e, dcg=dcg: e.tensor_copy(
                        fsv[:, dcg % 2, :], PS[dcg % 3][:]), "ve")
                    ev_of[dcg] = ev
                    if dcg >= 1:
                        proj_epilogue(dcg - 1)
                    if dcg == 15:
                        proj_epilogue(15)

        def emit_attn(upto=99):
            bank_ve = {0: 0, 1: 0, 2: 0}
            mk.wnow("T", "ve")
            qv = qkv_out.ap().rearrange("(dc p) f -> p dc f", p=128)
            ktov = kt_out.ap()
            wov = wo_out.ap()
            sup_order = list(range(8, 16)) + list(range(0, 8)) + list(range(16, 24))
            sup_te2 = {}
            kt_written = 0
            cc_kt = None
            for si, sp in enumerate(sup_order):
                mk.w("Y", "te", wslot_guard[si % 2])
                mk.w("Y", "ccC", cc_qkv)
                ld = mk.dma("Y", wbslot(si % 2, 256),
                            qv[:, :, sp * 256:(sp + 1) * 256],
                            sem="w%d" % (si % 2))
                mk.w("T", "w%d" % (si % 2), ld)
                base_col = sp * 256
                g_te = 0
                if base_col < 4096:
                    for cc2 in range(2):
                        col = base_col + cc2 * 128
                        ch = col // 128
                        for tb in range(2):
                            gidx = ch * 2 + tb
                            mk.w("T", "ve", bank_ve[gidx % 3])
                            for dc in range(16):
                                fn = (lambda e, si=si, cc2=cc2, dc=dc, tb=tb, gidx=gidx:
                                      e.matmul(PS[gidx % 3][:],
                                               wbslot(si % 2, 256)[:, dc, cc2 * 128:(cc2 + 1) * 128],
                                               xlntv[:, dc, tb * 512:(tb + 1) * 512],
                                               start=(dc == 0), stop=(dc == 15)))
                                if dc == 15:
                                    g_te = mk.dinc("T", fn, "te")
                                else:
                                    mk.d("T", fn)
                            mk.w("V", "te", g_te)
                            if col < 2048:
                                bank_ve[gidx % 3] = mk.dinc(
                                    "V", lambda e, ch=ch, tb=tb, gidx=gidx: e.tensor_copy(
                                        qtv[:, ch, tb * 512:(tb + 1) * 512],
                                        PS[gidx % 3][:]), "ve")
                            else:
                                kc = ch - 16
                                bank_ve[gidx % 3] = mk.dinc(
                                    "V", lambda e, kc=kc, tb=tb, gidx=gidx: e.tensor_copy(
                                        ktsv[:, kc % 2, tb * 512:(tb + 1) * 512],
                                        PS[gidx % 3][:]), "ve")
                        if col >= 2048:
                            kc = ch - 16
                            mk.wnow("G", "ve")
                            mk.dma("G", kt_in.ap()[kc], ktsv[:, kc % 2, :])
                            kt_written += 1
                            if kt_written == 16:
                                mk.w("G", "gd", mk.cnt["gd"])
                                cc_kt = mk.dinc("G", lambda e: e.collective_compute(
                                    "AllGather", ALU.bypass, replica_groups=G4,
                                    ins=[kt_in.ap().opt()], outs=[kt_out.ap().opt()]),
                                    "ccE")
                else:
                    vc0 = base_col - 4096
                    for tt in range(NT):
                        mk.w("T", "ve", bank_ve[tt % 3])
                        for dc in range(16):
                            fn = (lambda e, si=si, dc=dc, tt=tt:
                                  e.matmul(PS[tt % 3][:, 0:256],
                                           xlntv[:, dc, tt * 128:(tt + 1) * 128],
                                           wbslot(si % 2, 256)[:, dc, :],
                                           start=(dc == 0), stop=(dc == 15)))
                            if dc == 15:
                                g_te = mk.dinc("T", fn, "te")
                            else:
                                mk.d("T", fn)
                        mk.w("V", "te", g_te)
                        bank_ve[tt % 3] = mk.dinc(
                            "V", lambda e, tt=tt, vc0=vc0: e.tensor_copy(
                                vv[:, tt, vc0:vc0 + 256], PS[tt % 3][:, 0:256]), "ve")
                sup_te2[si] = g_te
                wslot_guard[si % 2] = g_te

            if upto < 1:
                return
            mk.wnow("T", "ve")
            mk.wnow("T", "se")
            se_exp = {}
            te_sdone = {}
            ve_sacc = {}
            ve_r = {}
            head_te = {}
            g = 0

            def emit_smm(j):
                hj, qtj = j // 8, j % 8
                mk.w("T", "ve", ve_r[j])
                if j >= 2:
                    mk.w("T", "ve", ve_sacc[j - 2])
                for kc8 in range(32):
                    fn = (lambda e, j=j, kc8=kc8: e.matmul(
                        PSS[j % 2][:, kc8:kc8 + 1],
                        ebv[:, j % 2, kc8 * 128:(kc8 + 1) * 128],
                        rbf[:, (j % 2) * 8:(j % 2) * 8 + 1],
                        start=True, stop=True))
                    if kc8 == 31:
                        te_sdone[j] = mk.dinc("T", fn, "te")
                    else:
                        mk.d("T", fn)
                mk.w("V", "te", te_sdone[j])
                if qtj == 0:
                    ve_sacc[j] = mk.dinc("V", lambda e, hj=hj, j=j: e.tensor_copy(
                        s_sb[:, hj * 32:(hj + 1) * 32], PSS[j % 2][:, 0:32]), "ve")
                else:
                    ve_sacc[j] = mk.dinc("V", lambda e, hj=hj, j=j:
                                         e.scalar_tensor_tensor(
                                             s_sb[:, hj * 32:(hj + 1) * 32],
                                             PSS[j % 2][:, 0:32], 1.0,
                                             s_sb[:, hj * 32:(hj + 1) * 32],
                                             ALU.mult, ALU.add), "ve")

            for h in range(NH):
                mk.w("Y", "te", wslot_guard[h % 2])
                mk.w("Y", "ccE", cc_kt)
                for r4 in range(4):
                    mk.dma("Y", wbv[:, h % 2, r4 * 1024:(r4 + 1) * 1024], ktov[r4, h],
                           sem="w%d" % (h % 2))
                ktl = mk.cnt["w%d" % (h % 2)]
                mk.w("T", "w%d" % (h % 2), ktl)
                for qt in range(8):
                    it = h * 8 + qt
                    for kc in range(8):
                        if g >= 3:
                            mk.w("T", "se", se_exp[g - 3])
                        fn = (lambda e, h=h, qt=qt, kc=kc, g=g: e.matmul(
                            PS[g % 3][:],
                            qtv[:, h, qt * 128:(qt + 1) * 128],
                            wbv[:, h % 2, kc * 512:(kc + 1) * 512],
                            start=True, stop=True))
                        te_mm = mk.dinc("T", fn, "te")
                        head_te[h] = te_mm
                        mk.w("S", "te", te_mm)
                        if it >= 2 and kc == 0:
                            mk.w("S", "te", te_sdone[it - 2])
                            mk.w("S", "ve", ve_r[it - 2])
                        se_exp[g] = mk.dinc("S", lambda e, it=it, kc=kc, g=g: e.activation(
                            ebv[:, it % 2, kc * 512:(kc + 1) * 512],
                            PS[g % 3][:], ACT.Exp, scale=SCL,
                            accum_out=zpv[:, it % 2, kc:kc + 1]), "se")
                        g += 1
                    mk.w("V", "se", se_exp[g - 1])
                    mk.d("V", lambda e, it=it: e.reduce_sum(
                        zs[:, it % 2:it % 2 + 1], zpv[:, it % 2, :], axis=AX.X))
                    mk.drain("V")
                    mk.d("V", lambda e, it=it: e.reciprocal(
                        zs[:, it % 2:it % 2 + 1], zs[:, it % 2:it % 2 + 1]))
                    mk.drain("V")
                    ve_r[it] = mk.dinc("V", lambda e, it=it: e.tensor_copy(
                        rbf[:, (it % 2) * 8:(it % 2) * 8 + 1], zs[:, it % 2:it % 2 + 1]), "ve")
                    if qt == 7:
                        wslot_guard[h % 2] = head_te[h]
                    if it >= 1:
                        emit_smm(it - 1)
            emit_smm(NH * 8 - 1)

            if upto < 2:
                return
            for b4 in range(4):
                mk.w("T", "ve", ve_sacc[NH * 8 - 1] if b4 == 0 else mk.cnt["ve"])
                td = mk.dinc("T", lambda e, b4=b4: e.transpose(
                    ps_tp32[:, b4 * 128:(b4 + 1) * 128],
                    s_sb[:, b4 * 128:(b4 + 1) * 128], identf[:]), "te")
                mk.w("V", "te", td)
                mk.dinc("V", lambda e, b4=b4: e.tensor_copy(
                    fstage[:, b4 * 128:(b4 + 1) * 128],
                    ps_tp32[:, b4 * 128:(b4 + 1) * 128]), "ve")
            mk.wnow("G", "ve")
            dfull = s_in.ap().rearrange("r (blk hlo) (kc8 f) -> hlo blk r kc8 f",
                                        hlo=4, kc8=8)
            for blk in range(4):
                for hlo in range(4):
                    mk.dma("G", dfull[hlo, blk],
                           fstage[hlo * 32:(hlo + 1) * 32,
                                  blk * 128:(blk + 1) * 128])
            mk.w("G", "gd", mk.cnt["gd"])
            cc_s = mk.dinc("G", lambda e: e.collective_compute(
                "ReduceScatter", ALU.add, replica_groups=G4,
                ins=[s_in.ap().opt()], outs=[s_out.ap().opt()]), "ccF")
            mk.w("G", "ccF", cc_s)
            sld = mk.dma("G", stmp, s_out.ap())
            for t in range(NT):
                mk.w("T", "gd", sld)
                mk.wnow("T", "ve")
                td = mk.dinc("T", lambda e, t=t: e.transpose(
                    ps_tp32[:, 0:16], stmp[:, t * 128:(t + 1) * 128],
                    identf[0:16, 0:16]), "te")
                mk.w("V", "te", td)
                mk.dinc("V", lambda e, t=t: e.tensor_copy(
                    stv[:, t, :], ps_tp32[:, 0:16]), "ve")

            if upto < 3:
                return
            for t in range(NT):
                for h in range(NH):
                    i = t * NH + h
                    mk.wnow("V", "te")
                    vd = mk.dinc("V", lambda e, t=t, h=h, i=i: e.tensor_scalar(
                        vstv[:, i % 2, :], vv[:, t, h * 128:(h + 1) * 128],
                        stv[:, t, h:h + 1], None, ALU.mult), "ve")
                    mk.w("T", "ve", vd)
                    k8 = i % 8
                    fn = lambda e, k8=k8, i=i: e.transpose(
                        ps_tpb[:, k8 * 128:(k8 + 1) * 128], vstv[:, i % 2, :], identb[:])
                    mk.dinc("T", fn, "te")
                    if k8 == 7:
                        mk.wnow("V", "te")
                        t0, h0 = (i - 7) // NH, (i - 7) % NH
                        dst2 = valtv[:, h0:h0 + 8, t0 * 128:(t0 + 1) * 128]
                        mk.dinc("V", lambda e, dst2=dst2: e.tensor_copy(
                            dst2, ps_tpb[:].rearrange("p (c f) -> p c f", f=128)), "ve")

            if upto < 4:
                return
            grp_te = {}
            mk.wnow("T", "ve")
            mk.wnow("T", "se")
            for dcg in range(16):
                mk.w("Y", "te", wslot_guard[dcg % 2])
                mk.w("Y", "ccD", cc_wo)
                for r8 in range(8):
                    mk.dma("Y", wbslot(dcg % 2, 128)[:, r8 * 2:(r8 + 1) * 2, :],
                           wov[r8, dcg].rearrange("(vc p) f -> p vc f", p=128),
                           sem="w%d" % (dcg % 2))
                ld = mk.cnt["w%d" % (dcg % 2)]
                for tb in range(2):
                    mk.w("T", "w%d" % (dcg % 2), ld)
                    for vc in range(16):
                        fn = (lambda e, dcg=dcg, vc=vc, tb=tb: e.matmul(
                            PS[(dcg * 2 + tb) % 3][:],
                            wbslot(dcg % 2, 128)[:, vc, :],
                            valtv[:, vc, tb * 512:(tb + 1) * 512],
                            start=(vc == 0), stop=(vc == 15)))
                        if vc == 15:
                            grp_te[(dcg, tb)] = mk.dinc("T", fn, "te")
                        else:
                            mk.d("T", fn)
                    if tb == 1:
                        wslot_guard[dcg % 2] = grp_te[(dcg, tb)]
                    mk.w("V", "te", grp_te[(dcg, tb)])
                    ev = mk.dinc("V", lambda e, dcg=dcg, tb=tb: e.tensor_copy(
                        fsv[:, tb, :], PS[(dcg * 2 + tb) % 3][:]), "ve")
                    for b4 in range(4):
                        mk.w("T", "ve", ev if b4 == 0 else mk.cnt["ve"])
                        td = mk.dinc("T", lambda e, tb=tb, b4=b4: e.transpose(
                            ps_tp32[:, b4 * 128:(b4 + 1) * 128],
                            fsv[:, tb, b4 * 128:(b4 + 1) * 128], identf[:]), "te")
                        mk.w("V", "te", td)
                        t_ = tb * 4 + b4
                        mk.dinc("V", lambda e, t_=t_, dcg=dcg, b4=b4:
                                e.scalar_tensor_tensor(
                                    residv[:, t_, dcg * 128:(dcg + 1) * 128],
                                    ps_tp32[:, b4 * 128:(b4 + 1) * 128], 1.0,
                                    residv[:, t_, dcg * 128:(dcg + 1) * 128],
                                    ALU.mult, ALU.add), "ve")

        if stage >= 2:
            emit_ffn()
            if debug >= 2 and stage < 9:
                mk.wnow("G", "ve")
                mk.dma("G", dbg["x1"], resid[:])
                mk.dma("G", dbg["ms"], ms_sb[:])
                mk.dma("G", dbg["ones"], ones_bf[:])
                mk.dma("G", dbg["rst"], rs_t[:])
                mk.dma("G", dbg["hsq"], work[:])
        if stage >= 3:
            emit_ln()
            emit_attn(upto=int(stage) - 3 if stage < 9 else 99)
            if debug >= 3 and stage >= 7:
                mk.wnow("G", "ve")
                mk.dma("G", dbg["x2"], resid[:])
                mk.dma("G", dbg["qt"], work[:, 0:16 * 1024])
                mk.dma("G", dbg["vvd"], work[:, 16 * 1024:32 * 1024])
                mk.dma("G", dbg["ktf"], kt_out.ap())
                mk.dma("G", dbg["ssb"], s_sb[:])
                mk.dma("G", dbg["std"], s_t[:])
        if stage >= 9:
            emit_ln()
            emit_ffn()

        mk.wnow("G", "ve")
        mk.wnow("G", "se")
        mk.wnow("G", "te")
        mk.dma("G", y_out.rearrange("(t p) d -> p t d", p=128),
               resid[:].rearrange("p (t d) -> p t d", d=D))
        mk.w("G", "gd", mk.cnt["gd"])

        @block.sync
        def _(e):
            for f in mk.ops["Y"]:
                f(e)

        @block.gpsimd
        def _(e):
            for f in mk.ops["G"]:
                f(e)

        @block.vector
        def _(e):
            for f in mk.ops["V"]:
                f(e)

        @block.scalar
        def _(e):
            for f in mk.ops["S"]:
                f(e)

        @block.tensor
        def _(e):
            for f in mk.ops["T"]:
                f(e)

    return nc


def host_inputs(L0_attn_x, L0_attn_w_qkv, L0_attn_w_o, L0_ffn_W_fc, L0_ffn_W_proj):
    x = np.ascontiguousarray(np.asarray(L0_attn_x, dtype=np.float32))
    wqkv = np.ascontiguousarray(np.asarray(L0_attn_w_qkv, dtype=np.float32))
    wo = np.ascontiguousarray(np.asarray(L0_attn_w_o, dtype=np.float32))
    wfc = np.ascontiguousarray(np.asarray(L0_ffn_W_fc, dtype=np.float32))
    wpr = np.ascontiguousarray(np.asarray(L0_ffn_W_proj, dtype=np.float32))
    identb = np.eye(128, dtype=ml_dtypes.bfloat16)
    identf = np.eye(128, dtype=np.float32)
    in_maps = []
    for c in range(8):
        b, r = c // 4, c % 4
        in_maps.append({
            "x_in": np.ascontiguousarray(x[b, r * TOK:(r + 1) * TOK, :]),
            "wfc_sl": np.ascontiguousarray(wfc[:, c * 256:(c + 1) * 256]),
            "wpr_sl": np.ascontiguousarray(wpr[:, c * 1024:(c + 1) * 1024]),
            "wqkv_sl": np.ascontiguousarray(wqkv[c * 256:(c + 1) * 256, :]),
            "wo_sl": np.ascontiguousarray(wo[c * 256:(c + 1) * 256, :]),
            "identb": identb,
            "identf": identf,
        })
    return in_maps


from concourse.bass_utils import run_bass_kernel_spmd

_NC_CACHE = {}


def kernel(L0_attn_x, L0_attn_w_qkv, L0_attn_w_o, L0_ffn_W_fc, L0_ffn_W_proj):
    if "nc" not in _NC_CACHE:
        _NC_CACHE["nc"] = build(debug=0, stage=9)
    nc = _NC_CACHE["nc"]
    in_maps = host_inputs(L0_attn_x, L0_attn_w_qkv, L0_attn_w_o,
                          L0_ffn_W_fc, L0_ffn_W_proj)
    res = run_bass_kernel_spmd(nc, in_maps, core_ids=list(range(8)))
    y = np.zeros((B, S, D), np.float32)
    for c in range(8):
        b, r = c // 4, c % 4
        y[b, r * TOK:(r + 1) * TOK] = np.asarray(res.results[c]["y"], np.float32)
    return y
